# version 14
# baseline (speedup 1.0000x reference)
import math
import contextlib
import numpy as np
import ml_dtypes
import concourse.bass as bass
import concourse.mybir as mybir
from concourse.bass_utils import run_bass_kernel_spmd

F32 = mybir.dt.float32
BF16 = mybir.dt.bfloat16
F16 = mybir.dt.float16
U8 = mybir.dt.uint8
AF = mybir.ActivationFunctionType
ALU = mybir.AluOpType
AX = mybir.AxisListType.X

MAGIC = 12582912.0
T, C, I = 2048, 1024, 2816
NCH = C // 128
NIC = I // 128
NTT = T // 128
H = T // 2
_EPS = math.exp(-1.0 / 12.0)
EPS_R = _EPS / 2.0
EPS_K = _EPS + 0.3 / 11.0
EPS_V = _EPS
QMAX = 127.0


_ENG_ATTR = {"v": "vector", "s": "scalar", "t": "tensor", "g": "gpsimd", "y": "sync"}


class Buf:
    __slots__ = ("t", "w", "rs", "extra", "name")

    def __init__(self, t, name=""):
        self.t = t
        self.w = None
        self.rs = []
        self.extra = []
        self.name = name

    def __getitem__(self, idx):
        return self.t[idx]


class Bld:
    def __init__(self, nc, ctx):
        self.nc = nc
        self.ctx = ctx
        self.prog = []
        self.sems = {}
        self.waited = {}
        self.dma_sems = set()

    def sbuf(self, name, shape, dtype):
        return Buf(self.ctx.enter_context(self.nc.sbuf_tensor(name, shape, dtype)), name)

    def psum(self, name, shape, dtype):
        return Buf(self.ctx.enter_context(self.nc.psum_tensor(name, shape, dtype)), name)

    def dram(self, name, shape, dtype, kind="Internal"):
        return Buf(self.nc.dram_tensor(name, shape, dtype, kind=kind), name)

    def region(self, name, nbytes):
        t = self.ctx.enter_context(self.nc.sbuf_tensor(name, [128, nbytes], U8))
        return self.nc.lookup_mloc(t).addr

    def at(self, addr, rel, name, shape, dtype):
        t = self.nc.alloc_sbuf_tensor_at(name, shape, dtype, offset=addr + rel)
        return Buf(t, name)

    def handoff(self, new_bufs, old_bufs):
        toks = []
        for b in old_bufs:
            if b.w:
                toks.append(b.w)
            toks.extend(b.rs)
            toks.extend(b.extra)
        for nb in new_bufs:
            nb.extra = list(toks)

    def _sem(self, key):
        if key not in self.sems:
            h = self.ctx.enter_context(self.nc.semaphore(key[:30]))
            self.sems[key] = [h, 0]
        return self.sems[key]

    def emit(self, eng, fn, reads=(), writes=(), sem=None, dma=False, n_dma=1,
             extra_waits=()):
        if dma:
            assert sem is not None
        sem_key = sem or ("c" + eng)
        s = self._sem(sem_key)
        amount = 16 * n_dma if dma else 1
        toks = list(extra_waits)
        for b in reads:
            if b.w:
                toks.append(b.w)
            toks.extend(b.extra)
        for b in writes:
            if b.w:
                toks.append(b.w)
            toks.extend(b.rs)
            toks.extend(b.extra)
        if dma:
            toks.append((sem_key, s[1]))
            self.dma_sems.add(sem_key)
        wd = self.waited.setdefault(eng, {})
        final = {}
        for (sk, v) in toks:
            if v > wd.get(sk, 0) and v > final.get(sk, 0):
                final[sk] = v
        s[1] += amount
        tok = (sem_key, s[1])
        waits = sorted(final.items())

        def closure(e, waits=waits, semh=s[0], amt=amount, fn=fn):
            for sk, v in waits:
                e.wait_ge(self.sems[sk][0], v)
            fn(e).then_inc(semh, amt)

        self.prog.append((eng, closure))
        for sk, v in final.items():
            wd[sk] = v
        for b in reads:
            b.rs.append(tok)
        for b in writes:
            b.w = tok
            b.rs = []
            b.extra = []
        return tok

    def tok(self, sem_key):
        return (sem_key, self._sem(sem_key)[1])

    def settle(self, bufs, tok):
        for b in bufs:
            b.w = tok
            b.rs = []
            b.extra = []

    def final_wait(self, eng, toks=None):
        toks = [self.tok(k) for k in sorted(self.dma_sems)]
        def closure(e, toks=toks):
            for sk, v in toks:
                e.wait_ge(self.sems[sk][0], v)
        self.prog.append((eng, closure))

    def build(self, block):
        per = {k: [] for k in _ENG_ATTR}
        for eng, closure in self.prog:
            per[eng].append(closure)
        for key, closures in per.items():
            if not closures:
                continue

            def body(e, closures=closures):
                for c in closures:
                    c(e)

            getattr(block, _ENG_ATTR[key])(body)


def build_l1():
    nc = bass.Bass("TRN2", target_bir_lowering=False)
    ctx = contextlib.ExitStack()
    with ctx:
        b = Bld(nc, ctx)
        x_d = Buf(nc.dram_tensor("x", [T, C], F32, kind="ExternalInput"), "x")
        wg_d = Buf(nc.dram_tensor("wg", [384, C], F32, kind="ExternalInput"), "wg")
        wu_d = Buf(nc.dram_tensor("wu", [384, C], F32, kind="ExternalInput"), "wu")
        wd_d = Buf(nc.dram_tensor("wd", [128, I], F32, kind="ExternalInput"), "wd")
        xnf_d = Buf(nc.dram_tensor("xnf", [T, C], F32, kind="ExternalOutput"), "xnf")
        xnb_d = Buf(nc.dram_tensor("xnb", [T, C], BF16, kind="ExternalOutput"), "xnb")
        lm1_d = Buf(nc.dram_tensor("lm1", [1, 1], F32, kind="ExternalOutput"), "lm1")
        wgq_d = Buf(nc.dram_tensor("wgq", [384, C], BF16, kind="ExternalOutput"), "wgq")
        wuq_d = Buf(nc.dram_tensor("wuq", [384, C], BF16, kind="ExternalOutput"), "wuq")
        wdq_d = Buf(nc.dram_tensor("wdq", [128, I], BF16, kind="ExternalOutput"), "wdq")
        aw_in = {p: Buf(nc.dram_tensor(f"W{p}", [128, C], F32, kind="ExternalInput"),
                        f"W{p}") for p in "rkvo"}
        aw_out = {p: Buf(nc.dram_tensor(f"wbf{p}", [128, C], BF16,
                                        kind="ExternalOutput"), f"wbf{p}")
                  for p in "rkvo"}
        colscr = b.dram("colscr", [128], F32)

        xt = [b.sbuf(f"xt{i}", [128, C], F32) for i in range(2)]
        xnf = [b.sbuf(f"xnf{i}", [128, C], F32) for i in range(2)]
        xnb = [b.sbuf(f"xnb{i}", [128, C], BF16) for i in range(2)]
        sq = [b.sbuf(f"sq{i}", [128, C], F32) for i in range(2)]
        ss = [b.sbuf(f"ss{i}", [128, 1], F32) for i in range(2)]
        sm = [b.sbuf(f"sm{i}", [128, 1], F32) for i in range(2)]
        mcol = [b.sbuf(f"mcol{i}", [128, 1], F32) for i in range(2)]
        vcol = [b.sbuf(f"vcol{i}", [128, 1], F32) for i in range(2)]
        sdcol = [b.sbuf(f"sdcol{i}", [128, 1], F32) for i in range(2)]
        rstd = [b.sbuf(f"rstd{i}", [128, 1], F32) for i in range(2)]
        nmr = [b.sbuf(f"nmr{i}", [128, 1], F32) for i in range(2)]
        mxc = b.sbuf("mxc", [128, 1], F32)
        mxacc = b.sbuf("mxacc", [128, 1], F32)
        row = b.sbuf("rowb", [1, 128], F32)
        one1 = b.sbuf("one1", [1, 1], F32)
        magic_col = b.sbuf("magic_col", [128, 1], F32)
        NPAR = 4
        gwst = [b.sbuf(f"gwst{i}", [128, I], F32) for i in range(NPAR)]
        gqo = [b.sbuf(f"gqo{i}", [128, I], BF16) for i in range(NPAR)]
        rmaxc = [b.sbuf(f"rmaxc{i}", [128, 1], F32) for i in range(NPAR)]
        rscl = [b.sbuf(f"rscl{i}", [128, 1], F32) for i in range(NPAR)]
        rinv = [b.sbuf(f"rinv{i}", [128, 1], F32) for i in range(NPAR)]

        b.emit("g", lambda g: g.memset(mxacc[:, :], 0.0), writes=[mxacc])
        b.emit("g", lambda g: g.memset(magic_col[:, :], MAGIC), writes=[magic_col])
        awst = [b.sbuf(f"awst{i}", [128, C], BF16) for i in range(2)]
        for pi, p in enumerate("rkvo"):
            st = awst[pi % 2]
            b.emit("g", lambda g, p=p, st=st: g.dma_start(
                out=st[:, :], in_=aw_in[p].t[0:128, :]),
                reads=[], writes=[st], sem=f"dawi{pi % 2}", dma=True)
            b.emit("y", lambda s, p=p, st=st: s.dma_start(
                out=aw_out[p].t[0:128, :], in_=st[:, :]),
                reads=[st], writes=[], sem=f"dawo{pi % 2}", dma=True)

        _unit_ctr = [0]

        def make_wq_unit(wsrc, wdst, c, ncols, sem_pref):
            def unit():
                par = _unit_ctr[0] % NPAR
                _unit_ctr[0] += 1
                st = gwst[par]
                qo = gqo[par]
                rm, rs, ri = rmaxc[par], rscl[par], rinv[par]
                b.emit("g", lambda g: g.dma_start(
                    out=st[:, 0:ncols], in_=wsrc.t[c * 128:(c + 1) * 128, :]),
                    reads=[], writes=[st], sem=f"dgw{par}", dma=True)
                b.emit("v", lambda v: v.reduce_max(
                    rm[:, :], st[:, 0:ncols], axis=AX, apply_absolute_value=True),
                    reads=[st], writes=[rm])
                b.emit("v", lambda v: v.tensor_scalar_mul(
                    rs[:, :], rm[:, :], 1.0 / QMAX),
                    reads=[rm], writes=[rs])
                b.emit("v", lambda v: v.reciprocal(ri[:, :], rs[:, :]),
                       reads=[rs], writes=[ri])
                b.emit("s", lambda s: s.activation(
                    st[:, 0:ncols], st[:, 0:ncols], AF.Identity,
                    bias=magic_col[:, 0:1], scale=ri[:, 0:1]),
                    reads=[st, ri, magic_col], writes=[st])
                b.emit("v", lambda v: v.tensor_scalar(
                    st[:, 0:ncols], st[:, 0:ncols], MAGIC, -(QMAX + 1.0),
                    ALU.subtract, ALU.max),
                    reads=[st], writes=[st])
                b.emit("v", lambda v: v.tensor_scalar(
                    qo[:, 0:ncols], st[:, 0:ncols], QMAX, rs[:, 0:1],
                    ALU.min, ALU.mult),
                    reads=[st, rs], writes=[qo])
                b.emit("y", lambda s: s.dma_start(
                    out=wdst.t[c * 128:(c + 1) * 128, :], in_=qo[:, 0:ncols]),
                    reads=[qo], writes=[], sem=sem_pref + str(par % 2), dma=True)
            return unit

        wq_units = []
        for c in range(3):
            wq_units.append(make_wq_unit(wg_d, wgq_d, c, C, "dwgq"))
            wq_units.append(make_wq_unit(wu_d, wuq_d, c, C, "dwuq"))
        wq_units.append(make_wq_unit(wd_d, wdq_d, 0, I, "dwdq"))
        wq_iter = iter(wq_units)
        for k in range(NTT):
            pp = k % 2
            x = xt[pp]
            xf = xnf[pp]
            xb = xnb[pp]
            sq_, ss_, sm_ = sq[pp], ss[pp], sm[pp]
            mc, vc, sd_, rs_, nm_ = mcol[pp], vcol[pp], sdcol[pp], rstd[pp], nmr[pp]
            b.emit("g", lambda g, k=k, x=x: g.dma_start(
                out=x[:, :], in_=x_d.t[k * 128:(k + 1) * 128, :]),
                reads=[], writes=[x], sem=f"dx{k % 2}", dma=True)
            b.emit("s", lambda s, x=x, sq_=sq_, ss_=ss_: s.activation(
                sq_[:, :], x[:, :], AF.Square, accum_out=ss_[:, 0:1]),
                reads=[x], writes=[sq_, ss_])
            b.emit("v", lambda v, x=x, sm_=sm_: v.reduce_sum(
                sm_[:, :], x[:, :], axis=AX),
                reads=[x], writes=[sm_])
            b.emit("v", lambda v, sm_=sm_, mc=mc: v.tensor_scalar_mul(
                mc[:, :], sm_[:, :], 1.0 / C),
                reads=[sm_], writes=[mc])
            b.emit("v", lambda v, ss_=ss_, vc=vc: v.tensor_scalar_mul(
                vc[:, :], ss_[:, :], 1.0 / C),
                reads=[ss_], writes=[vc])
            b.emit("v", lambda v, mc=mc, sq_=sq_: v.scalar_tensor_tensor(
                sq_[:, 0:1], mc[:, :], -1.0, mc[:, :], ALU.mult, ALU.mult),
                reads=[mc], writes=[sq_])
            b.emit("v", lambda v, vc=vc, sq_=sq_: v.tensor_add(
                vc[:, :], vc[:, :], sq_[:, 0:1]),
                reads=[sq_], writes=[vc])
            b.emit("v", lambda v, vc=vc: v.tensor_scalar_add(vc[:, :], vc[:, :], 1e-5),
                   reads=[vc], writes=[vc])
            b.emit("s", lambda s, vc=vc, sd_=sd_: s.activation(
                sd_[:, :], vc[:, :], AF.Sqrt),
                reads=[vc], writes=[sd_])
            b.emit("v", lambda v, sd_=sd_, rs_=rs_: v.reciprocal(
                rs_[:, :], sd_[:, :]),
                reads=[sd_], writes=[rs_])
            b.emit("v", lambda v, mc=mc, rs_=rs_, nm_=nm_: v.scalar_tensor_tensor(
                nm_[:, :], mc[:, :], -1.0, rs_[:, :], ALU.mult, ALU.mult),
                reads=[mc, rs_], writes=[nm_])
            b.emit("v", lambda v, x=x, xf=xf, rs_=rs_, nm_=nm_: v.tensor_scalar(
                xf[:, :], x[:, :], rs_[:, 0:1], nm_[:, 0:1], ALU.mult, ALU.add),
                reads=[x, rs_, nm_], writes=[xf])
            b.emit("v", lambda v, xf=xf: v.reduce_max(
                mxc[:, :], xf[:, :], axis=AX, apply_absolute_value=True),
                reads=[xf], writes=[mxc])
            b.emit("v", lambda v: v.tensor_max(mxacc[:, :], mxacc[:, :], mxc[:, :]),
                   reads=[mxc, mxacc], writes=[mxacc])
            b.emit("s", lambda s, xf=xf, xb=xb: s.activation(
                xb[:, :], xf[:, :], AF.Copy),
                reads=[xf], writes=[xb])
            b.emit("y", lambda s, k=k, xf=xf: s.dma_start(
                out=xnf_d.t[k * 128:(k + 1) * 128, :], in_=xf[:, :]),
                reads=[xf], writes=[], sem=f"doxf{k % 2}", dma=True)
            b.emit("y", lambda s, k=k, xb=xb: s.dma_start(
                out=xnb_d.t[k * 128:(k + 1) * 128, :], in_=xb[:, :]),
                reads=[xb], writes=[], sem=f"doxb{k % 2}", dma=True)
            u = next(wq_iter, None)
            if u is not None:
                u()
        for u in wq_iter:
            u()
        b.emit("y", lambda s: s.dma_start(out=colscr.t[:], in_=mxacc[:, 0:1]),
               reads=[mxacc], writes=[colscr], sem="dcs", dma=True)
        b.emit("y", lambda s: s.dma_start(out=row[0:1, :], in_=colscr.t[:]),
               reads=[colscr], writes=[row], sem="drow", dma=True)
        b.emit("v", lambda v: v.reduce_max(one1[0:1, 0:1], row[0:1, :], axis=AX),
               reads=[row], writes=[one1])
        b.emit("g", lambda g: g.dma_start(out=lm1_d.t[0:1, 0:1], in_=one1[0:1, 0:1]),
               reads=[one1], writes=[lm1_d], sem="dlm", dma=True)
        b.final_wait("g", [b.tok("doxf0"), b.tok("doxf1"), b.tok("doxb0"),
                           b.tok("doxb1"), b.tok("dlm"),
                           b.tok("dwgq0"), b.tok("dwgq1"), b.tok("dwuq0"),
                           b.tok("dwuq1"), b.tok("dwdq0"), b.tok("dwdq1")])
        block = ctx.enter_context(nc.Block())
        b.build(block)
    return nc


def build_l2():
    nc = bass.Bass("TRN2", target_bir_lowering=False)
    ctx = contextlib.ExitStack()
    with ctx:
        b = Bld(nc, ctx)
        D = lambda n, sh, dt, k="ExternalInput": Buf(nc.dram_tensor(n, sh, dt, kind=k), n)
        x_d = D("x", [T, C], F32)
        xnf_d = D("xnf", [T, C], F32)
        xnb_d = D("xnb", [T, C], BF16)
        lm1_d = D("lm1", [1, 8], F32)
        wbf = {p: D(f"wbf{p}", [C, C], BF16) for p in "rkvo"}
        br_d, bk_d, bv_d, bo_d = (D(n, [C], F32) for n in ("br", "bk", "bv", "bo"))
        td_d, U_d = D("td", [C], F32), D("Uv", [C], F32)
        clipx_d = D("clipx", [1], F32)
        wgq_dram, wuq_dram = D("wgq", [I, C], BF16), D("wuq", [I, C], BF16)
        acc_d = D("acc", [T, C], F32, "ExternalOutput")
        h_d = D("h", [I, T], F16, "ExternalOutput")
        lm2_d = D("lm2", [1, 1], F32, "ExternalOutput")
        qdram = b.dram("qdram", [T, C], BF16)
        colscr = b.dram("colscr", [128], F32)
        scaldr = b.dram("scaldr", [1, 2], F32)

        xnF = [b.sbuf(f"xnF{c}", [128, T], BF16) for c in range(NCH)]
        OWF = [b.sbuf(f"OWF{c}", [128, T], BF16) for c in range(NCH)]
        ramp = b.sbuf("ramp", [128, T], F32)
        bo_bc = b.sbuf("bo_bc", [128, C], F32)

        cols = {n: b.sbuf(f"col_{n}", [128, NCH], F32)
                for n in ("td", "U", "bk", "br", "bv", "w", "ubk")}
        hmax = b.sbuf("hmaxc", [128, 1], F32)
        hmx1 = b.sbuf("hmx1", [128, 1], F32)
        sx_col = b.sbuf("sx_col", [128, 1], F32)
        isx_col = b.sbuf("isx_col", [128, 1], F32)
        magic_col = b.sbuf("magic_col", [128, 1], F32)
        ecols = {}
        for nm, val in (("k", EPS_K), ("v", EPS_V), ("r", EPS_R)):
            ecols[nm] = b.sbuf(f"eps_{nm}", [128, 1], F32)
        aprevA = b.sbuf("aprevA", [128, 1], F32)
        aprevB = b.sbuf("aprevB", [128, 1], F32)
        row = b.sbuf("rowb", [1, 128], F32)
        one1 = b.sbuf("one1", [1, 1], F32)
        one2 = b.sbuf("one2", [1, 1], F32)
        lmrow = b.sbuf("lmrow", [1, 8], F32)
        sc2 = b.sbuf("sc2", [1, 2], F32)

        R1 = b.region("ov1", 24 * 2048)
        WT = {p: [b.at(R1, (p_i * 8 + c) * 2048, f"W{p}T{c}", [128, C], BF16)
                  for c in range(NCH)]
              for p_i, p in enumerate("kvr")}
        NS = 16
        R2 = b.region("ov2", NS * 4096)
        S = [b.at(R2, i * 4096, f"scr{i}", [128, H], F32) for i in range(NS)]

        ps = [b.psum(f"ps{i}", [128, T], F32) for i in range(2)]

        GD, SY = "g", "y"

        b.emit("g", lambda g: g.memset(magic_col[:, :], MAGIC), writes=[magic_col])
        for nm, val in (("k", EPS_K), ("v", EPS_V), ("r", EPS_R)):
            b.emit("g", lambda g, nm=nm, val=val: g.memset(ecols[nm][:, :], val),
                   writes=[ecols[nm]])
        b.emit("g", lambda g: g.iota(ramp[:, :], [[1, T]], base=-(T - 1),
                                     channel_multiplier=0,
                                     allow_small_or_imprecise_dtypes=True),
               writes=[ramp])
        b.emit("y", lambda s: s.dma_start(
            out=bo_bc[:, :], in_=bass.AP(bo_d.t, 0, [[0, 128], [1, C]])),
            reads=[], writes=[bo_bc], sem="dbo", dma=True)
        for n, d in (("td", td_d), ("U", U_d), ("bk", bk_d), ("br", br_d),
                     ("bv", bv_d)):
            b.emit("g", lambda g, n=n, d=d: g.dma_start(
                out=cols[n][:, :], in_=d.t.ap().rearrange("(f p) -> p f", p=128),
                allow_slow_non_contiguous=True),
                reads=[], writes=[cols[n]], sem=f"dc_{n}", dma=True)
        b.emit("s", lambda s: s.activation(cols["w"][:, :], cols["td"][:, :], AF.Relu),
               reads=[cols["td"]], writes=[cols["w"]])
        b.emit("v", lambda v: v.tensor_add(cols["ubk"][:, :], cols["U"][:, :],
                                           cols["bk"][:, :]),
               reads=[cols["U"], cols["bk"]], writes=[cols["ubk"]])
        b.emit("g", lambda g: g.dma_start(out=lmrow[0:1, :], in_=lm1_d.t[0:1, :]),
               reads=[], writes=[lmrow], sem="dlm1", dma=True)
        b.emit("g", lambda g: g.dma_start(out=one1[0:1, 0:1], in_=clipx_d.t[0:1]),
               reads=[], writes=[one1], sem="dcx", dma=True)
        b.emit("v", lambda v: v.reduce_max(one2[0:1, 0:1], lmrow[0:1, :], axis=AX),
               reads=[lmrow], writes=[one2])
        b.emit("s", lambda s: s.activation(one1[0:1, 0:1], one1[0:1, 0:1], AF.Sigmoid),
               reads=[one1], writes=[one1])
        b.emit("v", lambda v: v.tensor_mul(sc2[0:1, 0:1], one1[0:1, 0:1],
                                           one2[0:1, 0:1]),
               reads=[one1, one2], writes=[sc2])
        b.emit("v", lambda v: v.tensor_scalar_mul(sc2[0:1, 0:1], sc2[0:1, 0:1],
                                                  1.0 / QMAX),
               reads=[sc2], writes=[sc2])
        b.emit("v", lambda v: v.reciprocal(sc2[0:1, 1:2], sc2[0:1, 0:1]),
               reads=[sc2], writes=[sc2])
        b.emit("y", lambda s: s.dma_start(out=scaldr.t[0:1, :], in_=sc2[0:1, :]),
               reads=[sc2], writes=[scaldr], sem="dscw", dma=True)
        b.emit("y", lambda s: s.dma_start(
            out=sx_col[:, 0:1], in_=bass.AP(scaldr.t, 0, [[0, 128], [1, 1]])),
            reads=[scaldr], writes=[sx_col], sem="dsx", dma=True)
        b.emit("y", lambda s: s.dma_start(
            out=isx_col[:, 0:1], in_=bass.AP(scaldr.t, 1, [[0, 128], [1, 1]])),
            reads=[scaldr], writes=[isx_col], sem="disx", dma=True)

        for c in range(NCH):
            b.emit("y", lambda s, c=c: s.dma_start_transpose(
                xnF[c][:, :], xnb_d.t[0:T, c * 128:(c + 1) * 128]),
                reads=[], writes=[xnF[c]], sem=f"dxnF{c % 4}", dma=True)
        xnF_toks = [b.tok(f"dxnF{i}") for i in range(4)]
        for c in range(NCH):
            xnF[c].w = xnF_toks[0]
            xnF[c].extra = xnF_toks[1:]
        wo_store_toks = []
        for p in "kvr":
            for c in range(NCH):
                b.emit("y", lambda s, p=p, c=c: s.dma_start_transpose(
                    WT[p][c][:, :], wbf[p].t[0:C, c * 128:(c + 1) * 128]),
                    reads=[wbf[p]], writes=[WT[p][c]], sem=f"dwt{p}{c % 4}",
                    dma=True)
            tkk = [b.tok(f"dwt{p}{i}") for i in range(4)]
            for c in range(NCH):
                WT[p][c].w = tkk[0]
                WT[p][c].extra = tkk[1:]

        def proj_mm(tt, pbuf, p, j):
            last = None
            for c in range(NCH):
                for q in range(4):
                    last = tt.matmul(
                        pbuf[:, q * 512:(q + 1) * 512],
                        WT[p][c][:, j * 128:(j + 1) * 128],
                        xnF[c][:, q * 512:(q + 1) * 512],
                        start=(c == 0), stop=(c == NCH - 1))
            return last

        XF = [b.at(R2, 32768 + i * 4096, f"xf32_{i}", [128, C], F32)
              for i in range(2)]
        QS = [b.at(R2, 40960 + i * 2048, f"qs{i}", [128, C], BF16)
              for i in range(2)]
        wstore_toks = {}

        def make_xq_unit(k):
            def unit():
                xf = XF[k % 2]
                q = QS[k % 2]
                b.emit("s", lambda s: s.dma_start(
                    out=xf[:, :], in_=xnf_d.t[k * 128:(k + 1) * 128, :]),
                    reads=[], writes=[xf], sem=f"dxf{k % 2}", dma=True)
                b.emit("s", lambda s: s.activation(
                    xf[:, :], xf[:, :], AF.Identity, bias=magic_col[:, 0:1],
                    scale=isx_col[:, 0:1]),
                    reads=[xf, isx_col, magic_col], writes=[xf])
                b.emit("v", lambda v: v.tensor_scalar(
                    xf[:, :], xf[:, :], MAGIC, -(QMAX + 1.0), ALU.subtract, ALU.max),
                    reads=[xf], writes=[xf])
                b.emit("v", lambda v: v.tensor_scalar_min(
                    q[:, :], xf[:, :], QMAX),
                    reads=[xf], writes=[q])
                b.emit("y", lambda s: s.dma_start(
                    out=qdram.t[k * 128:(k + 1) * 128, :], in_=q[:, :]),
                    reads=[q], writes=[], sem=f"dqo{k % 2}", dma=True)
            return unit

        xq_units = [make_xq_unit(k) for k in range(NTT)]

        npsum = [0]

        def next_ps():
            p = ps[npsum[0] % 2]
            npsum[0] += 1
            return p

        SA = [b.sbuf(f"scanA{i}", [128, H + 1], F32) for i in range(2)]
        SB = [b.sbuf(f"scanB{i}", [128, H + 1], F32) for i in range(2)]
        b.emit("g", lambda g: g.memset(SA[0][:, 0:1], 0.0), writes=[SA[0]])
        b.emit("g", lambda g: g.memset(SB[0][:, 0:1], 0.0), writes=[SB[0]])
        KFh = [S[1], S[2]]
        VFh = [S[3], S[4]]
        SGRh = [S[5], S[6]]
        MW = [(S[7], S[8], S[9], S[10]), (S[11], S[12], S[13], S[14])]
        T1a, T1b = S[0], S[15]
        RW = (MW[0][0], MW[1][0])

        def emit_mm(p, j):
            pbuf = next_ps()
            b.emit("t", lambda tt, pbuf=pbuf, p=p, j=j: proj_mm(tt, pbuf, p, j),
                   reads=[WT[p][c] for c in range(NCH)] + xnF, writes=[pbuf])
            return pbuf

        def emit_evac(pbuf, eps, ecol_b, bias_b, j, d0, d1):
            ec = ecol_b[:, 0:1]
            bj = bias_b[:, j:j + 1]
            b.emit("v", lambda v, bj=bj: v.tensor_copy(T1a[:, 0:1], bj),
                   reads=[bias_b], writes=[T1a])
            b.emit("s", lambda s, pbuf=pbuf, ec=ec, bj=bj: s.activation(
                T1a[:, 1:H], pbuf[:, 0:H - 1], AF.Identity, bias=bj, scale=ec),
                reads=[pbuf, bias_b, ecol_b], writes=[T1a])
            b.emit("s", lambda s, pbuf=pbuf, ec=ec, bj=bj: s.activation(
                T1b[:, 0:H], pbuf[:, H - 1:T - 1], AF.Identity, bias=bj, scale=ec),
                reads=[pbuf, bias_b, ecol_b], writes=[T1b])
            b.emit("v", lambda v, pbuf=pbuf, eps=eps, d0=d0: v.scalar_tensor_tensor(
                d0[:, 0:H], pbuf[:, 1:H + 1], 1.0 - eps, T1a[:, 0:H],
                ALU.mult, ALU.add),
                reads=[pbuf, T1a], writes=[d0])
            b.emit("v", lambda v, pbuf=pbuf, eps=eps, d1=d1: v.scalar_tensor_tensor(
                d1[:, 0:H - 1], pbuf[:, H + 1:T], 1.0 - eps, T1b[:, 0:H - 1],
                ALU.mult, ALU.add),
                reads=[pbuf, T1b], writes=[d1])
            b.emit("v", lambda v, d1=d1: v.tensor_copy(
                d1[:, H - 1:H], T1b[:, H - 1:H]),
                reads=[T1b], writes=[d1])

        def emit_rinterp(pR, j):
            emit_evac(pR, EPS_R, ecols["r"], cols["br"], j, RW[0], RW[1])
            for hh in range(2):
                b.emit("s", lambda s, hh=hh: s.activation(
                    SGRh[hh][:, :], RW[hh][:, :], AF.Sigmoid),
                    reads=[RW[hh]], writes=[SGRh[hh]])

        def pre_k(j):
            p = emit_mm("k", j)
            emit_evac(p, EPS_K, ecols["k"], cols["bk"], j, KFh[0], KFh[1])

        def pre_v(j):
            p = emit_mm("v", j)
            emit_evac(p, EPS_V, ecols["v"], cols["bv"], j, VFh[0], VFh[1])

        def pre_r(j):
            p = emit_mm("r", j)
            emit_rinterp(p, j)

        def emit_middle(j, ins1=None, ins2=None, ins3=None):
            wj = cols["w"][:, j:j + 1]
            ubkj = cols["ubk"][:, j:j + 1]
            HV = [dict(W=MW[hh], kf=KFh[hh], vf=VFh[hh], sgr=SGRh[hh],
                       sa=SA[hh], sb=SB[hh], lo=hh * H) for hh in range(2)]

            def A1(hh):
                W4 = HV[hh]["W"][3]
                kf = HV[hh]["kf"]
                b.emit("s", lambda s, W4=W4, kf=kf, ubkj=ubkj: s.activation(
                    W4[:, :], kf[:, :], AF.Identity, bias=ubkj),
                    reads=[kf, cols["ubk"]], writes=[W4])

            def D1(hh):
                W1 = HV[hh]["W"][0]
                kf = HV[hh]["kf"]
                lo = HV[hh]["lo"]
                b.emit("v", lambda v, W1=W1, kf=kf, lo=lo, wj=wj:
                       v.scalar_tensor_tensor(W1[:, :], ramp[:, lo:lo + H], wj,
                                              kf[:, :], ALU.mult, ALU.add),
                       reads=[ramp, cols["w"], kf], writes=[W1])

            def DCLIP(buf):
                b.emit("v", lambda v, buf=buf: v.tensor_scalar(
                    buf[:, :], buf[:, :], -20.0, 10.0, ALU.max, ALU.min),
                    reads=[buf], writes=[buf])

            def A2(hh):
                W1, W2 = HV[hh]["W"][0], HV[hh]["W"][1]
                b.emit("s", lambda s, W1=W1, W2=W2: s.activation(
                    W2[:, :], W1[:, :], AF.Exp), reads=[W1], writes=[W2])

            def A3(hh):
                W4 = HV[hh]["W"][3]
                b.emit("s", lambda s, W4=W4: s.activation(
                    W4[:, :], W4[:, :], AF.Exp), reads=[W4], writes=[W4])

            def D4(hh):
                W2, W3 = HV[hh]["W"][1], HV[hh]["W"][2]
                vf = HV[hh]["vf"]
                b.emit("v", lambda v, W2=W2, W3=W3, vf=vf: v.tensor_mul(
                    W3[:, :], W2[:, :], vf[:, :]), reads=[W2, vf], writes=[W3])

            def DSCAN(hh, which):
                src = HV[hh]["W"][2] if which == 0 else HV[hh]["W"][1]
                dst = HV[hh]["sa"] if which == 0 else HV[hh]["sb"]
                if hh == 0:
                    b.emit("v", lambda v, src=src, dst=dst: v.tensor_tensor_scan(
                        dst[:, 1:H + 1], src[:, :], src[:, :], 0.0,
                        ALU.add, ALU.bypass),
                        reads=[src], writes=[dst])
                else:
                    prev = SA[0] if which == 0 else SB[0]
                    b.emit("v", lambda v, src=src, dst=dst, prev=prev:
                           v.tensor_tensor_scan(dst[:, 1:H + 1], src[:, :],
                                                src[:, :], prev[:, H:H + 1],
                                                ALU.add, ALU.bypass),
                           reads=[src, prev], writes=[dst])

            def DCARRY(which):
                dst = SA[1] if which == 0 else SB[1]
                prev = SA[0] if which == 0 else SB[0]
                b.emit("v", lambda v, dst=dst, prev=prev: v.tensor_copy(
                    dst[:, 0:1], prev[:, H:H + 1]), reads=[prev], writes=[dst])

            def D7(hh):
                W3, W4 = HV[hh]["W"][2], HV[hh]["W"][3]
                vf = HV[hh]["vf"]
                b.emit("v", lambda v, W3=W3, W4=W4, vf=vf: v.tensor_mul(
                    W3[:, :], W4[:, :], vf[:, :]), reads=[W4, vf], writes=[W3])

            def D8(hh):
                W2, W3 = HV[hh]["W"][1], HV[hh]["W"][2]
                sa = HV[hh]["sa"]
                b.emit("v", lambda v, W2=W2, W3=W3, sa=sa: v.tensor_add(
                    W2[:, :], sa[:, 0:H], W3[:, :]), reads=[sa, W3], writes=[W2])

            def D9(hh):
                W1, W4 = HV[hh]["W"][0], HV[hh]["W"][3]
                sb = HV[hh]["sb"]
                b.emit("v", lambda v, W1=W1, W4=W4, sb=sb: v.tensor_add(
                    W1[:, :], sb[:, 0:H], W4[:, :]), reads=[sb, W4], writes=[W1])

            def A4(hh):
                W1, W3 = HV[hh]["W"][0], HV[hh]["W"][2]
                b.emit("s", lambda s, W1=W1, W3=W3: s.activation(
                    W3[:, :], W1[:, :], AF.Ln), reads=[W1], writes=[W3])

            def A5(hh):
                W1, W3 = HV[hh]["W"][0], HV[hh]["W"][2]
                b.emit("s", lambda s, W1=W1, W3=W3: s.activation(
                    W1[:, :], W3[:, :], AF.Exp, scale=-1.0),
                    reads=[W3], writes=[W1])

            def D10(hh):
                W1, W2, W3 = HV[hh]["W"][0], HV[hh]["W"][1], HV[hh]["W"][2]
                b.emit("v", lambda v, W1=W1, W2=W2, W3=W3: v.tensor_mul(
                    W3[:, :], W2[:, :], W1[:, :]), reads=[W2, W1], writes=[W3])

            def D11(hh):
                W3 = HV[hh]["W"][2]
                sgr = HV[hh]["sgr"]
                lo = HV[hh]["lo"]
                b.emit("v", lambda v, W3=W3, sgr=sgr, lo=lo, j=j: v.tensor_mul(
                    OWF[j][:, lo:lo + H], W3[:, :], sgr[:, :]),
                    reads=[W3, sgr], writes=[OWF[j]])

            for hh in range(2):
                A1(hh)
                D1(hh)
                DCLIP(HV[hh]["W"][0])
                A2(hh)
            for hh in range(2):
                DCLIP(HV[hh]["W"][3])
                A3(hh)
            D4(0)
            DSCAN(0, 0)
            DSCAN(0, 1)
            D4(1)
            D7(0)
            D8(0)
            D9(0)
            A4(0)
            DSCAN(1, 0)
            DSCAN(1, 1)
            DCARRY(0)
            DCARRY(1)
            A5(0)
            if ins1 is not None:
                ins1()
            D7(1)
            D8(1)
            D9(1)
            A4(1)
            D10(0)
            D11(0)
            A5(1)
            if ins2 is not None:
                ins2()
            D10(1)
            D11(1)
            if ins3 is not None:
                ins3()

        pre_k(0)
        pre_v(0)
        pre_r(0)
        for j in range(NCH):
            n = j + 1
            emit_middle(j,
                        ins1=(lambda n=n: pre_k(n)) if n < NCH else None,
                        ins2=(lambda n=n: pre_v(n)) if n < NCH else None,
                        ins3=(lambda n=n: pre_r(n)) if n < NCH else None)

        WoT = [b.at(R1, c * 2048, f"WoT{c}", [128, C], BF16) for c in range(NCH)]
        GT = {w: [[b.at(R1, 16384 + (wi * 2 + bb) * NCH * 1024 + c * 1024,
                        f"{w}T{bb}_{c}", [128, 512], BF16)
                   for c in range(NCH)] for bb in range(2)]
              for wi, w in enumerate(("g", "u"))}
        old_wt = [WT[p][c] for p in "kvr" for c in range(NCH)]
        b.handoff(WoT + GT["g"][0] + GT["g"][1] + GT["u"][0] + GT["u"][1],
                  old_wt)

        for c in range(NCH):
            b.emit("y", lambda s, c=c: s.dma_start_transpose(
                WoT[c][:, :], wbf["o"].t[0:C, c * 128:(c + 1) * 128]),
                reads=[wbf["o"]], writes=[WoT[c]], sem="dwt_o", dma=True,
                extra_waits=wo_store_toks)
        b.settle(WoT, b.tok("dwt_o"))

        def load_gt_group(grp):
            gn = min(4, NIC - grp * 4)
            bb = grp % 2
            for wname, wdram in (("g", wgq_dram), ("u", wuq_dram)):
                for c in range(NCH):
                    b.emit("y", lambda s, wname=wname, wdram=wdram, c=c,
                           grp=grp, bb=bb, gn=gn: s.dma_start_transpose(
                               GT[wname][bb][c][:, 0:gn * 128],
                               wdram.t[grp * 512:grp * 512 + gn * 128,
                                       c * 128:(c + 1) * 128]),
                           reads=[], writes=[GT[wname][bb][c]],
                           sem=f"dgt{wname}{(bb * 2 + c % 2)}", dma=True)
            for wname in ("g", "u"):
                tks = [b.tok(f"dgt{wname}{bb * 2}"),
                       b.tok(f"dgt{wname}{bb * 2 + 1}")]
                for c in range(NCH):
                    GT[wname][bb][c].w = tks[0]
                    GT[wname][bb][c].extra = [tks[1]]

        load_gt_group(0)
        load_gt_group(1)

        XR = [b.at(R2, i * 4096, f"xr{i}", [128, C], F32) for i in range(2)]
        AST = [b.at(R2, 8192 + i * 4096, f"ast{i}", [128, C], F32) for i in range(2)]
        SGH = [b.at(R2, 16384 + i * 4096, f"sg{i}", [128, H], F32) for i in range(2)]
        HST = [b.at(R2, 24576 + i * 4096, f"hst{i}", [128, T], F16) for i in range(2)]
        b.handoff(XR + AST + SGH + HST + XF + QS, S)

        fi = iter(xq_units)
        for k in range(NTT):
            pW = next_ps()
            ast = AST[k % 2]
            x = XR[k % 2]
            b.emit("g", lambda g, k=k, x=x: g.dma_start(
                out=x[:, :], in_=x_d.t[k * 128:(k + 1) * 128, :]),
                reads=[], writes=[x], sem=f"dx{k % 2}", dma=True)
            b.emit("v", lambda v, x=x: v.tensor_add(
                x[:, :], x[:, :], bo_bc[:, :]),
                reads=[bo_bc, x], writes=[x])

            def womm(tt, pW=pW, k=k):
                last = None
                for c in range(NCH):
                    for q in range(2):
                        last = tt.matmul(
                            pW[:, q * 512:(q + 1) * 512],
                            OWF[c][:, k * 128:(k + 1) * 128],
                            WoT[c][:, q * 512:(q + 1) * 512],
                            start=(c == 0), stop=(c == NCH - 1))
                return last
            b.emit("t", womm, reads=OWF + WoT, writes=[pW])
            b.emit("v", lambda v, pW=pW, ast=ast, x=x: v.tensor_add(
                ast[:, :], pW[:, 0:C], x[:, :]),
                reads=[pW, x], writes=[ast])
            b.emit("y", lambda s, k=k, ast=ast: s.dma_start(
                out=acc_d.t[k * 128:(k + 1) * 128, :], in_=ast[:, :]),
                reads=[ast], writes=[], sem=f"dacc{k % 2}", dma=True)
            if k < 8:
                for _ in range(2):
                    u = next(fi, None)
                    if u is not None:
                        u()
            if k == 10:
                qtk = [b.tok("dqo0"), b.tok("dqo1")]
                for c in range(NCH):
                    b.emit("y", lambda s, c=c: s.dma_start_transpose(
                        xnF[c][:, :], qdram.t[0:T, c * 128:(c + 1) * 128]),
                        reads=[], writes=[xnF[c]], sem=f"dxqF{c % 4}",
                        dma=True, extra_waits=qtk)
                xqtk = [b.tok(f"dxqF{i}") for i in range(4)]
                for c in range(NCH):
                    xnF[c].w = xqtk[0]
                    xnF[c].extra = xqtk[1:]
        for u in fi:
            u()

        b.emit("g", lambda g: g.memset(hmax[:, :], 0.0), writes=[hmax])
        for j in range(NIC):
            grp, s_in = j // 4, j % 4
            bb = grp % 2
            if s_in == 0 and grp >= 1 and grp + 1 <= (NIC - 1) // 4:
                load_gt_group(grp + 1)
            pG = next_ps()
            pU = next_ps()
            sl = slice(s_in * 128, (s_in + 1) * 128)

            def gmm(tt, pG=pG, bb=bb, sl=sl):
                last = None
                for c in range(NCH):
                    for q in range(4):
                        last = tt.matmul(
                            pG[:, q * 512:(q + 1) * 512],
                            GT["g"][bb][c][:, sl],
                            xnF[c][:, q * 512:(q + 1) * 512],
                            start=(c == 0), stop=(c == NCH - 1))
                return last

            def umm(tt, pU=pU, bb=bb, sl=sl):
                last = None
                for c in range(NCH):
                    for q in range(4):
                        last = tt.matmul(
                            pU[:, q * 512:(q + 1) * 512],
                            GT["u"][bb][c][:, sl],
                            xnF[c][:, q * 512:(q + 1) * 512],
                            start=(c == 0), stop=(c == NCH - 1))
                return last
            b.emit("t", gmm, reads=[GT["g"][bb][c] for c in range(NCH)] + xnF,
                   writes=[pG])
            b.emit("t", umm, reads=[GT["u"][bb][c] for c in range(NCH)] + xnF,
                   writes=[pU])
            hst = HST[j % 2]
            for hh in range(2):
                sg = SGH[hh]
                lo = hh * H
                b.emit("s", lambda s, pG=pG, lo=lo, sg=sg: s.activation(
                    sg[:, :], pG[:, lo:lo + H], AF.Silu, scale=sx_col[:, 0:1]),
                    reads=[pG, sx_col], writes=[sg])
                b.emit("v", lambda v, pU=pU, lo=lo, sg=sg, hst=hst:
                       v.scalar_tensor_tensor(
                           hst[:, lo:lo + H], pU[:, lo:lo + H], sx_col[:, 0:1],
                           sg[:, :], ALU.mult, ALU.mult),
                       reads=[pU, sg, sx_col], writes=[hst])
            b.emit("v", lambda v, hst=hst: v.reduce_max(
                hmx1[:, :], hst[:, :], axis=AX, apply_absolute_value=True),
                reads=[hst], writes=[hmx1])
            b.emit("v", lambda v: v.tensor_max(hmax[:, :], hmax[:, :], hmx1[:, :]),
                   reads=[hmx1, hmax], writes=[hmax])
            b.emit("g", lambda g, j=j, hst=hst: g.dma_start(
                out=h_d.t[j * 128:(j + 1) * 128, :], in_=hst[:, :]),
                reads=[hst], writes=[], sem=f"dho{j % 2}", dma=True)

        b.emit("y", lambda s: s.dma_start(out=colscr.t[:], in_=hmax[:, 0:1]),
               reads=[hmax], writes=[colscr], sem="dcs2", dma=True)
        b.emit("y", lambda s: s.dma_start(out=row[0:1, :], in_=colscr.t[:]),
               reads=[colscr], writes=[row], sem="drow2", dma=True)
        b.emit("v", lambda v: v.reduce_max(one1[0:1, 0:1], row[0:1, :], axis=AX),
               reads=[row], writes=[one1])
        b.emit("g", lambda g: g.dma_start(out=lm2_d.t[0:1, 0:1], in_=one1[0:1, 0:1]),
               reads=[one1], writes=[lm2_d], sem="dlm2", dma=True)

        b.final_wait("g", [b.tok("dacc0"), b.tok("dacc1"), b.tok("dho0"),
                           b.tok("dho1"), b.tok("dlm2")])
        block = ctx.enter_context(nc.Block())
        b.build(block)
    return nc


def build_l3():
    nc = bass.Bass("TRN2", target_bir_lowering=False)
    ctx = contextlib.ExitStack()
    with ctx:
        b = Bld(nc, ctx)
        D = lambda n, sh, dt, k="ExternalInput": Buf(nc.dram_tensor(n, sh, dt, kind=k), n)
        h_d = D("h", [I, T], F16)
        acc_d = D("acc", [T, C], F32)
        wdq_dram = D("wdq", [C, I], BF16)
        lm2_d = D("lm2", [1, 8], F32)
        clipd_d = D("clipd", [1], F32)
        out_d = D("out", [T, C], F32, "ExternalOutput")
        scaldr = b.dram("scaldr", [1, 2], F32)

        qh = [b.sbuf(f"qh{j}", [128, T], BF16) for j in range(NIC)]
        wdT = [b.sbuf(f"wdT{j}", [128, C], BF16) for j in range(NIC)]
        hst = [b.sbuf(f"hld{i}", [128, T], F16) for i in range(2)]
        accr = [b.sbuf(f"accr{i}", [128, C], F32) for i in range(2)]
        ost = [b.sbuf(f"ost{i}", [128, C], F32) for i in range(2)]
        t32 = [b.sbuf(f"t32_{i}", [128, T], F32) for i in range(2)]
        s2_col = b.sbuf("s2_col", [128, 1], F32)
        is2_col = b.sbuf("is2_col", [128, 1], F32)
        one1 = b.sbuf("one1", [1, 1], F32)
        one2 = b.sbuf("one2", [1, 1], F32)
        lmrow = b.sbuf("lmrow", [1, 8], F32)
        sc2 = b.sbuf("sc2", [1, 2], F32)
        magic_col = b.sbuf("magic_col", [128, 1], F32)
        ps = [b.psum(f"ps{i}", [128, C], F32) for i in range(4)]
        b.emit("g", lambda g: g.memset(magic_col[:, :], MAGIC), writes=[magic_col])

        b.emit("g", lambda g: g.dma_start(out=lmrow[0:1, :], in_=lm2_d.t[0:1, :]),
               reads=[], writes=[lmrow], sem="dlm2", dma=True)
        b.emit("g", lambda g: g.dma_start(out=one1[0:1, 0:1], in_=clipd_d.t[0:1]),
               reads=[], writes=[one1], sem="dcd", dma=True)
        b.emit("v", lambda v: v.reduce_max(one2[0:1, 0:1], lmrow[0:1, :], axis=AX),
               reads=[lmrow], writes=[one2])
        b.emit("s", lambda s: s.activation(one1[0:1, 0:1], one1[0:1, 0:1], AF.Sigmoid),
               reads=[one1], writes=[one1])
        b.emit("v", lambda v: v.tensor_mul(sc2[0:1, 0:1], one1[0:1, 0:1],
                                           one2[0:1, 0:1]),
               reads=[one1, one2], writes=[sc2])
        b.emit("v", lambda v: v.tensor_scalar_mul(sc2[0:1, 0:1], sc2[0:1, 0:1],
                                                  1.0 / QMAX),
               reads=[sc2], writes=[sc2])
        b.emit("v", lambda v: v.reciprocal(sc2[0:1, 1:2], sc2[0:1, 0:1]),
               reads=[sc2], writes=[sc2])
        b.emit("y", lambda s: s.dma_start(out=scaldr.t[0:1, :], in_=sc2[0:1, :]),
               reads=[sc2], writes=[scaldr], sem="dscw", dma=True)
        b.emit("y", lambda s: s.dma_start(
            out=s2_col[:, 0:1], in_=bass.AP(scaldr.t, 0, [[0, 128], [1, 1]])),
            reads=[scaldr], writes=[s2_col], sem="ds2", dma=True)
        b.emit("y", lambda s: s.dma_start(
            out=is2_col[:, 0:1], in_=bass.AP(scaldr.t, 1, [[0, 128], [1, 1]])),
            reads=[scaldr], writes=[is2_col], sem="dis2", dma=True)

        for j in range(NIC):
            b.emit("y", lambda s, j=j: s.dma_start_transpose(
                wdT[j][:, :], wdq_dram.t[0:C, j * 128:(j + 1) * 128]),
                reads=[wdq_dram], writes=[wdT[j]], sem=f"dwdT{j % 2}", dma=True)
        wtk = [b.tok("dwdT0"), b.tok("dwdT1")]
        for j in range(NIC):
            wdT[j].w = wtk[0]
            wdT[j].extra = [wtk[1]]

        for j in range(NIC):
            hl = hst[j % 2]
            tl = t32[j % 2]
            b.emit("s", lambda s, j=j, hl=hl: s.dma_start(
                out=hl[:, :], in_=h_d.t[j * 128:(j + 1) * 128, :]),
                reads=[], writes=[hl], sem=f"dh{j % 2}", dma=True)
            b.emit("s", lambda s, hl=hl, tl=tl: s.activation(
                tl[:, :], hl[:, :], AF.Identity, bias=magic_col[:, 0:1],
                scale=is2_col[:, 0:1]),
                reads=[hl, is2_col, magic_col], writes=[tl])
            b.emit("v", lambda v, tl=tl: v.tensor_scalar(
                tl[:, :], tl[:, :], MAGIC, -(QMAX + 1.0), ALU.subtract, ALU.max),
                reads=[tl], writes=[tl])
            b.emit("v", lambda v, j=j, tl=tl: v.tensor_scalar_min(
                qh[j][:, :], tl[:, :], QMAX),
                reads=[tl], writes=[qh[j]])

        for k in range(NTT):
            pD = ps[k % 4]
            ac = accr[k % 2]
            os_ = ost[k % 2]
            b.emit("g", lambda g, k=k, ac=ac: g.dma_start(
                out=ac[:, :], in_=acc_d.t[k * 128:(k + 1) * 128, :]),
                reads=[], writes=[ac], sem=f"dac{k % 2}", dma=True)

            for j in range(NIC):
                def dmmj(tt, pD=pD, k=k, j=j):
                    last = None
                    for q in range(2):
                        last = tt.matmul(
                            pD[:, q * 512:(q + 1) * 512],
                            qh[j][:, k * 128:(k + 1) * 128],
                            wdT[j][:, q * 512:(q + 1) * 512],
                            start=(j == 0), stop=(j == NIC - 1))
                    return last
                b.emit("t", dmmj, reads=[qh[j], wdT[j]], writes=[pD])
            b.emit("v", lambda v, pD=pD, ac=ac, os_=os_: v.scalar_tensor_tensor(
                os_[:, :], pD[:, 0:C], s2_col[:, 0:1], ac[:, :],
                ALU.mult, ALU.add),
                reads=[pD, ac, s2_col], writes=[os_])
            b.emit("y", lambda s, k=k, os_=os_: s.dma_start(
                out=out_d.t[k * 128:(k + 1) * 128, :], in_=os_[:, :]),
                reads=[os_], writes=[], sem=f"dout{k % 2}", dma=True)

        b.final_wait("g", [b.tok("dout0"), b.tok("dout1")])
        block = ctx.enter_context(nc.Block())
        b.build(block)
    return nc


def kernel(L0_attn_x, L0_attn_Wr, L0_attn_br, L0_attn_Wk, L0_attn_bk,
           L0_attn_Wv, L0_attn_bv, L0_attn_Wo, L0_attn_bo, L0_attn_time_decay,
           L0_attn_U, L0_ffn_clip_x, L0_ffn_clip_down, L0_ffn_w_gate,
           L0_ffn_w_up, L0_ffn_w_down):
    n = 8
    x = np.ascontiguousarray(np.asarray(L0_attn_x, dtype=np.float32))
    xs = [np.ascontiguousarray(x[i]) for i in range(n)]
    f32 = lambda a: np.ascontiguousarray(np.asarray(a, dtype=np.float32))
    Wr, Wk, Wv, Wo = f32(L0_attn_Wr), f32(L0_attn_Wk), f32(L0_attn_Wv), f32(L0_attn_Wo)
    br, bk, bv, bo = f32(L0_attn_br), f32(L0_attn_bk), f32(L0_attn_bv), f32(L0_attn_bo)
    td, Uv = f32(L0_attn_time_decay), f32(L0_attn_U)
    clipx, clipd = f32(L0_ffn_clip_x), f32(L0_ffn_clip_down)
    wg, wu, wd = f32(L0_ffn_w_gate), f32(L0_ffn_w_up), f32(L0_ffn_w_down)

    cores = list(range(n))
    gstart = [0, 384, 768, 1152, 1536, 1920, 2304, 2560]
    grows = [384] * 6 + [256] * 2

    def wslice(w, i):
        blk = w[gstart[i]:gstart[i] + grows[i]]
        if grows[i] < 384:
            blk = np.concatenate([blk, blk[:384 - grows[i]]], axis=0)
        return np.ascontiguousarray(blk)

    nc1 = build_l1()
    in1 = [{"x": xs[i], "wg": wslice(wg, i), "wu": wslice(wu, i),
            "wd": np.ascontiguousarray(wd[i * 128:(i + 1) * 128]),
            "Wr": np.ascontiguousarray(Wr[i * 128:(i + 1) * 128]),
            "Wk": np.ascontiguousarray(Wk[i * 128:(i + 1) * 128]),
            "Wv": np.ascontiguousarray(Wv[i * 128:(i + 1) * 128]),
            "Wo": np.ascontiguousarray(Wo[i * 128:(i + 1) * 128])}
           for i in range(n)]
    r1 = run_bass_kernel_spmd(nc1, in1, core_ids=cores)
    lm1 = np.concatenate(
        [np.asarray(r1.results[i]["lm1"]).reshape(1, 1) for i in range(n)], axis=1)
    xnf = [np.asarray(r1.results[i]["xnf"]) for i in range(n)]
    xnb = [np.asarray(r1.results[i]["xnb"]) for i in range(n)]
    wgq = np.concatenate(
        [np.asarray(r1.results[i]["wgq"])[:grows[i]] for i in range(n)], axis=0)
    wuq = np.concatenate(
        [np.asarray(r1.results[i]["wuq"])[:grows[i]] for i in range(n)], axis=0)
    wdq = np.concatenate(
        [np.asarray(r1.results[i]["wdq"]) for i in range(n)], axis=0)
    wbf = {p: np.concatenate(
        [np.asarray(r1.results[i][f"wbf{p}"]) for i in range(n)], axis=0)
        for p in "rkvo"}

    nc2 = build_l2()
    in2 = [{"x": xs[i], "xnf": xnf[i], "xnb": xnb[i], "lm1": lm1,
            "wbfr": wbf["r"], "wbfk": wbf["k"], "wbfv": wbf["v"],
            "wbfo": wbf["o"],
            "br": br, "bk": bk, "bv": bv, "bo": bo,
            "td": td, "Uv": Uv, "clipx": clipx, "wgq": wgq, "wuq": wuq}
           for i in range(n)]
    r2 = run_bass_kernel_spmd(nc2, in2, core_ids=cores)
    lm2 = np.concatenate(
        [np.asarray(r2.results[i]["lm2"]).reshape(1, 1) for i in range(n)], axis=1)
    acc = [np.asarray(r2.results[i]["acc"]) for i in range(n)]
    hh = [np.asarray(r2.results[i]["h"]) for i in range(n)]

    nc3 = build_l3()
    in3 = [{"h": hh[i], "acc": acc[i], "wdq": wdq, "lm2": lm2, "clipd": clipd}
           for i in range(n)]
    r3 = run_bass_kernel_spmd(nc3, in3, core_ids=cores)
    out = np.stack([np.asarray(r3.results[i]["out"]) for i in range(n)], axis=0)
    return out.astype(np.float32)


# version 15
# speedup vs baseline: 1.0704x; 1.0704x over previous
import math
import contextlib
import numpy as np
import ml_dtypes
import concourse.bass as bass
import concourse.mybir as mybir
from concourse.bass_utils import run_bass_kernel_spmd

F32 = mybir.dt.float32
BF16 = mybir.dt.bfloat16
F16 = mybir.dt.float16
U8 = mybir.dt.uint8
AF = mybir.ActivationFunctionType
ALU = mybir.AluOpType
AX = mybir.AxisListType.X

MAGIC = 12582912.0
T, C, I = 2048, 1024, 2816
NCH = C // 128
NIC = I // 128
NTT = T // 128
H = T // 2
_EPS = math.exp(-1.0 / 12.0)
EPS_R = _EPS / 2.0
EPS_K = _EPS + 0.3 / 11.0
EPS_V = _EPS
QMAX = 127.0


_ENG_ATTR = {"v": "vector", "s": "scalar", "t": "tensor", "g": "gpsimd", "y": "sync"}


class Buf:
    __slots__ = ("t", "w", "rs", "extra", "name")

    def __init__(self, t, name=""):
        self.t = t
        self.w = None
        self.rs = []
        self.extra = []
        self.name = name

    def __getitem__(self, idx):
        return self.t[idx]


class Bld:
    def __init__(self, nc, ctx):
        self.nc = nc
        self.ctx = ctx
        self.prog = []
        self.sems = {}
        self.waited = {}
        self.dma_sems = set()

    def sbuf(self, name, shape, dtype):
        return Buf(self.ctx.enter_context(self.nc.sbuf_tensor(name, shape, dtype)), name)

    def psum(self, name, shape, dtype):
        return Buf(self.ctx.enter_context(self.nc.psum_tensor(name, shape, dtype)), name)

    def dram(self, name, shape, dtype, kind="Internal"):
        return Buf(self.nc.dram_tensor(name, shape, dtype, kind=kind), name)

    def region(self, name, nbytes):
        t = self.ctx.enter_context(self.nc.sbuf_tensor(name, [128, nbytes], U8))
        return self.nc.lookup_mloc(t).addr

    def at(self, addr, rel, name, shape, dtype):
        t = self.nc.alloc_sbuf_tensor_at(name, shape, dtype, offset=addr + rel)
        return Buf(t, name)

    def handoff(self, new_bufs, old_bufs):
        toks = []
        for b in old_bufs:
            if b.w:
                toks.append(b.w)
            toks.extend(b.rs)
            toks.extend(b.extra)
        for nb in new_bufs:
            nb.extra = list(toks)

    def _sem(self, key):
        if key not in self.sems:
            h = self.ctx.enter_context(self.nc.semaphore(key[:30]))
            self.sems[key] = [h, 0]
        return self.sems[key]

    def emit(self, eng, fn, reads=(), writes=(), sem=None, dma=False, n_dma=1,
             extra_waits=()):
        if dma:
            assert sem is not None
        sem_key = sem or ("c" + eng)
        s = self._sem(sem_key)
        amount = 16 * n_dma if dma else 1
        toks = list(extra_waits)
        for b in reads:
            if b.w:
                toks.append(b.w)
            toks.extend(b.extra)
        for b in writes:
            if b.w:
                toks.append(b.w)
            toks.extend(b.rs)
            toks.extend(b.extra)
        if dma:
            toks.append((sem_key, s[1]))
            self.dma_sems.add(sem_key)
        wd = self.waited.setdefault(eng, {})
        final = {}
        for (sk, v) in toks:
            if v > wd.get(sk, 0) and v > final.get(sk, 0):
                final[sk] = v
        s[1] += amount
        tok = (sem_key, s[1])
        waits = sorted(final.items())

        def closure(e, waits=waits, semh=s[0], amt=amount, fn=fn):
            for sk, v in waits:
                e.wait_ge(self.sems[sk][0], v)
            fn(e).then_inc(semh, amt)

        self.prog.append((eng, closure))
        for sk, v in final.items():
            wd[sk] = v
        for b in reads:
            b.rs.append(tok)
        for b in writes:
            b.w = tok
            b.rs = []
            b.extra = []
        return tok

    def tok(self, sem_key):
        return (sem_key, self._sem(sem_key)[1])

    def settle(self, bufs, tok):
        for b in bufs:
            b.w = tok
            b.rs = []
            b.extra = []

    def final_wait(self, eng, toks=None):
        toks = [self.tok(k) for k in sorted(self.dma_sems)]
        def closure(e, toks=toks):
            for sk, v in toks:
                e.wait_ge(self.sems[sk][0], v)
        self.prog.append((eng, closure))

    def build(self, block):
        per = {k: [] for k in _ENG_ATTR}
        for eng, closure in self.prog:
            per[eng].append(closure)
        for key, closures in per.items():
            if not closures:
                continue

            def body(e, closures=closures):
                for c in closures:
                    c(e)

            getattr(block, _ENG_ATTR[key])(body)


def build_l1():
    nc = bass.Bass("TRN2", target_bir_lowering=False)
    ctx = contextlib.ExitStack()
    with ctx:
        b = Bld(nc, ctx)
        x_d = Buf(nc.dram_tensor("x", [T, C], F32, kind="ExternalInput"), "x")
        wg_d = Buf(nc.dram_tensor("wg", [384, C], F32, kind="ExternalInput"), "wg")
        wu_d = Buf(nc.dram_tensor("wu", [384, C], F32, kind="ExternalInput"), "wu")
        wd_d = Buf(nc.dram_tensor("wd", [128, I], F32, kind="ExternalInput"), "wd")
        xnf_d = Buf(nc.dram_tensor("xnf", [T, C], F32, kind="ExternalOutput"), "xnf")
        xnb_d = Buf(nc.dram_tensor("xnb", [T, C], BF16, kind="ExternalOutput"), "xnb")
        lm1_d = Buf(nc.dram_tensor("lm1", [1, 1], F32, kind="ExternalOutput"), "lm1")
        wgq_d = Buf(nc.dram_tensor("wgq", [384, C], BF16, kind="ExternalOutput"), "wgq")
        wuq_d = Buf(nc.dram_tensor("wuq", [384, C], BF16, kind="ExternalOutput"), "wuq")
        wdq_d = Buf(nc.dram_tensor("wdq", [128, I], BF16, kind="ExternalOutput"), "wdq")
        aw_in = {p: Buf(nc.dram_tensor(f"W{p}", [128, C], F32, kind="ExternalInput"),
                        f"W{p}") for p in "rkvo"}
        aw_out = {p: Buf(nc.dram_tensor(f"wbf{p}", [128, C], BF16,
                                        kind="ExternalOutput"), f"wbf{p}")
                  for p in "rkvo"}
        colscr = b.dram("colscr", [128], F32)

        xt = [b.sbuf(f"xt{i}", [128, C], F32) for i in range(2)]
        xnf = [b.sbuf(f"xnf{i}", [128, C], F32) for i in range(2)]
        xnb = [b.sbuf(f"xnb{i}", [128, C], BF16) for i in range(2)]
        sq = [b.sbuf(f"sq{i}", [128, C], F32) for i in range(2)]
        ss = [b.sbuf(f"ss{i}", [128, 1], F32) for i in range(2)]
        sm = [b.sbuf(f"sm{i}", [128, 1], F32) for i in range(2)]
        mcol = [b.sbuf(f"mcol{i}", [128, 1], F32) for i in range(2)]
        vcol = [b.sbuf(f"vcol{i}", [128, 1], F32) for i in range(2)]
        sdcol = [b.sbuf(f"sdcol{i}", [128, 1], F32) for i in range(2)]
        rstd = [b.sbuf(f"rstd{i}", [128, 1], F32) for i in range(2)]
        nmr = [b.sbuf(f"nmr{i}", [128, 1], F32) for i in range(2)]
        mxc = b.sbuf("mxc", [128, 1], F32)
        mxacc = b.sbuf("mxacc", [128, 1], F32)
        row = b.sbuf("rowb", [1, 128], F32)
        one1 = b.sbuf("one1", [1, 1], F32)
        magic_col = b.sbuf("magic_col", [128, 1], F32)
        NPAR = 4
        gwst = [b.sbuf(f"gwst{i}", [128, I], F32) for i in range(NPAR)]
        gqo = [b.sbuf(f"gqo{i}", [128, I], BF16) for i in range(NPAR)]
        rmaxc = [b.sbuf(f"rmaxc{i}", [128, 1], F32) for i in range(NPAR)]
        rscl = [b.sbuf(f"rscl{i}", [128, 1], F32) for i in range(NPAR)]
        rinv = [b.sbuf(f"rinv{i}", [128, 1], F32) for i in range(NPAR)]

        b.emit("g", lambda g: g.memset(mxacc[:, :], 0.0), writes=[mxacc])
        b.emit("g", lambda g: g.memset(magic_col[:, :], MAGIC), writes=[magic_col])
        awst = [b.sbuf(f"awst{i}", [128, C], BF16) for i in range(2)]
        for pi, p in enumerate("rkvo"):
            st = awst[pi % 2]
            b.emit("g", lambda g, p=p, st=st: g.dma_start(
                out=st[:, :], in_=aw_in[p].t[0:128, :]),
                reads=[], writes=[st], sem=f"dawi{pi % 2}", dma=True)
            b.emit("y", lambda s, p=p, st=st: s.dma_start(
                out=aw_out[p].t[0:128, :], in_=st[:, :]),
                reads=[st], writes=[], sem=f"dawo{pi % 2}", dma=True)

        _unit_ctr = [0]

        def make_wq_unit(wsrc, wdst, c, ncols, sem_pref):
            def unit():
                par = _unit_ctr[0] % NPAR
                _unit_ctr[0] += 1
                st = gwst[par]
                qo = gqo[par]
                rm, rs, ri = rmaxc[par], rscl[par], rinv[par]
                b.emit("g", lambda g: g.dma_start(
                    out=st[:, 0:ncols], in_=wsrc.t[c * 128:(c + 1) * 128, :]),
                    reads=[], writes=[st], sem=f"dgw{par}", dma=True)
                b.emit("v", lambda v: v.reduce_max(
                    rm[:, :], st[:, 0:ncols], axis=AX, apply_absolute_value=True),
                    reads=[st], writes=[rm])
                b.emit("v", lambda v: v.tensor_scalar_mul(
                    rs[:, :], rm[:, :], 1.0 / QMAX),
                    reads=[rm], writes=[rs])
                b.emit("v", lambda v: v.reciprocal(ri[:, :], rs[:, :]),
                       reads=[rs], writes=[ri])
                b.emit("s", lambda s: s.activation(
                    st[:, 0:ncols], st[:, 0:ncols], AF.Identity,
                    bias=magic_col[:, 0:1], scale=ri[:, 0:1]),
                    reads=[st, ri, magic_col], writes=[st])
                b.emit("v", lambda v: v.tensor_scalar(
                    st[:, 0:ncols], st[:, 0:ncols], MAGIC, -(QMAX + 1.0),
                    ALU.subtract, ALU.max),
                    reads=[st], writes=[st])
                b.emit("v", lambda v: v.tensor_scalar(
                    qo[:, 0:ncols], st[:, 0:ncols], QMAX, rs[:, 0:1],
                    ALU.min, ALU.mult),
                    reads=[st, rs], writes=[qo])
                b.emit("y", lambda s: s.dma_start(
                    out=wdst.t[c * 128:(c + 1) * 128, :], in_=qo[:, 0:ncols]),
                    reads=[qo], writes=[], sem=sem_pref + str(par % 2), dma=True)
            return unit

        wq_units = []
        for c in range(3):
            wq_units.append(make_wq_unit(wg_d, wgq_d, c, C, "dwgq"))
            wq_units.append(make_wq_unit(wu_d, wuq_d, c, C, "dwuq"))
        wq_units.append(make_wq_unit(wd_d, wdq_d, 0, I, "dwdq"))
        wq_iter = iter(wq_units)
        for k in range(NTT):
            pp = k % 2
            x = xt[pp]
            xf = xnf[pp]
            xb = xnb[pp]
            sq_, ss_, sm_ = sq[pp], ss[pp], sm[pp]
            mc, vc, sd_, rs_, nm_ = mcol[pp], vcol[pp], sdcol[pp], rstd[pp], nmr[pp]
            b.emit("g", lambda g, k=k, x=x: g.dma_start(
                out=x[:, :], in_=x_d.t[k * 128:(k + 1) * 128, :]),
                reads=[], writes=[x], sem=f"dx{k % 2}", dma=True)
            b.emit("s", lambda s, x=x, sq_=sq_, ss_=ss_: s.activation(
                sq_[:, :], x[:, :], AF.Square, accum_out=ss_[:, 0:1]),
                reads=[x], writes=[sq_, ss_])
            b.emit("v", lambda v, x=x, sm_=sm_: v.reduce_sum(
                sm_[:, :], x[:, :], axis=AX),
                reads=[x], writes=[sm_])
            b.emit("v", lambda v, sm_=sm_, mc=mc: v.tensor_scalar_mul(
                mc[:, :], sm_[:, :], 1.0 / C),
                reads=[sm_], writes=[mc])
            b.emit("v", lambda v, ss_=ss_, vc=vc: v.tensor_scalar_mul(
                vc[:, :], ss_[:, :], 1.0 / C),
                reads=[ss_], writes=[vc])
            b.emit("v", lambda v, mc=mc, sq_=sq_: v.scalar_tensor_tensor(
                sq_[:, 0:1], mc[:, :], -1.0, mc[:, :], ALU.mult, ALU.mult),
                reads=[mc], writes=[sq_])
            b.emit("v", lambda v, vc=vc, sq_=sq_: v.tensor_add(
                vc[:, :], vc[:, :], sq_[:, 0:1]),
                reads=[sq_], writes=[vc])
            b.emit("v", lambda v, vc=vc: v.tensor_scalar_add(vc[:, :], vc[:, :], 1e-5),
                   reads=[vc], writes=[vc])
            b.emit("s", lambda s, vc=vc, sd_=sd_: s.activation(
                sd_[:, :], vc[:, :], AF.Sqrt),
                reads=[vc], writes=[sd_])
            b.emit("v", lambda v, sd_=sd_, rs_=rs_: v.reciprocal(
                rs_[:, :], sd_[:, :]),
                reads=[sd_], writes=[rs_])
            b.emit("v", lambda v, mc=mc, rs_=rs_, nm_=nm_: v.scalar_tensor_tensor(
                nm_[:, :], mc[:, :], -1.0, rs_[:, :], ALU.mult, ALU.mult),
                reads=[mc, rs_], writes=[nm_])
            b.emit("v", lambda v, x=x, xf=xf, rs_=rs_, nm_=nm_: v.tensor_scalar(
                xf[:, :], x[:, :], rs_[:, 0:1], nm_[:, 0:1], ALU.mult, ALU.add),
                reads=[x, rs_, nm_], writes=[xf])
            b.emit("v", lambda v, xf=xf: v.reduce_max(
                mxc[:, :], xf[:, :], axis=AX, apply_absolute_value=True),
                reads=[xf], writes=[mxc])
            b.emit("v", lambda v: v.tensor_max(mxacc[:, :], mxacc[:, :], mxc[:, :]),
                   reads=[mxc, mxacc], writes=[mxacc])
            b.emit("s", lambda s, xf=xf, xb=xb: s.activation(
                xb[:, :], xf[:, :], AF.Copy),
                reads=[xf], writes=[xb])
            b.emit("y", lambda s, k=k, xf=xf: s.dma_start(
                out=xnf_d.t[k * 128:(k + 1) * 128, :], in_=xf[:, :]),
                reads=[xf], writes=[], sem=f"doxf{k % 2}", dma=True)
            b.emit("y", lambda s, k=k, xb=xb: s.dma_start(
                out=xnb_d.t[k * 128:(k + 1) * 128, :], in_=xb[:, :]),
                reads=[xb], writes=[], sem=f"doxb{k % 2}", dma=True)
            u = next(wq_iter, None)
            if u is not None:
                u()
        for u in wq_iter:
            u()
        b.emit("y", lambda s: s.dma_start(out=colscr.t[:], in_=mxacc[:, 0:1]),
               reads=[mxacc], writes=[colscr], sem="dcs", dma=True)
        b.emit("y", lambda s: s.dma_start(out=row[0:1, :], in_=colscr.t[:]),
               reads=[colscr], writes=[row], sem="drow", dma=True)
        b.emit("v", lambda v: v.reduce_max(one1[0:1, 0:1], row[0:1, :], axis=AX),
               reads=[row], writes=[one1])
        b.emit("g", lambda g: g.dma_start(out=lm1_d.t[0:1, 0:1], in_=one1[0:1, 0:1]),
               reads=[one1], writes=[lm1_d], sem="dlm", dma=True)
        b.final_wait("g", [b.tok("doxf0"), b.tok("doxf1"), b.tok("doxb0"),
                           b.tok("doxb1"), b.tok("dlm"),
                           b.tok("dwgq0"), b.tok("dwgq1"), b.tok("dwuq0"),
                           b.tok("dwuq1"), b.tok("dwdq0"), b.tok("dwdq1")])
        block = ctx.enter_context(nc.Block())
        b.build(block)
    return nc


def build_l2():
    nc = bass.Bass("TRN2", target_bir_lowering=False)
    ctx = contextlib.ExitStack()
    with ctx:
        b = Bld(nc, ctx)
        D = lambda n, sh, dt, k="ExternalInput": Buf(nc.dram_tensor(n, sh, dt, kind=k), n)
        x_d = D("x", [T, C], F32)
        xnf_d = D("xnf", [T, C], F32)
        xnb_d = D("xnb", [T, C], BF16)
        lm1_d = D("lm1", [1, 8], F32)
        wbf = {p: D(f"wbf{p}", [C, C], BF16) for p in "rkvo"}
        br_d, bk_d, bv_d, bo_d = (D(n, [C], F32) for n in ("br", "bk", "bv", "bo"))
        td_d, U_d = D("td", [C], F32), D("Uv", [C], F32)
        clipx_d = D("clipx", [1], F32)
        wgq_dram, wuq_dram = D("wgq", [I, C], BF16), D("wuq", [I, C], BF16)
        acc_d = D("acc", [T, C], F32, "ExternalOutput")
        h_d = D("h", [I, T], F16, "ExternalOutput")
        lm2_d = D("lm2", [1, 1], F32, "ExternalOutput")
        qdram = b.dram("qdram", [T, C], BF16)
        colscr = b.dram("colscr", [128], F32)
        scaldr = b.dram("scaldr", [1, 2], F32)

        xnF = [b.sbuf(f"xnF{c}", [128, T], BF16) for c in range(NCH)]
        OWF = [b.sbuf(f"OWF{c}", [128, T], BF16) for c in range(NCH)]
        ramp = b.sbuf("ramp", [128, T], F32)
        bo_bc = b.sbuf("bo_bc", [128, C], F32)

        cols = {n: b.sbuf(f"col_{n}", [128, NCH], F32)
                for n in ("td", "U", "bk", "br", "bv", "w", "ubk")}
        hmax = b.sbuf("hmaxc", [128, 1], F32)
        hmx1 = b.sbuf("hmx1", [128, 1], F32)
        sx_col = b.sbuf("sx_col", [128, 1], F32)
        isx_col = b.sbuf("isx_col", [128, 1], F32)
        magic_col = b.sbuf("magic_col", [128, 1], F32)
        ecols = {}
        for nm, val in (("k", EPS_K), ("v", EPS_V), ("r", EPS_R)):
            ecols[nm] = b.sbuf(f"eps_{nm}", [128, 1], F32)
        aprevA = b.sbuf("aprevA", [128, 1], F32)
        aprevB = b.sbuf("aprevB", [128, 1], F32)
        row = b.sbuf("rowb", [1, 128], F32)
        one1 = b.sbuf("one1", [1, 1], F32)
        one2 = b.sbuf("one2", [1, 1], F32)
        lmrow = b.sbuf("lmrow", [1, 8], F32)
        sc2 = b.sbuf("sc2", [1, 2], F32)

        R1 = b.region("ov1", 24 * 2048)
        WT = {p: [b.at(R1, (p_i * 8 + c) * 2048, f"W{p}T{c}", [128, C], BF16)
                  for c in range(NCH)]
              for p_i, p in enumerate("kvr")}
        NS = 16
        R2 = b.region("ov2", NS * 4096)
        S = [b.at(R2, i * 4096, f"scr{i}", [128, H], F32) for i in range(NS)]

        ps = [b.psum(f"ps{i}", [128, T], F32) for i in range(2)]

        GD, SY = "g", "y"

        b.emit("g", lambda g: g.memset(magic_col[:, :], MAGIC), writes=[magic_col])
        for nm, val in (("k", EPS_K), ("v", EPS_V), ("r", EPS_R)):
            b.emit("g", lambda g, nm=nm, val=val: g.memset(ecols[nm][:, :], val),
                   writes=[ecols[nm]])
        b.emit("g", lambda g: g.iota(ramp[:, :], [[1, T]], base=-(T - 1),
                                     channel_multiplier=0,
                                     allow_small_or_imprecise_dtypes=True),
               writes=[ramp])
        b.emit("y", lambda s: s.dma_start(
            out=bo_bc[:, :], in_=bass.AP(bo_d.t, 0, [[0, 128], [1, C]])),
            reads=[], writes=[bo_bc], sem="dbo", dma=True)
        for n, d in (("td", td_d), ("U", U_d), ("bk", bk_d), ("br", br_d),
                     ("bv", bv_d)):
            b.emit("g", lambda g, n=n, d=d: g.dma_start(
                out=cols[n][:, :], in_=d.t.ap().rearrange("(f p) -> p f", p=128),
                allow_slow_non_contiguous=True),
                reads=[], writes=[cols[n]], sem=f"dc_{n}", dma=True)
        b.emit("s", lambda s: s.activation(cols["w"][:, :], cols["td"][:, :], AF.Relu),
               reads=[cols["td"]], writes=[cols["w"]])
        b.emit("v", lambda v: v.tensor_add(cols["ubk"][:, :], cols["U"][:, :],
                                           cols["bk"][:, :]),
               reads=[cols["U"], cols["bk"]], writes=[cols["ubk"]])
        b.emit("g", lambda g: g.dma_start(out=lmrow[0:1, :], in_=lm1_d.t[0:1, :]),
               reads=[], writes=[lmrow], sem="dlm1", dma=True)
        b.emit("g", lambda g: g.dma_start(out=one1[0:1, 0:1], in_=clipx_d.t[0:1]),
               reads=[], writes=[one1], sem="dcx", dma=True)
        b.emit("v", lambda v: v.reduce_max(one2[0:1, 0:1], lmrow[0:1, :], axis=AX),
               reads=[lmrow], writes=[one2])
        b.emit("s", lambda s: s.activation(one1[0:1, 0:1], one1[0:1, 0:1], AF.Sigmoid),
               reads=[one1], writes=[one1])
        b.emit("v", lambda v: v.tensor_mul(sc2[0:1, 0:1], one1[0:1, 0:1],
                                           one2[0:1, 0:1]),
               reads=[one1, one2], writes=[sc2])
        b.emit("v", lambda v: v.tensor_scalar_mul(sc2[0:1, 0:1], sc2[0:1, 0:1],
                                                  1.0 / QMAX),
               reads=[sc2], writes=[sc2])
        b.emit("v", lambda v: v.reciprocal(sc2[0:1, 1:2], sc2[0:1, 0:1]),
               reads=[sc2], writes=[sc2])
        b.emit("y", lambda s: s.dma_start(out=scaldr.t[0:1, :], in_=sc2[0:1, :]),
               reads=[sc2], writes=[scaldr], sem="dscw", dma=True)
        b.emit("y", lambda s: s.dma_start(
            out=sx_col[:, 0:1], in_=bass.AP(scaldr.t, 0, [[0, 128], [1, 1]])),
            reads=[scaldr], writes=[sx_col], sem="dsx", dma=True)
        b.emit("y", lambda s: s.dma_start(
            out=isx_col[:, 0:1], in_=bass.AP(scaldr.t, 1, [[0, 128], [1, 1]])),
            reads=[scaldr], writes=[isx_col], sem="disx", dma=True)

        for c in range(NCH):
            b.emit("y", lambda s, c=c: s.dma_start_transpose(
                xnF[c][:, :], xnb_d.t[0:T, c * 128:(c + 1) * 128]),
                reads=[], writes=[xnF[c]], sem=f"dxnF{c % 4}", dma=True)
        xnF_toks = [b.tok(f"dxnF{i}") for i in range(4)]
        for c in range(NCH):
            xnF[c].w = xnF_toks[0]
            xnF[c].extra = xnF_toks[1:]
        wo_store_toks = []
        for p in "kvr":
            for c in range(NCH):
                b.emit("y", lambda s, p=p, c=c: s.dma_start_transpose(
                    WT[p][c][:, :], wbf[p].t[0:C, c * 128:(c + 1) * 128]),
                    reads=[wbf[p]], writes=[WT[p][c]], sem=f"dwt{p}{c % 4}",
                    dma=True)
            tkk = [b.tok(f"dwt{p}{i}") for i in range(4)]
            for c in range(NCH):
                WT[p][c].w = tkk[0]
                WT[p][c].extra = tkk[1:]

        def proj_mm(tt, pbuf, p, j):
            last = None
            for c in range(NCH):
                for q in range(4):
                    last = tt.matmul(
                        pbuf[:, q * 512:(q + 1) * 512],
                        WT[p][c][:, j * 128:(j + 1) * 128],
                        xnF[c][:, q * 512:(q + 1) * 512],
                        start=(c == 0), stop=(c == NCH - 1))
            return last

        XF = [b.at(R2, 32768 + i * 4096, f"xf32_{i}", [128, C], F32)
              for i in range(2)]
        QS = [b.at(R2, 40960 + i * 2048, f"qs{i}", [128, C], BF16)
              for i in range(2)]
        wstore_toks = {}

        def make_xq_unit(k):
            def unit():
                xf = XF[k % 2]
                q = QS[k % 2]
                b.emit("s", lambda s: s.dma_start(
                    out=xf[:, :], in_=xnf_d.t[k * 128:(k + 1) * 128, :]),
                    reads=[], writes=[xf], sem=f"dxf{k % 2}", dma=True)
                b.emit("s", lambda s: s.activation(
                    xf[:, :], xf[:, :], AF.Identity, bias=magic_col[:, 0:1],
                    scale=isx_col[:, 0:1]),
                    reads=[xf, isx_col, magic_col], writes=[xf])
                b.emit("v", lambda v: v.tensor_scalar(
                    xf[:, :], xf[:, :], MAGIC, -(QMAX + 1.0), ALU.subtract, ALU.max),
                    reads=[xf], writes=[xf])
                b.emit("v", lambda v: v.tensor_scalar_min(
                    q[:, :], xf[:, :], QMAX),
                    reads=[xf], writes=[q])
                b.emit("y", lambda s: s.dma_start(
                    out=qdram.t[k * 128:(k + 1) * 128, :], in_=q[:, :]),
                    reads=[q], writes=[], sem=f"dqo{k % 2}", dma=True)
            return unit

        xq_units = [make_xq_unit(k) for k in range(NTT)]

        npsum = [0]

        def next_ps():
            p = ps[npsum[0] % 2]
            npsum[0] += 1
            return p

        SA = [b.sbuf(f"scanA{i}", [128, H + 1], F32) for i in range(2)]
        SB = [b.sbuf(f"scanB{i}", [128, H + 1], F32) for i in range(2)]
        b.emit("g", lambda g: g.memset(SA[0][:, 0:1], 0.0), writes=[SA[0]])
        b.emit("g", lambda g: g.memset(SB[0][:, 0:1], 0.0), writes=[SB[0]])
        KFh = [S[1], S[2]]
        VFh = [S[3], S[4]]
        SGRh = [S[5], S[6]]
        MW = [(S[7], S[8], S[9], S[10]), (S[11], S[12], S[13], S[14])]
        T1a, T1b = S[0], S[15]
        RW = (MW[0][0], MW[1][0])

        def emit_mm(p, j):
            pbuf = next_ps()
            b.emit("t", lambda tt, pbuf=pbuf, p=p, j=j: proj_mm(tt, pbuf, p, j),
                   reads=[WT[p][c] for c in range(NCH)] + xnF, writes=[pbuf])
            return pbuf

        def emit_evac(pbuf, eps, ecol_b, bias_b, j, d0, d1):
            ec = ecol_b[:, 0:1]
            bj = bias_b[:, j:j + 1]
            b.emit("v", lambda v, bj=bj: v.tensor_copy(T1a[:, 0:1], bj),
                   reads=[bias_b], writes=[T1a])
            b.emit("s", lambda s, pbuf=pbuf, ec=ec, bj=bj: s.activation(
                T1a[:, 1:H], pbuf[:, 0:H - 1], AF.Identity, bias=bj, scale=ec),
                reads=[pbuf, bias_b, ecol_b], writes=[T1a])
            b.emit("s", lambda s, pbuf=pbuf, ec=ec, bj=bj: s.activation(
                T1b[:, 0:H], pbuf[:, H - 1:T - 1], AF.Identity, bias=bj, scale=ec),
                reads=[pbuf, bias_b, ecol_b], writes=[T1b])
            b.emit("v", lambda v, pbuf=pbuf, eps=eps, d0=d0: v.scalar_tensor_tensor(
                d0[:, 0:H], pbuf[:, 1:H + 1], 1.0 - eps, T1a[:, 0:H],
                ALU.mult, ALU.add),
                reads=[pbuf, T1a], writes=[d0])
            b.emit("v", lambda v, pbuf=pbuf, eps=eps, d1=d1: v.scalar_tensor_tensor(
                d1[:, 0:H - 1], pbuf[:, H + 1:T], 1.0 - eps, T1b[:, 0:H - 1],
                ALU.mult, ALU.add),
                reads=[pbuf, T1b], writes=[d1])
            b.emit("v", lambda v, d1=d1: v.tensor_copy(
                d1[:, H - 1:H], T1b[:, H - 1:H]),
                reads=[T1b], writes=[d1])

        def emit_rinterp(pR, j):
            emit_evac(pR, EPS_R, ecols["r"], cols["br"], j, RW[0], RW[1])
            for hh in range(2):
                b.emit("s", lambda s, hh=hh: s.activation(
                    SGRh[hh][:, :], RW[hh][:, :], AF.Sigmoid),
                    reads=[RW[hh]], writes=[SGRh[hh]])

        def pre_k(j):
            p = emit_mm("k", j)
            emit_evac(p, EPS_K, ecols["k"], cols["bk"], j, KFh[0], KFh[1])

        def pre_v(j):
            p = emit_mm("v", j)
            emit_evac(p, EPS_V, ecols["v"], cols["bv"], j, VFh[0], VFh[1])

        def pre_r(j):
            p = emit_mm("r", j)
            emit_rinterp(p, j)

        def emit_middle(j, ins1=None, ins2=None, ins3=None):
            wj = cols["w"][:, j:j + 1]
            ubkj = cols["ubk"][:, j:j + 1]
            HV = [dict(W=MW[hh], kf=KFh[hh], vf=VFh[hh], sgr=SGRh[hh],
                       sa=SA[hh], sb=SB[hh], lo=hh * H) for hh in range(2)]

            def A1(hh):
                W4 = HV[hh]["W"][3]
                kf = HV[hh]["kf"]
                b.emit("s", lambda s, W4=W4, kf=kf, ubkj=ubkj: s.activation(
                    W4[:, :], kf[:, :], AF.Identity, bias=ubkj),
                    reads=[kf, cols["ubk"]], writes=[W4])

            def D1(hh):
                W1 = HV[hh]["W"][0]
                kf = HV[hh]["kf"]
                lo = HV[hh]["lo"]
                b.emit("v", lambda v, W1=W1, kf=kf, lo=lo, wj=wj:
                       v.scalar_tensor_tensor(W1[:, :], ramp[:, lo:lo + H], wj,
                                              kf[:, :], ALU.mult, ALU.add),
                       reads=[ramp, cols["w"], kf], writes=[W1])

            def DCLIP(buf):
                b.emit("v", lambda v, buf=buf: v.tensor_scalar(
                    buf[:, :], buf[:, :], -20.0, 10.0, ALU.max, ALU.min),
                    reads=[buf], writes=[buf])

            def A2(hh):
                W1, W2 = HV[hh]["W"][0], HV[hh]["W"][1]
                b.emit("s", lambda s, W1=W1, W2=W2: s.activation(
                    W2[:, :], W1[:, :], AF.Exp), reads=[W1], writes=[W2])

            def A3(hh):
                W4 = HV[hh]["W"][3]
                b.emit("s", lambda s, W4=W4: s.activation(
                    W4[:, :], W4[:, :], AF.Exp), reads=[W4], writes=[W4])

            def D4(hh):
                W2, W3 = HV[hh]["W"][1], HV[hh]["W"][2]
                vf = HV[hh]["vf"]
                b.emit("v", lambda v, W2=W2, W3=W3, vf=vf: v.tensor_mul(
                    W3[:, :], W2[:, :], vf[:, :]), reads=[W2, vf], writes=[W3])

            def DSCAN(hh, which):
                src = HV[hh]["W"][2] if which == 0 else HV[hh]["W"][1]
                dst = HV[hh]["sa"] if which == 0 else HV[hh]["sb"]
                if hh == 0:
                    b.emit("v", lambda v, src=src, dst=dst: v.tensor_tensor_scan(
                        dst[:, 1:H + 1], src[:, :], src[:, :], 0.0,
                        ALU.add, ALU.bypass),
                        reads=[src], writes=[dst])
                else:
                    prev = SA[0] if which == 0 else SB[0]
                    b.emit("v", lambda v, src=src, dst=dst, prev=prev:
                           v.tensor_tensor_scan(dst[:, 1:H + 1], src[:, :],
                                                src[:, :], prev[:, H:H + 1],
                                                ALU.add, ALU.bypass),
                           reads=[src, prev], writes=[dst])

            def DCARRY(which):
                dst = SA[1] if which == 0 else SB[1]
                prev = SA[0] if which == 0 else SB[0]
                b.emit("v", lambda v, dst=dst, prev=prev: v.tensor_copy(
                    dst[:, 0:1], prev[:, H:H + 1]), reads=[prev], writes=[dst])

            def D7(hh):
                W3, W4 = HV[hh]["W"][2], HV[hh]["W"][3]
                vf = HV[hh]["vf"]
                b.emit("v", lambda v, W3=W3, W4=W4, vf=vf: v.tensor_mul(
                    W3[:, :], W4[:, :], vf[:, :]), reads=[W4, vf], writes=[W3])

            def D8(hh):
                W2, W3 = HV[hh]["W"][1], HV[hh]["W"][2]
                sa = HV[hh]["sa"]
                b.emit("v", lambda v, W2=W2, W3=W3, sa=sa: v.tensor_add(
                    W2[:, :], sa[:, 0:H], W3[:, :]), reads=[sa, W3], writes=[W2])

            def D9(hh):
                W1, W4 = HV[hh]["W"][0], HV[hh]["W"][3]
                sb = HV[hh]["sb"]
                b.emit("v", lambda v, W1=W1, W4=W4, sb=sb: v.tensor_add(
                    W1[:, :], sb[:, 0:H], W4[:, :]), reads=[sb, W4], writes=[W1])

            def A4(hh):
                W1, W3 = HV[hh]["W"][0], HV[hh]["W"][2]
                b.emit("s", lambda s, W1=W1, W3=W3: s.activation(
                    W3[:, :], W1[:, :], AF.Ln), reads=[W1], writes=[W3])

            def A5(hh):
                W1, W3 = HV[hh]["W"][0], HV[hh]["W"][2]
                b.emit("s", lambda s, W1=W1, W3=W3: s.activation(
                    W1[:, :], W3[:, :], AF.Exp, scale=-1.0),
                    reads=[W3], writes=[W1])

            def D10(hh):
                W1, W2, W3 = HV[hh]["W"][0], HV[hh]["W"][1], HV[hh]["W"][2]
                b.emit("v", lambda v, W1=W1, W2=W2, W3=W3: v.tensor_mul(
                    W3[:, :], W2[:, :], W1[:, :]), reads=[W2, W1], writes=[W3])

            def D11(hh):
                W3 = HV[hh]["W"][2]
                sgr = HV[hh]["sgr"]
                lo = HV[hh]["lo"]
                b.emit("v", lambda v, W3=W3, sgr=sgr, lo=lo, j=j: v.tensor_mul(
                    OWF[j][:, lo:lo + H], W3[:, :], sgr[:, :]),
                    reads=[W3, sgr], writes=[OWF[j]])

            for hh in range(2):
                A1(hh)
                D1(hh)
                DCLIP(HV[hh]["W"][0])
                A2(hh)
            for hh in range(2):
                DCLIP(HV[hh]["W"][3])
                A3(hh)
            D4(0)
            DSCAN(0, 0)
            DSCAN(0, 1)
            D4(1)
            D7(0)
            D8(0)
            D9(0)
            A4(0)
            DSCAN(1, 0)
            DSCAN(1, 1)
            DCARRY(0)
            DCARRY(1)
            A5(0)
            if ins1 is not None:
                ins1()
            D7(1)
            D8(1)
            D9(1)
            A4(1)
            D10(0)
            D11(0)
            A5(1)
            if ins2 is not None:
                ins2()
            D10(1)
            D11(1)
            if ins3 is not None:
                ins3()

        pre_k(0)
        pre_v(0)
        pre_r(0)
        for j in range(NCH):
            n = j + 1
            emit_middle(j,
                        ins1=(lambda n=n: pre_k(n)) if n < NCH else None,
                        ins2=(lambda n=n: pre_v(n)) if n < NCH else None,
                        ins3=(lambda n=n: pre_r(n)) if n < NCH else None)

        WoT = [b.at(R1, c * 2048, f"WoT{c}", [128, C], BF16) for c in range(NCH)]
        GT = {w: [[b.at(R1, 16384 + (wi * 2 + bb) * NCH * 1024 + c * 1024,
                        f"{w}T{bb}_{c}", [128, 512], BF16)
                   for c in range(NCH)] for bb in range(2)]
              for wi, w in enumerate(("g", "u"))}
        old_wt = [WT[p][c] for p in "kvr" for c in range(NCH)]
        b.handoff(WoT + GT["g"][0] + GT["g"][1] + GT["u"][0] + GT["u"][1],
                  old_wt)

        for c in range(NCH):
            b.emit("y", lambda s, c=c: s.dma_start_transpose(
                WoT[c][:, :], wbf["o"].t[0:C, c * 128:(c + 1) * 128]),
                reads=[wbf["o"]], writes=[WoT[c]], sem="dwt_o", dma=True,
                extra_waits=wo_store_toks)
        b.settle(WoT, b.tok("dwt_o"))

        def load_gt_group(grp):
            gn = min(4, NIC - grp * 4)
            bb = grp % 2
            for wname, wdram in (("g", wgq_dram), ("u", wuq_dram)):
                for c in range(NCH):
                    b.emit("y", lambda s, wname=wname, wdram=wdram, c=c,
                           grp=grp, bb=bb, gn=gn: s.dma_start_transpose(
                               GT[wname][bb][c][:, 0:gn * 128],
                               wdram.t[grp * 512:grp * 512 + gn * 128,
                                       c * 128:(c + 1) * 128]),
                           reads=[], writes=[GT[wname][bb][c]],
                           sem=f"dgt{wname}{(bb * 2 + c % 2)}", dma=True)
            for wname in ("g", "u"):
                tks = [b.tok(f"dgt{wname}{bb * 2}"),
                       b.tok(f"dgt{wname}{bb * 2 + 1}")]
                for c in range(NCH):
                    GT[wname][bb][c].w = tks[0]
                    GT[wname][bb][c].extra = [tks[1]]

        load_gt_group(0)
        load_gt_group(1)

        XR = [b.at(R2, i * 4096, f"xr{i}", [128, C], F32) for i in range(2)]
        AST = [b.at(R2, 8192 + i * 4096, f"ast{i}", [128, C], F32) for i in range(2)]
        SGH = [b.at(R2, 16384 + i * 4096, f"sg{i}", [128, H], F32) for i in range(2)]
        HST = [b.at(R2, 24576 + i * 4096, f"hst{i}", [128, T], F16) for i in range(2)]
        b.handoff(XR + AST + SGH + HST + XF + QS, S)

        fi = iter(xq_units)
        for k in range(NTT):
            pW = next_ps()
            ast = AST[k % 2]
            x = XR[k % 2]
            b.emit("g", lambda g, k=k, x=x: g.dma_start(
                out=x[:, :], in_=x_d.t[k * 128:(k + 1) * 128, :]),
                reads=[], writes=[x], sem=f"dx{k % 2}", dma=True)
            b.emit("v", lambda v, x=x: v.tensor_add(
                x[:, :], x[:, :], bo_bc[:, :]),
                reads=[bo_bc, x], writes=[x])

            def womm(tt, pW=pW, k=k):
                last = None
                for c in range(NCH):
                    for q in range(2):
                        last = tt.matmul(
                            pW[:, q * 512:(q + 1) * 512],
                            OWF[c][:, k * 128:(k + 1) * 128],
                            WoT[c][:, q * 512:(q + 1) * 512],
                            start=(c == 0), stop=(c == NCH - 1))
                return last
            b.emit("t", womm, reads=OWF + WoT, writes=[pW])
            b.emit("v", lambda v, pW=pW, ast=ast, x=x: v.tensor_add(
                ast[:, :], pW[:, 0:C], x[:, :]),
                reads=[pW, x], writes=[ast])
            b.emit("y", lambda s, k=k, ast=ast: s.dma_start(
                out=acc_d.t[k * 128:(k + 1) * 128, :], in_=ast[:, :]),
                reads=[ast], writes=[], sem=f"dacc{k % 2}", dma=True)
            if k < 8:
                for _ in range(2):
                    u = next(fi, None)
                    if u is not None:
                        u()
            if k == 10:
                qtk = [b.tok("dqo0"), b.tok("dqo1")]
                for c in range(NCH):
                    b.emit("y", lambda s, c=c: s.dma_start_transpose(
                        xnF[c][:, :], qdram.t[0:T, c * 128:(c + 1) * 128]),
                        reads=[], writes=[xnF[c]], sem=f"dxqF{c % 4}",
                        dma=True, extra_waits=qtk)
                xqtk = [b.tok(f"dxqF{i}") for i in range(4)]
                for c in range(NCH):
                    xnF[c].w = xqtk[0]
                    xnF[c].extra = xqtk[1:]
        for u in fi:
            u()

        b.emit("g", lambda g: g.memset(hmax[:, :], 0.0), writes=[hmax])
        for j in range(NIC):
            grp, s_in = j // 4, j % 4
            bb = grp % 2
            if s_in == 0 and grp >= 1 and grp + 1 <= (NIC - 1) // 4:
                load_gt_group(grp + 1)
            pG = next_ps()
            pU = next_ps()
            sl = slice(s_in * 128, (s_in + 1) * 128)

            def gmm(tt, pG=pG, bb=bb, sl=sl):
                last = None
                for c in range(NCH):
                    for q in range(4):
                        last = tt.matmul(
                            pG[:, q * 512:(q + 1) * 512],
                            GT["g"][bb][c][:, sl],
                            xnF[c][:, q * 512:(q + 1) * 512],
                            start=(c == 0), stop=(c == NCH - 1))
                return last

            def umm(tt, pU=pU, bb=bb, sl=sl):
                last = None
                for c in range(NCH):
                    for q in range(4):
                        last = tt.matmul(
                            pU[:, q * 512:(q + 1) * 512],
                            GT["u"][bb][c][:, sl],
                            xnF[c][:, q * 512:(q + 1) * 512],
                            start=(c == 0), stop=(c == NCH - 1))
                return last
            b.emit("t", gmm, reads=[GT["g"][bb][c] for c in range(NCH)] + xnF,
                   writes=[pG])
            b.emit("t", umm, reads=[GT["u"][bb][c] for c in range(NCH)] + xnF,
                   writes=[pU])
            hst = HST[j % 2]
            for hh in range(2):
                sg = SGH[hh]
                lo = hh * H
                b.emit("s", lambda s, pG=pG, lo=lo, sg=sg: s.activation(
                    sg[:, :], pG[:, lo:lo + H], AF.Silu, scale=sx_col[:, 0:1]),
                    reads=[pG, sx_col], writes=[sg])
                b.emit("v", lambda v, pU=pU, lo=lo, sg=sg, hst=hst:
                       v.scalar_tensor_tensor(
                           hst[:, lo:lo + H], pU[:, lo:lo + H], sx_col[:, 0:1],
                           sg[:, :], ALU.mult, ALU.mult),
                       reads=[pU, sg, sx_col], writes=[hst])
            b.emit("v", lambda v, hst=hst: v.reduce_max(
                hmx1[:, :], hst[:, :], axis=AX, apply_absolute_value=True),
                reads=[hst], writes=[hmx1])
            b.emit("v", lambda v: v.tensor_max(hmax[:, :], hmax[:, :], hmx1[:, :]),
                   reads=[hmx1, hmax], writes=[hmax])
            b.emit("g", lambda g, j=j, hst=hst: g.dma_start(
                out=h_d.t[j * 128:(j + 1) * 128, :], in_=hst[:, :]),
                reads=[hst], writes=[], sem=f"dho{j % 2}", dma=True)

        b.emit("y", lambda s: s.dma_start(out=colscr.t[:], in_=hmax[:, 0:1]),
               reads=[hmax], writes=[colscr], sem="dcs2", dma=True)
        b.emit("y", lambda s: s.dma_start(out=row[0:1, :], in_=colscr.t[:]),
               reads=[colscr], writes=[row], sem="drow2", dma=True)
        b.emit("v", lambda v: v.reduce_max(one1[0:1, 0:1], row[0:1, :], axis=AX),
               reads=[row], writes=[one1])
        b.emit("g", lambda g: g.dma_start(out=lm2_d.t[0:1, 0:1], in_=one1[0:1, 0:1]),
               reads=[one1], writes=[lm2_d], sem="dlm2", dma=True)

        b.final_wait("g", [b.tok("dacc0"), b.tok("dacc1"), b.tok("dho0"),
                           b.tok("dho1"), b.tok("dlm2")])
        block = ctx.enter_context(nc.Block())
        b.build(block)
    return nc


def build_l3():
    nc = bass.Bass("TRN2", target_bir_lowering=False)
    ctx = contextlib.ExitStack()
    with ctx:
        b = Bld(nc, ctx)
        D = lambda n, sh, dt, k="ExternalInput": Buf(nc.dram_tensor(n, sh, dt, kind=k), n)
        h_d = D("h", [I, T], F16)
        acc_d = D("acc", [T, C], F32)
        wdq_dram = D("wdq", [C, I], BF16)
        lm2_d = D("lm2", [1, 8], F32)
        clipd_d = D("clipd", [1], F32)
        out_d = D("out", [T, C], F32, "ExternalOutput")
        scaldr = b.dram("scaldr", [1, 2], F32)

        qh = [b.sbuf(f"qh{j}", [128, T], BF16) for j in range(NIC)]
        wdT = [b.sbuf(f"wdT{j}", [128, C], BF16) for j in range(NIC)]
        hst = [b.sbuf(f"hld{i}", [128, T], F16) for i in range(2)]
        accr = [b.sbuf(f"accr{i}", [128, C], F32) for i in range(2)]
        ost = [b.sbuf(f"ost{i}", [128, C], F32) for i in range(2)]
        t32 = [b.sbuf(f"t32_{i}", [128, T], F32) for i in range(2)]
        s2_col = b.sbuf("s2_col", [128, 1], F32)
        is2_col = b.sbuf("is2_col", [128, 1], F32)
        one1 = b.sbuf("one1", [1, 1], F32)
        one2 = b.sbuf("one2", [1, 1], F32)
        lmrow = b.sbuf("lmrow", [1, 8], F32)
        sc2 = b.sbuf("sc2", [1, 2], F32)
        magic_col = b.sbuf("magic_col", [128, 1], F32)
        ps = [b.psum(f"ps{i}", [128, C], F32) for i in range(4)]
        b.emit("g", lambda g: g.memset(magic_col[:, :], MAGIC), writes=[magic_col])

        b.emit("g", lambda g: g.dma_start(out=lmrow[0:1, :], in_=lm2_d.t[0:1, :]),
               reads=[], writes=[lmrow], sem="dlm2", dma=True)
        b.emit("g", lambda g: g.dma_start(out=one1[0:1, 0:1], in_=clipd_d.t[0:1]),
               reads=[], writes=[one1], sem="dcd", dma=True)
        b.emit("v", lambda v: v.reduce_max(one2[0:1, 0:1], lmrow[0:1, :], axis=AX),
               reads=[lmrow], writes=[one2])
        b.emit("s", lambda s: s.activation(one1[0:1, 0:1], one1[0:1, 0:1], AF.Sigmoid),
               reads=[one1], writes=[one1])
        b.emit("v", lambda v: v.tensor_mul(sc2[0:1, 0:1], one1[0:1, 0:1],
                                           one2[0:1, 0:1]),
               reads=[one1, one2], writes=[sc2])
        b.emit("v", lambda v: v.tensor_scalar_mul(sc2[0:1, 0:1], sc2[0:1, 0:1],
                                                  1.0 / QMAX),
               reads=[sc2], writes=[sc2])
        b.emit("v", lambda v: v.reciprocal(sc2[0:1, 1:2], sc2[0:1, 0:1]),
               reads=[sc2], writes=[sc2])
        b.emit("y", lambda s: s.dma_start(out=scaldr.t[0:1, :], in_=sc2[0:1, :]),
               reads=[sc2], writes=[scaldr], sem="dscw", dma=True)
        b.emit("y", lambda s: s.dma_start(
            out=s2_col[:, 0:1], in_=bass.AP(scaldr.t, 0, [[0, 128], [1, 1]])),
            reads=[scaldr], writes=[s2_col], sem="ds2", dma=True)
        b.emit("y", lambda s: s.dma_start(
            out=is2_col[:, 0:1], in_=bass.AP(scaldr.t, 1, [[0, 128], [1, 1]])),
            reads=[scaldr], writes=[is2_col], sem="dis2", dma=True)

        for j in range(NIC):
            b.emit("y", lambda s, j=j: s.dma_start_transpose(
                wdT[j][:, :], wdq_dram.t[0:C, j * 128:(j + 1) * 128]),
                reads=[wdq_dram], writes=[wdT[j]], sem=f"dwdT{j % 2}", dma=True)
        wtk = [b.tok("dwdT0"), b.tok("dwdT1")]
        for j in range(NIC):
            wdT[j].w = wtk[0]
            wdT[j].extra = [wtk[1]]

        for j in range(NIC):
            hl = hst[j % 2]
            tl = t32[j % 2]
            b.emit("s", lambda s, j=j, hl=hl: s.dma_start(
                out=hl[:, :], in_=h_d.t[j * 128:(j + 1) * 128, :]),
                reads=[], writes=[hl], sem=f"dh{j % 2}", dma=True)
            b.emit("s", lambda s, hl=hl, tl=tl: s.activation(
                tl[:, :], hl[:, :], AF.Identity, bias=magic_col[:, 0:1],
                scale=is2_col[:, 0:1]),
                reads=[hl, is2_col, magic_col], writes=[tl])
            b.emit("v", lambda v, tl=tl: v.tensor_scalar(
                tl[:, :], tl[:, :], MAGIC, -(QMAX + 1.0), ALU.subtract, ALU.max),
                reads=[tl], writes=[tl])
            b.emit("v", lambda v, j=j, tl=tl: v.tensor_scalar_min(
                qh[j][:, :], tl[:, :], QMAX),
                reads=[tl], writes=[qh[j]])

        for k in range(NTT):
            pD = ps[k % 4]
            ac = accr[k % 2]
            os_ = ost[k % 2]
            b.emit("g", lambda g, k=k, ac=ac: g.dma_start(
                out=ac[:, :], in_=acc_d.t[k * 128:(k + 1) * 128, :]),
                reads=[], writes=[ac], sem=f"dac{k % 2}", dma=True)

            for j0 in range(0, NIC, 6):
                j1 = min(j0 + 6, NIC)

                def dmmj(tt, pD=pD, k=k, j0=j0, j1=j1):
                    last = None
                    for j in range(j0, j1):
                        for q in range(2):
                            last = tt.matmul(
                                pD[:, q * 512:(q + 1) * 512],
                                qh[j][:, k * 128:(k + 1) * 128],
                                wdT[j][:, q * 512:(q + 1) * 512],
                                start=(j == 0), stop=(j == NIC - 1))
                    return last
                b.emit("t", dmmj,
                       reads=[qh[j] for j in range(j0, j1)] +
                             [wdT[j] for j in range(j0, j1)], writes=[pD])
            b.emit("v", lambda v, pD=pD, ac=ac, os_=os_: v.scalar_tensor_tensor(
                os_[:, :], pD[:, 0:C], s2_col[:, 0:1], ac[:, :],
                ALU.mult, ALU.add),
                reads=[pD, ac, s2_col], writes=[os_])
            b.emit("y", lambda s, k=k, os_=os_: s.dma_start(
                out=out_d.t[k * 128:(k + 1) * 128, :], in_=os_[:, :]),
                reads=[os_], writes=[], sem=f"dout{k % 2}", dma=True)

        b.final_wait("g", [b.tok("dout0"), b.tok("dout1")])
        block = ctx.enter_context(nc.Block())
        b.build(block)
    return nc


def kernel(L0_attn_x, L0_attn_Wr, L0_attn_br, L0_attn_Wk, L0_attn_bk,
           L0_attn_Wv, L0_attn_bv, L0_attn_Wo, L0_attn_bo, L0_attn_time_decay,
           L0_attn_U, L0_ffn_clip_x, L0_ffn_clip_down, L0_ffn_w_gate,
           L0_ffn_w_up, L0_ffn_w_down):
    n = 8
    x = np.ascontiguousarray(np.asarray(L0_attn_x, dtype=np.float32))
    xs = [np.ascontiguousarray(x[i]) for i in range(n)]
    f32 = lambda a: np.ascontiguousarray(np.asarray(a, dtype=np.float32))
    Wr, Wk, Wv, Wo = f32(L0_attn_Wr), f32(L0_attn_Wk), f32(L0_attn_Wv), f32(L0_attn_Wo)
    br, bk, bv, bo = f32(L0_attn_br), f32(L0_attn_bk), f32(L0_attn_bv), f32(L0_attn_bo)
    td, Uv = f32(L0_attn_time_decay), f32(L0_attn_U)
    clipx, clipd = f32(L0_ffn_clip_x), f32(L0_ffn_clip_down)
    wg, wu, wd = f32(L0_ffn_w_gate), f32(L0_ffn_w_up), f32(L0_ffn_w_down)

    cores = list(range(n))
    gstart = [0, 384, 768, 1152, 1536, 1920, 2304, 2560]
    grows = [384] * 6 + [256] * 2

    def wslice(w, i):
        blk = w[gstart[i]:gstart[i] + grows[i]]
        if grows[i] < 384:
            blk = np.concatenate([blk, blk[:384 - grows[i]]], axis=0)
        return np.ascontiguousarray(blk)

    nc1 = build_l1()
    in1 = [{"x": xs[i], "wg": wslice(wg, i), "wu": wslice(wu, i),
            "wd": np.ascontiguousarray(wd[i * 128:(i + 1) * 128]),
            "Wr": np.ascontiguousarray(Wr[i * 128:(i + 1) * 128]),
            "Wk": np.ascontiguousarray(Wk[i * 128:(i + 1) * 128]),
            "Wv": np.ascontiguousarray(Wv[i * 128:(i + 1) * 128]),
            "Wo": np.ascontiguousarray(Wo[i * 128:(i + 1) * 128])}
           for i in range(n)]
    r1 = run_bass_kernel_spmd(nc1, in1, core_ids=cores)
    lm1 = np.concatenate(
        [np.asarray(r1.results[i]["lm1"]).reshape(1, 1) for i in range(n)], axis=1)
    xnf = [np.asarray(r1.results[i]["xnf"]) for i in range(n)]
    xnb = [np.asarray(r1.results[i]["xnb"]) for i in range(n)]
    wgq = np.concatenate(
        [np.asarray(r1.results[i]["wgq"])[:grows[i]] for i in range(n)], axis=0)
    wuq = np.concatenate(
        [np.asarray(r1.results[i]["wuq"])[:grows[i]] for i in range(n)], axis=0)
    wdq = np.concatenate(
        [np.asarray(r1.results[i]["wdq"]) for i in range(n)], axis=0)
    wbf = {p: np.concatenate(
        [np.asarray(r1.results[i][f"wbf{p}"]) for i in range(n)], axis=0)
        for p in "rkvo"}

    nc2 = build_l2()
    in2 = [{"x": xs[i], "xnf": xnf[i], "xnb": xnb[i], "lm1": lm1,
            "wbfr": wbf["r"], "wbfk": wbf["k"], "wbfv": wbf["v"],
            "wbfo": wbf["o"],
            "br": br, "bk": bk, "bv": bv, "bo": bo,
            "td": td, "Uv": Uv, "clipx": clipx, "wgq": wgq, "wuq": wuq}
           for i in range(n)]
    r2 = run_bass_kernel_spmd(nc2, in2, core_ids=cores)
    lm2 = np.concatenate(
        [np.asarray(r2.results[i]["lm2"]).reshape(1, 1) for i in range(n)], axis=1)
    acc = [np.asarray(r2.results[i]["acc"]) for i in range(n)]
    hh = [np.asarray(r2.results[i]["h"]) for i in range(n)]

    nc3 = build_l3()
    in3 = [{"h": hh[i], "acc": acc[i], "wdq": wdq, "lm2": lm2, "clipd": clipd}
           for i in range(n)]
    r3 = run_bass_kernel_spmd(nc3, in3, core_ids=cores)
    out = np.stack([np.asarray(r3.results[i]["out"]) for i in range(n)], axis=0)
    return out.astype(np.float32)


# version 17
# speedup vs baseline: 1.0785x; 1.0076x over previous
import math
import contextlib
import numpy as np
import ml_dtypes
import concourse.bass as bass
import concourse.mybir as mybir
from concourse.bass_utils import run_bass_kernel_spmd

F32 = mybir.dt.float32
BF16 = mybir.dt.bfloat16
F16 = mybir.dt.float16
U8 = mybir.dt.uint8
AF = mybir.ActivationFunctionType
ALU = mybir.AluOpType
AX = mybir.AxisListType.X

MAGIC = 12582912.0
T, C, I = 2048, 1024, 2816
NCH = C // 128
NIC = I // 128
NTT = T // 128
H = T // 2
_EPS = math.exp(-1.0 / 12.0)
EPS_R = _EPS / 2.0
EPS_K = _EPS + 0.3 / 11.0
EPS_V = _EPS
QMAX = 127.0


_ENG_ATTR = {"v": "vector", "s": "scalar", "t": "tensor", "g": "gpsimd", "y": "sync"}


class Buf:
    __slots__ = ("t", "w", "rs", "extra", "name")

    def __init__(self, t, name=""):
        self.t = t
        self.w = None
        self.rs = []
        self.extra = []
        self.name = name

    def __getitem__(self, idx):
        return self.t[idx]


class Bld:
    def __init__(self, nc, ctx):
        self.nc = nc
        self.ctx = ctx
        self.prog = []
        self.sems = {}
        self.waited = {}
        self.dma_sems = set()

    def sbuf(self, name, shape, dtype):
        return Buf(self.ctx.enter_context(self.nc.sbuf_tensor(name, shape, dtype)), name)

    def psum(self, name, shape, dtype):
        return Buf(self.ctx.enter_context(self.nc.psum_tensor(name, shape, dtype)), name)

    def dram(self, name, shape, dtype, kind="Internal"):
        return Buf(self.nc.dram_tensor(name, shape, dtype, kind=kind), name)

    def region(self, name, nbytes):
        t = self.ctx.enter_context(self.nc.sbuf_tensor(name, [128, nbytes], U8))
        return self.nc.lookup_mloc(t).addr

    def at(self, addr, rel, name, shape, dtype):
        t = self.nc.alloc_sbuf_tensor_at(name, shape, dtype, offset=addr + rel)
        return Buf(t, name)

    def handoff(self, new_bufs, old_bufs):
        toks = []
        for b in old_bufs:
            if b.w:
                toks.append(b.w)
            toks.extend(b.rs)
            toks.extend(b.extra)
        for nb in new_bufs:
            nb.extra = list(toks)

    def _sem(self, key):
        if key not in self.sems:
            h = self.ctx.enter_context(self.nc.semaphore(key[:30]))
            self.sems[key] = [h, 0]
        return self.sems[key]

    def emit(self, eng, fn, reads=(), writes=(), sem=None, dma=False, n_dma=1,
             extra_waits=()):
        if dma:
            assert sem is not None
        sem_key = sem or ("c" + eng)
        s = self._sem(sem_key)
        amount = 16 * n_dma if dma else 1
        toks = list(extra_waits)
        for b in reads:
            if b.w:
                toks.append(b.w)
            toks.extend(b.extra)
        for b in writes:
            if b.w:
                toks.append(b.w)
            toks.extend(b.rs)
            toks.extend(b.extra)
        if dma:
            toks.append((sem_key, s[1]))
            self.dma_sems.add(sem_key)
        wd = self.waited.setdefault(eng, {})
        final = {}
        for (sk, v) in toks:
            if v > wd.get(sk, 0) and v > final.get(sk, 0):
                final[sk] = v
        s[1] += amount
        tok = (sem_key, s[1])
        waits = sorted(final.items())

        def closure(e, waits=waits, semh=s[0], amt=amount, fn=fn):
            for sk, v in waits:
                e.wait_ge(self.sems[sk][0], v)
            fn(e).then_inc(semh, amt)

        self.prog.append((eng, closure))
        for sk, v in final.items():
            wd[sk] = v
        for b in reads:
            b.rs.append(tok)
        for b in writes:
            b.w = tok
            b.rs = []
            b.extra = []
        return tok

    def tok(self, sem_key):
        return (sem_key, self._sem(sem_key)[1])

    def settle(self, bufs, tok):
        for b in bufs:
            b.w = tok
            b.rs = []
            b.extra = []

    def final_wait(self, eng, toks=None):
        toks = [self.tok(k) for k in sorted(self.dma_sems)]
        def closure(e, toks=toks):
            for sk, v in toks:
                e.wait_ge(self.sems[sk][0], v)
        self.prog.append((eng, closure))

    def build(self, block):
        per = {k: [] for k in _ENG_ATTR}
        for eng, closure in self.prog:
            per[eng].append(closure)
        for key, closures in per.items():
            if not closures:
                continue

            def body(e, closures=closures):
                for c in closures:
                    c(e)

            getattr(block, _ENG_ATTR[key])(body)


def build_l1():
    nc = bass.Bass("TRN2", target_bir_lowering=False)
    ctx = contextlib.ExitStack()
    with ctx:
        b = Bld(nc, ctx)
        x_d = Buf(nc.dram_tensor("x", [T, C], F32, kind="ExternalInput"), "x")
        wg_d = Buf(nc.dram_tensor("wg", [384, C], F32, kind="ExternalInput"), "wg")
        wu_d = Buf(nc.dram_tensor("wu", [384, C], F32, kind="ExternalInput"), "wu")
        wd_d = Buf(nc.dram_tensor("wd", [128, I], F32, kind="ExternalInput"), "wd")
        xnf_d = Buf(nc.dram_tensor("xnf", [T, C], F32, kind="ExternalOutput"), "xnf")
        xnb_d = Buf(nc.dram_tensor("xnb", [T, C], BF16, kind="ExternalOutput"), "xnb")
        lm1_d = Buf(nc.dram_tensor("lm1", [1, 1], F32, kind="ExternalOutput"), "lm1")
        wgq_d = Buf(nc.dram_tensor("wgq", [384, C], BF16, kind="ExternalOutput"), "wgq")
        wuq_d = Buf(nc.dram_tensor("wuq", [384, C], BF16, kind="ExternalOutput"), "wuq")
        wdq_d = Buf(nc.dram_tensor("wdq", [128, I], BF16, kind="ExternalOutput"), "wdq")
        aw_in = {p: Buf(nc.dram_tensor(f"W{p}", [128, C], F32, kind="ExternalInput"),
                        f"W{p}") for p in "rkvo"}
        wtt_d = Buf(nc.dram_tensor("wtt", [128, 3072], BF16,
                                   kind="ExternalOutput"), "wtt")
        wot_d = Buf(nc.dram_tensor("wot", [128, 1024], BF16,
                                   kind="ExternalOutput"), "wot")
        colscr = b.dram("colscr", [128], F32)

        xt = [b.sbuf(f"xt{i}", [128, C], F32) for i in range(2)]
        xnf = [b.sbuf(f"xnf{i}", [128, C], F32) for i in range(2)]
        xnb = [b.sbuf(f"xnb{i}", [128, C], BF16) for i in range(2)]
        sq = [b.sbuf(f"sq{i}", [128, C], F32) for i in range(2)]
        ss = [b.sbuf(f"ss{i}", [128, 1], F32) for i in range(2)]
        sm = [b.sbuf(f"sm{i}", [128, 1], F32) for i in range(2)]
        mcol = [b.sbuf(f"mcol{i}", [128, 1], F32) for i in range(2)]
        vcol = [b.sbuf(f"vcol{i}", [128, 1], F32) for i in range(2)]
        sdcol = [b.sbuf(f"sdcol{i}", [128, 1], F32) for i in range(2)]
        rstd = [b.sbuf(f"rstd{i}", [128, 1], F32) for i in range(2)]
        nmr = [b.sbuf(f"nmr{i}", [128, 1], F32) for i in range(2)]
        mxc = b.sbuf("mxc", [128, 1], F32)
        mxacc = b.sbuf("mxacc", [128, 1], F32)
        row = b.sbuf("rowb", [1, 128], F32)
        one1 = b.sbuf("one1", [1, 1], F32)
        magic_col = b.sbuf("magic_col", [128, 1], F32)
        NPAR = 4
        gwst = [b.sbuf(f"gwst{i}", [128, I], F32) for i in range(NPAR)]
        gqo = [b.sbuf(f"gqo{i}", [128, I], BF16) for i in range(NPAR)]
        rmaxc = [b.sbuf(f"rmaxc{i}", [128, 1], F32) for i in range(NPAR)]
        rscl = [b.sbuf(f"rscl{i}", [128, 1], F32) for i in range(NPAR)]
        rinv = [b.sbuf(f"rinv{i}", [128, 1], F32) for i in range(NPAR)]

        b.emit("g", lambda g: g.memset(mxacc[:, :], 0.0), writes=[mxacc])
        b.emit("g", lambda g: g.memset(magic_col[:, :], MAGIC), writes=[magic_col])
        awst = [b.sbuf(f"awst{i}", [128, C], BF16) for i in range(2)]
        ident = b.sbuf("ident", [128, 128], BF16)
        iota_a = b.sbuf("iota_a", [128, 128], F32)
        iota_b = b.sbuf("iota_b", [128, 128], F32)
        ttile = b.sbuf("ttile", [128, 3072], BF16)
        otile = b.sbuf("otile", [128, 1024], BF16)
        pt = [b.psum(f"pt{i}", [128, 1024], BF16) for i in range(2)]
        b.emit("g", lambda g: g.iota(iota_a[:, :], [[0, 128]], base=0,
                                     channel_multiplier=1,
                                     allow_small_or_imprecise_dtypes=True),
               writes=[iota_a])
        b.emit("g", lambda g: g.iota(iota_b[:, :], [[1, 128]], base=0,
                                     channel_multiplier=0,
                                     allow_small_or_imprecise_dtypes=True),
               writes=[iota_b])
        b.emit("v", lambda v: v.tensor_tensor(ident[:, :], iota_a[:, :],
                                              iota_b[:, :], ALU.is_equal),
               reads=[iota_a, iota_b], writes=[ident])
        for pi, p in enumerate("rkvo"):
            st = awst[pi % 2]
            pp = pt[pi % 2]
            dst = otile if p == "o" else ttile
            off = 0 if p == "o" else {"r": 0, "k": 1, "v": 2}[p] * 1024
            b.emit("g", lambda g, p=p, st=st: g.dma_start(
                out=st[:, :], in_=aw_in[p].t[0:128, :]),
                reads=[], writes=[st], sem=f"dawi{pi % 2}", dma=True)

            def trans(tt, st=st, pp=pp):
                last = None
                for c in range(NCH):
                    last = tt.transpose(pp[:, c * 128:(c + 1) * 128],
                                        st[:, c * 128:(c + 1) * 128],
                                        ident[:, :])
                return last
            b.emit("t", trans, reads=[st, ident], writes=[pp])
            b.emit("s", lambda s, pp=pp, dst=dst, off=off: s.activation(
                dst[:, off:off + 1024], pp[:, :], AF.Copy),
                reads=[pp], writes=[dst])
        b.emit("y", lambda s: s.dma_start(out=wtt_d.t[0:128, :],
                                          in_=ttile[:, :]),
               reads=[ttile], writes=[], sem="dwtt", dma=True)
        b.emit("y", lambda s: s.dma_start(out=wot_d.t[0:128, :],
                                          in_=otile[:, :]),
               reads=[otile], writes=[], sem="dwot", dma=True)

        _unit_ctr = [0]

        def make_wq_unit(wsrc, wdst, c, ncols, sem_pref):
            def unit():
                par = _unit_ctr[0] % NPAR
                _unit_ctr[0] += 1
                st = gwst[par]
                qo = gqo[par]
                rm, rs, ri = rmaxc[par], rscl[par], rinv[par]
                b.emit("g", lambda g: g.dma_start(
                    out=st[:, 0:ncols], in_=wsrc.t[c * 128:(c + 1) * 128, :]),
                    reads=[], writes=[st], sem=f"dgw{par}", dma=True)
                b.emit("v", lambda v: v.reduce_max(
                    rm[:, :], st[:, 0:ncols], axis=AX, apply_absolute_value=True),
                    reads=[st], writes=[rm])
                b.emit("v", lambda v: v.tensor_scalar_mul(
                    rs[:, :], rm[:, :], 1.0 / QMAX),
                    reads=[rm], writes=[rs])
                b.emit("v", lambda v: v.reciprocal(ri[:, :], rs[:, :]),
                       reads=[rs], writes=[ri])
                b.emit("s", lambda s: s.activation(
                    st[:, 0:ncols], st[:, 0:ncols], AF.Identity,
                    bias=magic_col[:, 0:1], scale=ri[:, 0:1]),
                    reads=[st, ri, magic_col], writes=[st])
                b.emit("v", lambda v: v.tensor_scalar(
                    st[:, 0:ncols], st[:, 0:ncols], MAGIC, -(QMAX + 1.0),
                    ALU.subtract, ALU.max),
                    reads=[st], writes=[st])
                b.emit("v", lambda v: v.tensor_scalar(
                    qo[:, 0:ncols], st[:, 0:ncols], QMAX, rs[:, 0:1],
                    ALU.min, ALU.mult),
                    reads=[st, rs], writes=[qo])
                b.emit("y", lambda s: s.dma_start(
                    out=wdst.t[c * 128:(c + 1) * 128, :], in_=qo[:, 0:ncols]),
                    reads=[qo], writes=[], sem=sem_pref + str(par % 2), dma=True)
            return unit

        wq_units = []
        for c in range(3):
            wq_units.append(make_wq_unit(wg_d, wgq_d, c, C, "dwgq"))
            wq_units.append(make_wq_unit(wu_d, wuq_d, c, C, "dwuq"))
        wq_units.append(make_wq_unit(wd_d, wdq_d, 0, I, "dwdq"))
        wq_iter = iter(wq_units)
        for k in range(NTT):
            pp = k % 2
            x = xt[pp]
            xf = xnf[pp]
            xb = xnb[pp]
            sq_, ss_, sm_ = sq[pp], ss[pp], sm[pp]
            mc, vc, sd_, rs_, nm_ = mcol[pp], vcol[pp], sdcol[pp], rstd[pp], nmr[pp]
            b.emit("g", lambda g, k=k, x=x: g.dma_start(
                out=x[:, :], in_=x_d.t[k * 128:(k + 1) * 128, :]),
                reads=[], writes=[x], sem=f"dx{k % 2}", dma=True)
            b.emit("s", lambda s, x=x, sq_=sq_, ss_=ss_: s.activation(
                sq_[:, :], x[:, :], AF.Square, accum_out=ss_[:, 0:1]),
                reads=[x], writes=[sq_, ss_])
            b.emit("v", lambda v, x=x, sm_=sm_: v.reduce_sum(
                sm_[:, :], x[:, :], axis=AX),
                reads=[x], writes=[sm_])
            b.emit("v", lambda v, sm_=sm_, mc=mc: v.tensor_scalar_mul(
                mc[:, :], sm_[:, :], 1.0 / C),
                reads=[sm_], writes=[mc])
            b.emit("v", lambda v, ss_=ss_, vc=vc: v.tensor_scalar_mul(
                vc[:, :], ss_[:, :], 1.0 / C),
                reads=[ss_], writes=[vc])
            b.emit("v", lambda v, mc=mc, sq_=sq_: v.scalar_tensor_tensor(
                sq_[:, 0:1], mc[:, :], -1.0, mc[:, :], ALU.mult, ALU.mult),
                reads=[mc], writes=[sq_])
            b.emit("v", lambda v, vc=vc, sq_=sq_: v.tensor_add(
                vc[:, :], vc[:, :], sq_[:, 0:1]),
                reads=[sq_], writes=[vc])
            b.emit("v", lambda v, vc=vc: v.tensor_scalar_add(vc[:, :], vc[:, :], 1e-5),
                   reads=[vc], writes=[vc])
            b.emit("s", lambda s, vc=vc, sd_=sd_: s.activation(
                sd_[:, :], vc[:, :], AF.Sqrt),
                reads=[vc], writes=[sd_])
            b.emit("v", lambda v, sd_=sd_, rs_=rs_: v.reciprocal(
                rs_[:, :], sd_[:, :]),
                reads=[sd_], writes=[rs_])
            b.emit("v", lambda v, mc=mc, rs_=rs_, nm_=nm_: v.scalar_tensor_tensor(
                nm_[:, :], mc[:, :], -1.0, rs_[:, :], ALU.mult, ALU.mult),
                reads=[mc, rs_], writes=[nm_])
            b.emit("v", lambda v, x=x, xf=xf, rs_=rs_, nm_=nm_: v.tensor_scalar(
                xf[:, :], x[:, :], rs_[:, 0:1], nm_[:, 0:1], ALU.mult, ALU.add),
                reads=[x, rs_, nm_], writes=[xf])
            b.emit("v", lambda v, xf=xf: v.reduce_max(
                mxc[:, :], xf[:, :], axis=AX, apply_absolute_value=True),
                reads=[xf], writes=[mxc])
            b.emit("v", lambda v: v.tensor_max(mxacc[:, :], mxacc[:, :], mxc[:, :]),
                   reads=[mxc, mxacc], writes=[mxacc])
            b.emit("s", lambda s, xf=xf, xb=xb: s.activation(
                xb[:, :], xf[:, :], AF.Copy),
                reads=[xf], writes=[xb])
            b.emit("y", lambda s, k=k, xf=xf: s.dma_start(
                out=xnf_d.t[k * 128:(k + 1) * 128, :], in_=xf[:, :]),
                reads=[xf], writes=[], sem=f"doxf{k % 2}", dma=True)
            b.emit("y", lambda s, k=k, xb=xb: s.dma_start(
                out=xnb_d.t[k * 128:(k + 1) * 128, :], in_=xb[:, :]),
                reads=[xb], writes=[], sem=f"doxb{k % 2}", dma=True)
            u = next(wq_iter, None)
            if u is not None:
                u()
        for u in wq_iter:
            u()
        b.emit("y", lambda s: s.dma_start(out=colscr.t[:], in_=mxacc[:, 0:1]),
               reads=[mxacc], writes=[colscr], sem="dcs", dma=True)
        b.emit("y", lambda s: s.dma_start(out=row[0:1, :], in_=colscr.t[:]),
               reads=[colscr], writes=[row], sem="drow", dma=True)
        b.emit("v", lambda v: v.reduce_max(one1[0:1, 0:1], row[0:1, :], axis=AX),
               reads=[row], writes=[one1])
        b.emit("g", lambda g: g.dma_start(out=lm1_d.t[0:1, 0:1], in_=one1[0:1, 0:1]),
               reads=[one1], writes=[lm1_d], sem="dlm", dma=True)
        b.final_wait("g", [b.tok("doxf0"), b.tok("doxf1"), b.tok("doxb0"),
                           b.tok("doxb1"), b.tok("dlm"),
                           b.tok("dwgq0"), b.tok("dwgq1"), b.tok("dwuq0"),
                           b.tok("dwuq1"), b.tok("dwdq0"), b.tok("dwdq1")])
        block = ctx.enter_context(nc.Block())
        b.build(block)
    return nc


def build_l2():
    nc = bass.Bass("TRN2", target_bir_lowering=False)
    ctx = contextlib.ExitStack()
    with ctx:
        b = Bld(nc, ctx)
        D = lambda n, sh, dt, k="ExternalInput": Buf(nc.dram_tensor(n, sh, dt, kind=k), n)
        x_d = D("x", [T, C], F32)
        xnf_d = D("xnf", [T, C], F32)
        xnb_d = D("xnb", [T, C], BF16)
        lm1_d = D("lm1", [1, 8], F32)
        wtt_d = D("wtt", [NCH, 128, 3072], BF16)
        wot_d = D("wot", [NCH, 128, 1024], BF16)
        br_d, bk_d, bv_d, bo_d = (D(n, [C], F32) for n in ("br", "bk", "bv", "bo"))
        td_d, U_d = D("td", [C], F32), D("Uv", [C], F32)
        clipx_d = D("clipx", [1], F32)
        wgq_dram, wuq_dram = D("wgq", [I, C], BF16), D("wuq", [I, C], BF16)
        acc_d = D("acc", [T, C], F32, "ExternalOutput")
        h_d = D("h", [I, T], F16, "ExternalOutput")
        lm2_d = D("lm2", [1, 1], F32, "ExternalOutput")
        qdram = b.dram("qdram", [T, C], BF16)
        owdram = b.dram("owdram", [C, T], BF16)
        colscr = b.dram("colscr", [128], F32)
        scaldr = b.dram("scaldr", [1, 2], F32)

        R4 = b.region("ov4", NCH * 4096)
        xnF = [b.at(R4, c * 4096, f"xnF{c}", [128, T], BF16) for c in range(NCH)]
        xqF = [b.sbuf(f"xqF{c}", [128, T], BF16) for c in range(NCH)]
        ramp = b.sbuf("ramp", [128, T], F32)
        R3 = b.region("ov3", 16384)
        WTJ = [b.at(R3, jj * 6144, f"wtj{jj}", [128, 3072], BF16)
               for jj in range(2)]
        bo_bc = b.at(R3, 12288, "bo_bc", [128, C], F32)
        P_IDX = {"r": 0, "k": 1, "v": 2}
        GT = {w: [[b.sbuf(f"{w}T{bb}_{c}", [128, 512], BF16)
                   for c in range(NCH)] for bb in range(1)]
              for w in ("g", "u")}
        OWS = [b.sbuf(f"ows{i}", [128, H], BF16) for i in range(2)]
        OWL = [b.sbuf(f"owl{i}", [128, C], BF16) for i in range(2)]
        HST = b.sbuf("hst0", [128, T], F16)
        SGH = [b.sbuf(f"sg{i}", [128, H], F32) for i in range(2)]

        cols = {n: b.sbuf(f"col_{n}", [128, NCH], F32)
                for n in ("td", "U", "bk", "br", "bv", "w", "ubk")}
        hmax = b.sbuf("hmaxc", [128, 1], F32)
        hmx1 = b.sbuf("hmx1", [128, 1], F32)
        sx_col = b.sbuf("sx_col", [128, 1], F32)
        isx_col = b.sbuf("isx_col", [128, 1], F32)
        magic_col = b.sbuf("magic_col", [128, 1], F32)
        nmagic_col = b.sbuf("nmagic_col", [128, 1], F32)
        ecols = {}
        for nm, val in (("k", EPS_K), ("v", EPS_V), ("r", EPS_R)):
            ecols[nm] = b.sbuf(f"eps_{nm}", [128, 1], F32)
        row = b.sbuf("rowb", [1, 128], F32)
        one1 = b.sbuf("one1", [1, 1], F32)
        one2 = b.sbuf("one2", [1, 1], F32)
        lmrow = b.sbuf("lmrow", [1, 8], F32)
        sc2 = b.sbuf("sc2", [1, 2], F32)

        NS = 16
        R2 = b.region("ov2", NS * 4096)
        S = [b.at(R2, i * 4096, f"scr{i}", [128, H], F32) for i in range(NS)]

        ps = [b.psum(f"ps{i}", [128, T], F32) for i in range(2)]

        GD, SY = "g", "y"

        b.emit("g", lambda g: g.memset(magic_col[:, :], MAGIC), writes=[magic_col])
        b.emit("g", lambda g: g.memset(nmagic_col[:, :], -MAGIC), writes=[nmagic_col])
        b.emit("g", lambda g: g.memset(hmax[:, :], 0.0), writes=[hmax])
        for nm, val in (("k", EPS_K), ("v", EPS_V), ("r", EPS_R)):
            b.emit("g", lambda g, nm=nm, val=val: g.memset(ecols[nm][:, :], val),
                   writes=[ecols[nm]])
        b.emit("g", lambda g: g.iota(ramp[:, :], [[1, T]], base=-(T - 1),
                                     channel_multiplier=0,
                                     allow_small_or_imprecise_dtypes=True),
               writes=[ramp])
        b.emit("y", lambda s: s.dma_start(
            out=bo_bc[:, :], in_=bass.AP(bo_d.t, 0, [[0, 128], [1, C]])),
            reads=[], writes=[bo_bc], sem="dbo", dma=True)
        for n, d in (("td", td_d), ("U", U_d), ("bk", bk_d), ("br", br_d),
                     ("bv", bv_d)):
            b.emit("g", lambda g, n=n, d=d: g.dma_start(
                out=cols[n][:, :], in_=d.t.ap().rearrange("(f p) -> p f", p=128),
                allow_slow_non_contiguous=True),
                reads=[], writes=[cols[n]], sem=f"dc_{n}", dma=True)
        b.emit("s", lambda s: s.activation(cols["w"][:, :], cols["td"][:, :], AF.Relu),
               reads=[cols["td"]], writes=[cols["w"]])
        b.emit("v", lambda v: v.tensor_add(cols["ubk"][:, :], cols["U"][:, :],
                                           cols["bk"][:, :]),
               reads=[cols["U"], cols["bk"]], writes=[cols["ubk"]])
        b.emit("g", lambda g: g.dma_start(out=lmrow[0:1, :], in_=lm1_d.t[0:1, :]),
               reads=[], writes=[lmrow], sem="dlm1", dma=True)
        b.emit("g", lambda g: g.dma_start(out=one1[0:1, 0:1], in_=clipx_d.t[0:1]),
               reads=[], writes=[one1], sem="dcx", dma=True)
        b.emit("v", lambda v: v.reduce_max(one2[0:1, 0:1], lmrow[0:1, :], axis=AX),
               reads=[lmrow], writes=[one2])
        b.emit("s", lambda s: s.activation(one1[0:1, 0:1], one1[0:1, 0:1], AF.Sigmoid),
               reads=[one1], writes=[one1])
        b.emit("v", lambda v: v.tensor_mul(sc2[0:1, 0:1], one1[0:1, 0:1],
                                           one2[0:1, 0:1]),
               reads=[one1, one2], writes=[sc2])
        b.emit("v", lambda v: v.tensor_scalar_mul(sc2[0:1, 0:1], sc2[0:1, 0:1],
                                                  1.0 / QMAX),
               reads=[sc2], writes=[sc2])
        b.emit("v", lambda v: v.reciprocal(sc2[0:1, 1:2], sc2[0:1, 0:1]),
               reads=[sc2], writes=[sc2])
        b.emit("y", lambda s: s.dma_start(out=scaldr.t[0:1, :], in_=sc2[0:1, :]),
               reads=[sc2], writes=[scaldr], sem="dscw", dma=True)
        b.emit("y", lambda s: s.dma_start(
            out=sx_col[:, 0:1], in_=bass.AP(scaldr.t, 0, [[0, 128], [1, 1]])),
            reads=[scaldr], writes=[sx_col], sem="dsx", dma=True)
        b.emit("y", lambda s: s.dma_start(
            out=isx_col[:, 0:1], in_=bass.AP(scaldr.t, 1, [[0, 128], [1, 1]])),
            reads=[scaldr], writes=[isx_col], sem="disx", dma=True)

        for c in range(NCH):
            b.emit("y", lambda s, c=c: s.dma_start_transpose(
                xnF[c][:, :], xnb_d.t[0:T, c * 128:(c + 1) * 128]),
                reads=[], writes=[xnF[c]], sem=f"dxnF{c % 4}", dma=True)
        xnF_toks = [b.tok(f"dxnF{i}") for i in range(4)]
        for c in range(NCH):
            xnF[c].w = xnF_toks[0]
            xnF[c].extra = xnF_toks[1:]

        def load_wtj(j):
            jj = j % 2
            b.emit("y", lambda s, j=j, jj=jj: s.dma_start(
                out=WTJ[jj][:, :], in_=wtt_d.t[j]),
                reads=[wtt_d], writes=[WTJ[jj]], sem=f"dwtj{j % 2}", dma=True)

        def load_gt_group(grp):
            gn = min(4, NIC - grp * 4)
            bb = 0
            for wname, wdram in (("g", wgq_dram), ("u", wuq_dram)):
                for c in range(NCH):
                    b.emit("y", lambda s, wname=wname, wdram=wdram, c=c,
                           grp=grp, bb=bb, gn=gn: s.dma_start_transpose(
                               GT[wname][bb][c][:, 0:gn * 128],
                               wdram.t[grp * 512:grp * 512 + gn * 128,
                                       c * 128:(c + 1) * 128]),
                           reads=[], writes=[GT[wname][bb][c]],
                           sem=f"dgt{wname}{(bb * 2 + c % 2)}", dma=True)
            for wname in ("g", "u"):
                tks = [b.tok(f"dgt{wname}{bb * 2}"),
                       b.tok(f"dgt{wname}{bb * 2 + 1}")]
                for c in range(NCH):
                    GT[wname][bb][c].w = tks[0]
                    GT[wname][bb][c].extra = [tks[1]]

        load_wtj(0)
        load_wtj(1)
        load_gt_group(0)

        def proj_mm(tt, pbuf, p, j):
            last = None
            base = P_IDX[p] * 1024
            for c in range(NCH):
                for q in range(4):
                    last = tt.matmul(
                        pbuf[:, q * 512:(q + 1) * 512],
                        WTJ[j % 2][:, base + c * 128:base + (c + 1) * 128],
                        xnF[c][:, q * 512:(q + 1) * 512],
                        start=(c == 0), stop=(c == NCH - 1))
            return last

        XF = [b.at(R2, (12 + i) * 4096, f"xf32_{i}", [128, C], F32)
              for i in range(2)]
        QS = [b.at(R2, 14 * 4096 + i * 2048, f"qs{i}", [128, C], BF16)
              for i in range(2)]

        for k in range(NTT):
            xf = XF[k % 2]
            q = QS[k % 2]
            ldq = "g" if k % 2 == 0 else "s"
            b.emit(ldq, lambda e, k=k, xf=xf: e.dma_start(
                out=xf[:, :], in_=xnf_d.t[k * 128:(k + 1) * 128, :]),
                reads=[], writes=[xf], sem=f"dxf{k % 2}", dma=True)
            b.emit("s", lambda s, xf=xf: s.activation(
                xf[:, :], xf[:, :], AF.Identity, bias=magic_col[:, 0:1],
                scale=isx_col[:, 0:1]),
                reads=[xf, isx_col, magic_col], writes=[xf])
            b.emit("s", lambda s, xf=xf: s.activation(
                xf[:, :], xf[:, :], AF.Identity, bias=nmagic_col[:, 0:1]),
                reads=[xf, nmagic_col], writes=[xf])
            b.emit("v", lambda v, xf=xf, q=q: v.tensor_scalar(
                q[:, :], xf[:, :], -(QMAX + 1.0), QMAX, ALU.max, ALU.min),
                reads=[xf], writes=[q])
            b.emit("g", lambda g, k=k, q=q: g.dma_start(
                out=qdram.t[k * 128:(k + 1) * 128, :], in_=q[:, :]),
                reads=[q], writes=[], sem=f"dqo{k % 2}", dma=True)
        qtk = [b.tok("dqo0"), b.tok("dqo1")]
        for c in range(NCH):
            b.emit("y", lambda s, c=c: s.dma_start_transpose(
                xqF[c][:, :], qdram.t[0:T, c * 128:(c + 1) * 128]),
                reads=[], writes=[xqF[c]], sem=f"dxqF{c % 4}",
                dma=True, extra_waits=qtk)
        xqtk = [b.tok(f"dxqF{i}") for i in range(4)]
        for c in range(NCH):
            xqF[c].w = xqtk[0]
            xqF[c].extra = xqtk[1:]
        b.handoff([S[12], S[13], S[14]], XF + QS)

        npsum = [0]

        def next_ps():
            p = ps[npsum[0] % 2]
            npsum[0] += 1
            return p

        SA = [b.sbuf(f"scanA{i}", [128, H + 1], F32) for i in range(2)]
        SB = [b.sbuf(f"scanB{i}", [128, H + 1], F32) for i in range(2)]
        b.emit("g", lambda g: g.memset(SA[0][:, 0:1], 0.0), writes=[SA[0]])
        b.emit("g", lambda g: g.memset(SB[0][:, 0:1], 0.0), writes=[SB[0]])
        KFh = [S[1], S[2]]
        VFh = [S[3], S[4]]
        SGRh = [S[5], S[6]]
        MW = [(S[7], S[8], S[9], S[10]), (S[11], S[12], S[13], S[14])]
        T1a, T1b = S[0], S[15]
        RW = (MW[0][0], MW[1][0])

        def emit_mm(p, j):
            pbuf = next_ps()
            b.emit("t", lambda tt, pbuf=pbuf, p=p, j=j: proj_mm(tt, pbuf, p, j),
                   reads=[WTJ[j % 2]] + xnF, writes=[pbuf])
            return pbuf

        def emit_evac(pbuf, eps, ecol_b, bias_b, j, d0, d1):
            ec = ecol_b[:, 0:1]
            bj = bias_b[:, j:j + 1]
            b.emit("v", lambda v, bj=bj: v.tensor_copy(T1a[:, 0:1], bj),
                   reads=[bias_b], writes=[T1a])
            b.emit("s", lambda s, pbuf=pbuf, ec=ec, bj=bj: s.activation(
                T1a[:, 1:H], pbuf[:, 0:H - 1], AF.Identity, bias=bj, scale=ec),
                reads=[pbuf, bias_b, ecol_b], writes=[T1a])
            b.emit("s", lambda s, pbuf=pbuf, ec=ec, bj=bj: s.activation(
                T1b[:, 0:H], pbuf[:, H - 1:T - 1], AF.Identity, bias=bj, scale=ec),
                reads=[pbuf, bias_b, ecol_b], writes=[T1b])
            b.emit("v", lambda v, pbuf=pbuf, eps=eps, d0=d0: v.scalar_tensor_tensor(
                d0[:, 0:H], pbuf[:, 1:H + 1], 1.0 - eps, T1a[:, 0:H],
                ALU.mult, ALU.add),
                reads=[pbuf, T1a], writes=[d0])
            b.emit("v", lambda v, pbuf=pbuf, eps=eps, d1=d1: v.scalar_tensor_tensor(
                d1[:, 0:H - 1], pbuf[:, H + 1:T], 1.0 - eps, T1b[:, 0:H - 1],
                ALU.mult, ALU.add),
                reads=[pbuf, T1b], writes=[d1])
            b.emit("v", lambda v, d1=d1: v.tensor_copy(
                d1[:, H - 1:H], T1b[:, H - 1:H]),
                reads=[T1b], writes=[d1])

        def emit_rinterp(pR, j):
            emit_evac(pR, EPS_R, ecols["r"], cols["br"], j, RW[0], RW[1])
            for hh in range(2):
                b.emit("s", lambda s, hh=hh: s.activation(
                    SGRh[hh][:, :], RW[hh][:, :], AF.Sigmoid),
                    reads=[RW[hh]], writes=[SGRh[hh]])

        def pre_k(j):
            p = emit_mm("k", j)
            emit_evac(p, EPS_K, ecols["k"], cols["bk"], j, KFh[0], KFh[1])

        def pre_v(j):
            p = emit_mm("v", j)
            emit_evac(p, EPS_V, ecols["v"], cols["bv"], j, VFh[0], VFh[1])

        def pre_r(j):
            p = emit_mm("r", j)
            emit_rinterp(p, j)

        def emit_middle(j, ins1=None, ins2=None, ins3=None):
            wj = cols["w"][:, j:j + 1]
            ubkj = cols["ubk"][:, j:j + 1]
            HV = [dict(W=MW[hh], kf=KFh[hh], vf=VFh[hh], sgr=SGRh[hh],
                       sa=SA[hh], sb=SB[hh], lo=hh * H) for hh in range(2)]

            def A1(hh):
                W4 = HV[hh]["W"][3]
                kf = HV[hh]["kf"]
                b.emit("s", lambda s, W4=W4, kf=kf, ubkj=ubkj: s.activation(
                    W4[:, :], kf[:, :], AF.Identity, bias=ubkj),
                    reads=[kf, cols["ubk"]], writes=[W4])

            def D1(hh):
                W1 = HV[hh]["W"][0]
                kf = HV[hh]["kf"]
                lo = HV[hh]["lo"]
                b.emit("v", lambda v, W1=W1, kf=kf, lo=lo, wj=wj:
                       v.scalar_tensor_tensor(W1[:, :], ramp[:, lo:lo + H], wj,
                                              kf[:, :], ALU.mult, ALU.add),
                       reads=[ramp, cols["w"], kf], writes=[W1])

            def DCLIP(buf):
                b.emit("v", lambda v, buf=buf: v.tensor_scalar(
                    buf[:, :], buf[:, :], -20.0, 10.0, ALU.max, ALU.min),
                    reads=[buf], writes=[buf])

            def A2(hh):
                W1, W2 = HV[hh]["W"][0], HV[hh]["W"][1]
                b.emit("s", lambda s, W1=W1, W2=W2: s.activation(
                    W2[:, :], W1[:, :], AF.Exp), reads=[W1], writes=[W2])

            def A3(hh):
                W4 = HV[hh]["W"][3]
                b.emit("s", lambda s, W4=W4: s.activation(
                    W4[:, :], W4[:, :], AF.Exp), reads=[W4], writes=[W4])

            def D4(hh):
                W2, W3 = HV[hh]["W"][1], HV[hh]["W"][2]
                vf = HV[hh]["vf"]
                b.emit("v", lambda v, W2=W2, W3=W3, vf=vf: v.tensor_mul(
                    W3[:, :], W2[:, :], vf[:, :]), reads=[W2, vf], writes=[W3])

            def DSCAN(hh, which):
                src = HV[hh]["W"][2] if which == 0 else HV[hh]["W"][1]
                dst = HV[hh]["sa"] if which == 0 else HV[hh]["sb"]
                if hh == 0:
                    b.emit("v", lambda v, src=src, dst=dst: v.tensor_tensor_scan(
                        dst[:, 1:H + 1], src[:, :], src[:, :], 0.0,
                        ALU.add, ALU.bypass),
                        reads=[src], writes=[dst])
                else:
                    prev = SA[0] if which == 0 else SB[0]
                    b.emit("v", lambda v, src=src, dst=dst, prev=prev:
                           v.tensor_tensor_scan(dst[:, 1:H + 1], src[:, :],
                                                src[:, :], prev[:, H:H + 1],
                                                ALU.add, ALU.bypass),
                           reads=[src, prev], writes=[dst])

            def DCARRY(which):
                dst = SA[1] if which == 0 else SB[1]
                prev = SA[0] if which == 0 else SB[0]
                b.emit("v", lambda v, dst=dst, prev=prev: v.tensor_copy(
                    dst[:, 0:1], prev[:, H:H + 1]), reads=[prev], writes=[dst])

            def D7(hh):
                W3, W4 = HV[hh]["W"][2], HV[hh]["W"][3]
                vf = HV[hh]["vf"]
                b.emit("v", lambda v, W3=W3, W4=W4, vf=vf: v.tensor_mul(
                    W3[:, :], W4[:, :], vf[:, :]), reads=[W4, vf], writes=[W3])

            def D8(hh):
                W2, W3 = HV[hh]["W"][1], HV[hh]["W"][2]
                sa = HV[hh]["sa"]
                b.emit("v", lambda v, W2=W2, W3=W3, sa=sa: v.tensor_add(
                    W2[:, :], sa[:, 0:H], W3[:, :]), reads=[sa, W3], writes=[W2])

            def D9(hh):
                W1, W4 = HV[hh]["W"][0], HV[hh]["W"][3]
                sb = HV[hh]["sb"]
                b.emit("v", lambda v, W1=W1, W4=W4, sb=sb: v.tensor_add(
                    W1[:, :], sb[:, 0:H], W4[:, :]), reads=[sb, W4], writes=[W1])

            def A4(hh):
                W1, W3 = HV[hh]["W"][0], HV[hh]["W"][2]
                b.emit("s", lambda s, W1=W1, W3=W3: s.activation(
                    W3[:, :], W1[:, :], AF.Ln), reads=[W1], writes=[W3])

            def A5(hh):
                W1, W3 = HV[hh]["W"][0], HV[hh]["W"][2]
                b.emit("s", lambda s, W1=W1, W3=W3: s.activation(
                    W1[:, :], W3[:, :], AF.Exp, scale=-1.0),
                    reads=[W3], writes=[W1])

            def D10(hh):
                W1, W2, W3 = HV[hh]["W"][0], HV[hh]["W"][1], HV[hh]["W"][2]
                b.emit("v", lambda v, W1=W1, W2=W2, W3=W3: v.tensor_mul(
                    W3[:, :], W2[:, :], W1[:, :]), reads=[W2, W1], writes=[W3])

            def D11(hh):
                W3 = HV[hh]["W"][2]
                sgr = HV[hh]["sgr"]
                lo = HV[hh]["lo"]
                ows = OWS[hh]
                b.emit("v", lambda v, W3=W3, sgr=sgr, ows=ows: v.tensor_mul(
                    ows[:, :], W3[:, :], sgr[:, :]),
                    reads=[W3, sgr], writes=[ows])
                b.emit("y", lambda s, ows=ows, lo=lo, j=j: s.dma_start(
                    out=owdram.t[j * 128:(j + 1) * 128, lo:lo + H],
                    in_=ows[:, :]),
                    reads=[ows], writes=[], sem=f"dow{hh}", dma=True)

            for hh in range(2):
                A1(hh)
                D1(hh)
                DCLIP(HV[hh]["W"][0])
                A2(hh)
            for hh in range(2):
                DCLIP(HV[hh]["W"][3])
                A3(hh)
            D4(0)
            DSCAN(0, 0)
            DSCAN(0, 1)
            D4(1)
            D7(0)
            D8(0)
            D9(0)
            A4(0)
            DSCAN(1, 0)
            DSCAN(1, 1)
            DCARRY(0)
            DCARRY(1)
            A5(0)
            if ins1 is not None:
                ins1()
            D7(1)
            D8(1)
            D9(1)
            A4(1)
            D10(0)
            D11(0)
            A5(1)
            if ins2 is not None:
                ins2()
            D10(1)
            D11(1)
            if ins3 is not None:
                ins3()

        def emit_gate(j):
            grp, s_in = j // 4, j % 4
            bb = 0
            pG = next_ps()
            pU = next_ps()
            sl = slice(s_in * 128, (s_in + 1) * 128)

            def gmm(tt, pG=pG, bb=bb, sl=sl):
                last = None
                for c in range(NCH):
                    for q in range(4):
                        last = tt.matmul(
                            pG[:, q * 512:(q + 1) * 512],
                            GT["g"][bb][c][:, sl],
                            xqF[c][:, q * 512:(q + 1) * 512],
                            start=(c == 0), stop=(c == NCH - 1))
                return last

            def umm(tt, pU=pU, bb=bb, sl=sl):
                last = None
                for c in range(NCH):
                    for q in range(4):
                        last = tt.matmul(
                            pU[:, q * 512:(q + 1) * 512],
                            GT["u"][bb][c][:, sl],
                            xqF[c][:, q * 512:(q + 1) * 512],
                            start=(c == 0), stop=(c == NCH - 1))
                return last
            b.emit("t", gmm, reads=[GT["g"][bb][c] for c in range(NCH)] + xqF,
                   writes=[pG])
            b.emit("t", umm, reads=[GT["u"][bb][c] for c in range(NCH)] + xqF,
                   writes=[pU])
            for hh in range(2):
                sg = SGH[hh]
                lo = hh * H
                b.emit("s", lambda s, pG=pG, lo=lo, sg=sg: s.activation(
                    sg[:, :], pG[:, lo:lo + H], AF.Silu, scale=sx_col[:, 0:1]),
                    reads=[pG, sx_col], writes=[sg])
                b.emit("v", lambda v, pU=pU, lo=lo, sg=sg:
                       v.scalar_tensor_tensor(
                           HST[:, lo:lo + H], pU[:, lo:lo + H], sx_col[:, 0:1],
                           sg[:, :], ALU.mult, ALU.mult),
                       reads=[pU, sg, sx_col], writes=[HST])
            b.emit("v", lambda v: v.reduce_max(
                hmx1[:, :], HST[:, :], axis=AX, apply_absolute_value=True),
                reads=[HST], writes=[hmx1])
            b.emit("v", lambda v: v.tensor_max(hmax[:, :], hmax[:, :], hmx1[:, :]),
                   reads=[hmx1, hmax], writes=[hmax])
            b.emit("s", lambda s, j=j: s.dma_start(
                out=h_d.t[j * 128:(j + 1) * 128, :], in_=HST[:, :]),
                reads=[HST], writes=[], sem=f"dho{j % 2}", dma=True)
            if s_in == 3 and grp + 1 <= (NIC - 1) // 4:
                load_gt_group(grp + 1)

        gate_pos = [0]

        def emit_gates(nmax):
            for _ in range(nmax):
                jg = gate_pos[0]
                if jg >= NIC:
                    return
                gate_pos[0] += 1
                emit_gate(jg)

        def ins3_fn(n):
            def f():
                if n < NCH:
                    pre_r(n)
                    if n + 1 < NCH:
                        load_wtj(n + 1)
                if n >= 2:
                    emit_gates(2)
            return f

        pre_k(0)
        pre_v(0)
        pre_r(0)
        for j in range(NCH):
            n = j + 1
            emit_middle(j,
                        ins1=(lambda n=n: pre_k(n)) if n < NCH else None,
                        ins2=(lambda n=n: pre_v(n)) if n < NCH else None,
                        ins3=ins3_fn(n))

        WoT = [b.at(R4, 16384 + c * 2048, f"WoT{c}", [128, C], BF16)
               for c in range(NCH)]
        b.handoff(WoT, xnF)
        for c in range(NCH):
            b.emit("y", lambda s, c=c: s.dma_start(
                out=WoT[c][:, :], in_=wot_d.t[c]),
                reads=[wot_d], writes=[WoT[c]], sem=f"dwt_o{c % 2}", dma=True)
        wotk = [b.tok("dwt_o0"), b.tok("dwt_o1")]
        for c in range(NCH):
            WoT[c].w = wotk[0]
            WoT[c].extra = [wotk[1]]

        XR = [b.at(R2, i * 4096, f"xr{i}", [128, C], F32) for i in range(2)]
        AST = [b.at(R2, 8192 + i * 4096, f"ast{i}", [128, C], F32)
               for i in range(2)]
        b.handoff(XR + AST, S)

        owtk = [b.tok("dow0"), b.tok("dow1")]
        for k in range(NTT):
            pW = next_ps()
            ast = AST[k % 2]
            x = XR[k % 2]
            ow = OWL[k % 2]
            b.emit("g", lambda g, k=k, x=x: g.dma_start(
                out=x[:, :], in_=x_d.t[k * 128:(k + 1) * 128, :]),
                reads=[], writes=[x], sem=f"dx{k % 2}", dma=True)
            b.emit("g", lambda g, k=k, ow=ow: g.dma_start(
                out=ow.t.ap().rearrange("p (c t) -> p c t", c=NCH),
                in_=owdram.t[0:C, k * 128:(k + 1) * 128].rearrange(
                    "(c p) t -> p c t", p=128)),
                reads=[], writes=[ow], sem=f"dowl{k % 2}", dma=True,
                extra_waits=owtk)
            b.emit("v", lambda v, x=x: v.tensor_add(
                x[:, :], x[:, :], bo_bc[:, :]),
                reads=[bo_bc, x], writes=[x])

            def womm(tt, pW=pW, k=k, ow=ow):
                last = None
                for c in range(NCH):
                    for q in range(2):
                        last = tt.matmul(
                            pW[:, q * 512:(q + 1) * 512],
                            ow[:, c * 128:(c + 1) * 128],
                            WoT[c][:, q * 512:(q + 1) * 512],
                            start=(c == 0), stop=(c == NCH - 1))
                return last
            b.emit("t", womm, reads=[ow] + WoT, writes=[pW])
            b.emit("v", lambda v, pW=pW, ast=ast, x=x: v.tensor_add(
                ast[:, :], pW[:, 0:C], x[:, :]),
                reads=[pW, x], writes=[ast])
            b.emit("s", lambda s, k=k, ast=ast: s.dma_start(
                out=acc_d.t[k * 128:(k + 1) * 128, :], in_=ast[:, :]),
                reads=[ast], writes=[], sem=f"dacc{k % 2}", dma=True)
            if k % 2 == 1:
                emit_gates(1)
        emit_gates(NIC)

        b.emit("y", lambda s: s.dma_start(out=colscr.t[:], in_=hmax[:, 0:1]),
               reads=[hmax], writes=[colscr], sem="dcs2", dma=True)
        b.emit("y", lambda s: s.dma_start(out=row[0:1, :], in_=colscr.t[:]),
               reads=[colscr], writes=[row], sem="drow2", dma=True)
        b.emit("v", lambda v: v.reduce_max(one1[0:1, 0:1], row[0:1, :], axis=AX),
               reads=[row], writes=[one1])
        b.emit("g", lambda g: g.dma_start(out=lm2_d.t[0:1, 0:1], in_=one1[0:1, 0:1]),
               reads=[one1], writes=[lm2_d], sem="dlm2", dma=True)

        b.final_wait("g", [b.tok("dacc0"), b.tok("dacc1"), b.tok("dho0"),
                           b.tok("dho1"), b.tok("dlm2")])
        block = ctx.enter_context(nc.Block())
        b.build(block)
    return nc


def build_l3():
    nc = bass.Bass("TRN2", target_bir_lowering=False)
    ctx = contextlib.ExitStack()
    with ctx:
        b = Bld(nc, ctx)
        D = lambda n, sh, dt, k="ExternalInput": Buf(nc.dram_tensor(n, sh, dt, kind=k), n)
        h_d = D("h", [I, T], F16)
        acc_d = D("acc", [T, C], F32)
        wdq_dram = D("wdq", [C, I], BF16)
        lm2_d = D("lm2", [1, 8], F32)
        clipd_d = D("clipd", [1], F32)
        out_d = D("out", [T, C], F32, "ExternalOutput")
        scaldr = b.dram("scaldr", [1, 2], F32)

        qh = [b.sbuf(f"qh{j}", [128, T], BF16) for j in range(NIC)]
        wdT = [b.sbuf(f"wdT{j}", [128, C], BF16) for j in range(NIC)]
        hst = [b.sbuf(f"hld{i}", [128, T], F16) for i in range(2)]
        accr = [b.sbuf(f"accr{i}", [128, C], F32) for i in range(2)]
        ost = [b.sbuf(f"ost{i}", [128, C], F32) for i in range(2)]
        t32 = [b.sbuf(f"t32_{i}", [128, T], F32) for i in range(2)]
        s2_col = b.sbuf("s2_col", [128, 1], F32)
        is2_col = b.sbuf("is2_col", [128, 1], F32)
        one1 = b.sbuf("one1", [1, 1], F32)
        one2 = b.sbuf("one2", [1, 1], F32)
        lmrow = b.sbuf("lmrow", [1, 8], F32)
        sc2 = b.sbuf("sc2", [1, 2], F32)
        magic_col = b.sbuf("magic_col", [128, 1], F32)
        ps = [b.psum(f"ps{i}", [128, C], F32) for i in range(4)]
        b.emit("g", lambda g: g.memset(magic_col[:, :], MAGIC), writes=[magic_col])

        b.emit("g", lambda g: g.dma_start(out=lmrow[0:1, :], in_=lm2_d.t[0:1, :]),
               reads=[], writes=[lmrow], sem="dlm2", dma=True)
        b.emit("g", lambda g: g.dma_start(out=one1[0:1, 0:1], in_=clipd_d.t[0:1]),
               reads=[], writes=[one1], sem="dcd", dma=True)
        b.emit("v", lambda v: v.reduce_max(one2[0:1, 0:1], lmrow[0:1, :], axis=AX),
               reads=[lmrow], writes=[one2])
        b.emit("s", lambda s: s.activation(one1[0:1, 0:1], one1[0:1, 0:1], AF.Sigmoid),
               reads=[one1], writes=[one1])
        b.emit("v", lambda v: v.tensor_mul(sc2[0:1, 0:1], one1[0:1, 0:1],
                                           one2[0:1, 0:1]),
               reads=[one1, one2], writes=[sc2])
        b.emit("v", lambda v: v.tensor_scalar_mul(sc2[0:1, 0:1], sc2[0:1, 0:1],
                                                  1.0 / QMAX),
               reads=[sc2], writes=[sc2])
        b.emit("v", lambda v: v.reciprocal(sc2[0:1, 1:2], sc2[0:1, 0:1]),
               reads=[sc2], writes=[sc2])
        b.emit("y", lambda s: s.dma_start(out=scaldr.t[0:1, :], in_=sc2[0:1, :]),
               reads=[sc2], writes=[scaldr], sem="dscw", dma=True)
        b.emit("y", lambda s: s.dma_start(
            out=s2_col[:, 0:1], in_=bass.AP(scaldr.t, 0, [[0, 128], [1, 1]])),
            reads=[scaldr], writes=[s2_col], sem="ds2", dma=True)
        b.emit("y", lambda s: s.dma_start(
            out=is2_col[:, 0:1], in_=bass.AP(scaldr.t, 1, [[0, 128], [1, 1]])),
            reads=[scaldr], writes=[is2_col], sem="dis2", dma=True)

        for j in range(NIC):
            b.emit("y", lambda s, j=j: s.dma_start_transpose(
                wdT[j][:, :], wdq_dram.t[0:C, j * 128:(j + 1) * 128]),
                reads=[wdq_dram], writes=[wdT[j]], sem=f"dwdT{j % 2}", dma=True)
        wtk = [b.tok("dwdT0"), b.tok("dwdT1")]
        for j in range(NIC):
            wdT[j].w = wtk[0]
            wdT[j].extra = [wtk[1]]

        for j in range(NIC):
            hl = hst[j % 2]
            tl = t32[j % 2]
            b.emit("g", lambda g, j=j, hl=hl: g.dma_start(
                out=hl[:, :], in_=h_d.t[j * 128:(j + 1) * 128, :]),
                reads=[], writes=[hl], sem=f"dh{j % 2}", dma=True)
            b.emit("s", lambda s, hl=hl, tl=tl: s.activation(
                tl[:, :], hl[:, :], AF.Identity, bias=magic_col[:, 0:1],
                scale=is2_col[:, 0:1]),
                reads=[hl, is2_col, magic_col], writes=[tl])
            b.emit("v", lambda v, tl=tl: v.tensor_scalar(
                tl[:, :], tl[:, :], MAGIC, -(QMAX + 1.0), ALU.subtract, ALU.max),
                reads=[tl], writes=[tl])
            b.emit("v", lambda v, j=j, tl=tl: v.tensor_scalar_min(
                qh[j][:, :], tl[:, :], QMAX),
                reads=[tl], writes=[qh[j]])

        for k in range(NTT):
            pD = ps[k % 4]
            ac = accr[k % 2]
            os_ = ost[k % 2]
            b.emit("g", lambda g, k=k, ac=ac: g.dma_start(
                out=ac[:, :], in_=acc_d.t[k * 128:(k + 1) * 128, :]),
                reads=[], writes=[ac], sem=f"dac{k % 2}", dma=True)

            def dmm(tt, pD=pD, k=k):
                last = None
                for j in range(NIC):
                    for q in range(2):
                        last = tt.matmul(
                            pD[:, q * 512:(q + 1) * 512],
                            qh[j][:, k * 128:(k + 1) * 128],
                            wdT[j][:, q * 512:(q + 1) * 512],
                            start=(j == 0), stop=(j == NIC - 1))
                return last
            b.emit("t", dmm, reads=qh + wdT, writes=[pD])
            b.emit("v", lambda v, pD=pD, ac=ac, os_=os_: v.scalar_tensor_tensor(
                os_[:, :], pD[:, 0:C], s2_col[:, 0:1], ac[:, :],
                ALU.mult, ALU.add),
                reads=[pD, ac, s2_col], writes=[os_])
            b.emit("g", lambda g, k=k, os_=os_: g.dma_start(
                out=out_d.t[k * 128:(k + 1) * 128, :], in_=os_[:, :]),
                reads=[os_], writes=[], sem=f"dout{k % 2}", dma=True)

        b.final_wait("g", [b.tok("dout0"), b.tok("dout1")])
        block = ctx.enter_context(nc.Block())
        b.build(block)
    return nc


def kernel(L0_attn_x, L0_attn_Wr, L0_attn_br, L0_attn_Wk, L0_attn_bk,
           L0_attn_Wv, L0_attn_bv, L0_attn_Wo, L0_attn_bo, L0_attn_time_decay,
           L0_attn_U, L0_ffn_clip_x, L0_ffn_clip_down, L0_ffn_w_gate,
           L0_ffn_w_up, L0_ffn_w_down):
    n = 8
    x = np.ascontiguousarray(np.asarray(L0_attn_x, dtype=np.float32))
    xs = [np.ascontiguousarray(x[i]) for i in range(n)]
    f32 = lambda a: np.ascontiguousarray(np.asarray(a, dtype=np.float32))
    Wr, Wk, Wv, Wo = f32(L0_attn_Wr), f32(L0_attn_Wk), f32(L0_attn_Wv), f32(L0_attn_Wo)
    br, bk, bv, bo = f32(L0_attn_br), f32(L0_attn_bk), f32(L0_attn_bv), f32(L0_attn_bo)
    td, Uv = f32(L0_attn_time_decay), f32(L0_attn_U)
    clipx, clipd = f32(L0_ffn_clip_x), f32(L0_ffn_clip_down)
    wg, wu, wd = f32(L0_ffn_w_gate), f32(L0_ffn_w_up), f32(L0_ffn_w_down)

    cores = list(range(n))
    gstart = [0, 384, 768, 1152, 1536, 1920, 2304, 2560]
    grows = [384] * 6 + [256] * 2

    def wslice(w, i):
        blk = w[gstart[i]:gstart[i] + grows[i]]
        if grows[i] < 384:
            blk = np.concatenate([blk, blk[:384 - grows[i]]], axis=0)
        return np.ascontiguousarray(blk)

    nc1 = build_l1()
    in1 = [{"x": xs[i], "wg": wslice(wg, i), "wu": wslice(wu, i),
            "wd": np.ascontiguousarray(wd[i * 128:(i + 1) * 128]),
            "Wr": np.ascontiguousarray(Wr[i * 128:(i + 1) * 128]),
            "Wk": np.ascontiguousarray(Wk[i * 128:(i + 1) * 128]),
            "Wv": np.ascontiguousarray(Wv[i * 128:(i + 1) * 128]),
            "Wo": np.ascontiguousarray(Wo[i * 128:(i + 1) * 128])}
           for i in range(n)]
    r1 = run_bass_kernel_spmd(nc1, in1, core_ids=cores)
    lm1 = np.concatenate(
        [np.asarray(r1.results[i]["lm1"]).reshape(1, 1) for i in range(n)], axis=1)
    xnf = [np.asarray(r1.results[i]["xnf"]) for i in range(n)]
    xnb = [np.asarray(r1.results[i]["xnb"]) for i in range(n)]
    wgq = np.concatenate(
        [np.asarray(r1.results[i]["wgq"])[:grows[i]] for i in range(n)], axis=0)
    wuq = np.concatenate(
        [np.asarray(r1.results[i]["wuq"])[:grows[i]] for i in range(n)], axis=0)
    wdq = np.concatenate(
        [np.asarray(r1.results[i]["wdq"]) for i in range(n)], axis=0)
    wtt = np.ascontiguousarray(
        np.stack([np.asarray(r1.results[i]["wtt"]) for i in range(n)], axis=0))
    wo_j = np.stack([np.asarray(r1.results[i]["wot"]).reshape(128, 8, 128)
                     for i in range(n)], axis=0)
    wot = np.ascontiguousarray(
        wo_j.transpose(2, 1, 0, 3).reshape(8, 128, 1024))

    nc2 = build_l2()
    in2 = [{"x": xs[i], "xnf": xnf[i], "xnb": xnb[i], "lm1": lm1,
            "wtt": wtt, "wot": wot,
            "br": br, "bk": bk, "bv": bv, "bo": bo,
            "td": td, "Uv": Uv, "clipx": clipx, "wgq": wgq, "wuq": wuq}
           for i in range(n)]
    r2 = run_bass_kernel_spmd(nc2, in2, core_ids=cores)
    lm2 = np.concatenate(
        [np.asarray(r2.results[i]["lm2"]).reshape(1, 1) for i in range(n)], axis=1)
    acc = [np.asarray(r2.results[i]["acc"]) for i in range(n)]
    hh = [np.asarray(r2.results[i]["h"]) for i in range(n)]

    nc3 = build_l3()
    in3 = [{"h": hh[i], "acc": acc[i], "wdq": wdq, "lm2": lm2, "clipd": clipd}
           for i in range(n)]
    r3 = run_bass_kernel_spmd(nc3, in3, core_ids=cores)
    out = np.stack([np.asarray(r3.results[i]["out"]) for i in range(n)], axis=0)
    return out.astype(np.float32)


# version 18
# speedup vs baseline: 1.0939x; 1.0143x over previous
import math
import contextlib
import numpy as np
import ml_dtypes
import concourse.bass as bass
import concourse.mybir as mybir
from concourse.bass_utils import run_bass_kernel_spmd

F32 = mybir.dt.float32
BF16 = mybir.dt.bfloat16
F16 = mybir.dt.float16
U8 = mybir.dt.uint8
AF = mybir.ActivationFunctionType
ALU = mybir.AluOpType
AX = mybir.AxisListType.X

MAGIC = 12582912.0
T, C, I = 2048, 1024, 2816
NCH = C // 128
NIC = I // 128
NTT = T // 128
H = T // 2
_EPS = math.exp(-1.0 / 12.0)
EPS_R = _EPS / 2.0
EPS_K = _EPS + 0.3 / 11.0
EPS_V = _EPS
QMAX = 127.0


_ENG_ATTR = {"v": "vector", "s": "scalar", "t": "tensor", "g": "gpsimd", "y": "sync"}


class Buf:
    __slots__ = ("t", "w", "rs", "extra", "name")

    def __init__(self, t, name=""):
        self.t = t
        self.w = None
        self.rs = []
        self.extra = []
        self.name = name

    def __getitem__(self, idx):
        return self.t[idx]


class Bld:
    def __init__(self, nc, ctx):
        self.nc = nc
        self.ctx = ctx
        self.prog = []
        self.sems = {}
        self.waited = {}
        self.dma_sems = set()

    def sbuf(self, name, shape, dtype):
        return Buf(self.ctx.enter_context(self.nc.sbuf_tensor(name, shape, dtype)), name)

    def psum(self, name, shape, dtype):
        return Buf(self.ctx.enter_context(self.nc.psum_tensor(name, shape, dtype)), name)

    def dram(self, name, shape, dtype, kind="Internal"):
        return Buf(self.nc.dram_tensor(name, shape, dtype, kind=kind), name)

    def region(self, name, nbytes):
        t = self.ctx.enter_context(self.nc.sbuf_tensor(name, [128, nbytes], U8))
        return self.nc.lookup_mloc(t).addr

    def at(self, addr, rel, name, shape, dtype):
        t = self.nc.alloc_sbuf_tensor_at(name, shape, dtype, offset=addr + rel)
        return Buf(t, name)

    def handoff(self, new_bufs, old_bufs):
        toks = []
        for b in old_bufs:
            if b.w:
                toks.append(b.w)
            toks.extend(b.rs)
            toks.extend(b.extra)
        for nb in new_bufs:
            nb.extra = list(toks)

    def _sem(self, key):
        if key not in self.sems:
            h = self.ctx.enter_context(self.nc.semaphore(key[:30]))
            self.sems[key] = [h, 0]
        return self.sems[key]

    def emit(self, eng, fn, reads=(), writes=(), sem=None, dma=False, n_dma=1,
             extra_waits=()):
        if dma:
            assert sem is not None
        sem_key = sem or ("c" + eng)
        s = self._sem(sem_key)
        amount = 16 * n_dma if dma else 1
        toks = list(extra_waits)
        for b in reads:
            if b.w:
                toks.append(b.w)
            toks.extend(b.extra)
        for b in writes:
            if b.w:
                toks.append(b.w)
            toks.extend(b.rs)
            toks.extend(b.extra)
        if dma:
            toks.append((sem_key, s[1]))
            self.dma_sems.add(sem_key)
        wd = self.waited.setdefault(eng, {})
        final = {}
        for (sk, v) in toks:
            if v > wd.get(sk, 0) and v > final.get(sk, 0):
                final[sk] = v
        s[1] += amount
        tok = (sem_key, s[1])
        waits = sorted(final.items())

        def closure(e, waits=waits, semh=s[0], amt=amount, fn=fn):
            for sk, v in waits:
                e.wait_ge(self.sems[sk][0], v)
            fn(e).then_inc(semh, amt)

        self.prog.append((eng, closure))
        for sk, v in final.items():
            wd[sk] = v
        for b in reads:
            b.rs.append(tok)
        for b in writes:
            b.w = tok
            b.rs = []
            b.extra = []
        return tok

    def tok(self, sem_key):
        return (sem_key, self._sem(sem_key)[1])

    def settle(self, bufs, tok):
        for b in bufs:
            b.w = tok
            b.rs = []
            b.extra = []

    def final_wait(self, eng, toks=None):
        toks = [self.tok(k) for k in sorted(self.dma_sems)]
        def closure(e, toks=toks):
            for sk, v in toks:
                e.wait_ge(self.sems[sk][0], v)
        self.prog.append((eng, closure))

    def build(self, block):
        per = {k: [] for k in _ENG_ATTR}
        for eng, closure in self.prog:
            per[eng].append(closure)
        for key, closures in per.items():
            if not closures:
                continue

            def body(e, closures=closures):
                for c in closures:
                    c(e)

            getattr(block, _ENG_ATTR[key])(body)


def build_l1():
    nc = bass.Bass("TRN2", target_bir_lowering=False)
    ctx = contextlib.ExitStack()
    with ctx:
        b = Bld(nc, ctx)
        x_d = Buf(nc.dram_tensor("x", [T, C], F32, kind="ExternalInput"), "x")
        wg_d = Buf(nc.dram_tensor("wg", [384, C], F32, kind="ExternalInput"), "wg")
        wu_d = Buf(nc.dram_tensor("wu", [384, C], F32, kind="ExternalInput"), "wu")
        wd_d = Buf(nc.dram_tensor("wd", [128, I], F32, kind="ExternalInput"), "wd")
        xnf_d = Buf(nc.dram_tensor("xnf", [T, C], F32, kind="ExternalOutput"), "xnf")
        xnb_d = Buf(nc.dram_tensor("xnb", [T, C], BF16, kind="ExternalOutput"), "xnb")
        lm1_d = Buf(nc.dram_tensor("lm1", [1, 1], F32, kind="ExternalOutput"), "lm1")
        wgq_d = Buf(nc.dram_tensor("wgqt", [3, NCH, 128, 128], BF16,
                                   kind="ExternalOutput"), "wgqt")
        wuq_d = Buf(nc.dram_tensor("wuqt", [3, NCH, 128, 128], BF16,
                                   kind="ExternalOutput"), "wuqt")
        wdq_d = Buf(nc.dram_tensor("wdqt", [NIC, 128, 128], BF16,
                                   kind="ExternalOutput"), "wdqt")
        aw_in = {p: Buf(nc.dram_tensor(f"W{p}", [128, C], F32, kind="ExternalInput"),
                        f"W{p}") for p in "rkvo"}
        wtt_d = Buf(nc.dram_tensor("wtt", [128, 3072], BF16,
                                   kind="ExternalOutput"), "wtt")
        wot_d = Buf(nc.dram_tensor("wot", [128, 1024], BF16,
                                   kind="ExternalOutput"), "wot")
        colscr = b.dram("colscr", [128], F32)

        xt = [b.sbuf(f"xt{i}", [128, C], F32) for i in range(2)]
        xnf = [b.sbuf(f"xnf{i}", [128, C], F32) for i in range(2)]
        xnb = [b.sbuf(f"xnb{i}", [128, C], BF16) for i in range(2)]
        sq = [b.sbuf(f"sq{i}", [128, C], F32) for i in range(2)]
        ss = [b.sbuf(f"ss{i}", [128, 1], F32) for i in range(2)]
        sm = [b.sbuf(f"sm{i}", [128, 1], F32) for i in range(2)]
        mcol = [b.sbuf(f"mcol{i}", [128, 1], F32) for i in range(2)]
        vcol = [b.sbuf(f"vcol{i}", [128, 1], F32) for i in range(2)]
        sdcol = [b.sbuf(f"sdcol{i}", [128, 1], F32) for i in range(2)]
        rstd = [b.sbuf(f"rstd{i}", [128, 1], F32) for i in range(2)]
        nmr = [b.sbuf(f"nmr{i}", [128, 1], F32) for i in range(2)]
        mxc = b.sbuf("mxc", [128, 1], F32)
        mxacc = b.sbuf("mxacc", [128, 1], F32)
        row = b.sbuf("rowb", [1, 128], F32)
        one1 = b.sbuf("one1", [1, 1], F32)
        magic_col = b.sbuf("magic_col", [128, 1], F32)
        NPAR = 4
        gwst = [b.sbuf(f"gwst{i}", [128, I], F32) for i in range(NPAR)]
        gqo = [b.sbuf(f"gqo{i}", [128, I], BF16) for i in range(NPAR)]
        rmaxc = [b.sbuf(f"rmaxc{i}", [128, 1], F32) for i in range(NPAR)]
        rscl = [b.sbuf(f"rscl{i}", [128, 1], F32) for i in range(NPAR)]
        rinv = [b.sbuf(f"rinv{i}", [128, 1], F32) for i in range(NPAR)]

        b.emit("g", lambda g: g.memset(mxacc[:, :], 0.0), writes=[mxacc])
        b.emit("g", lambda g: g.memset(magic_col[:, :], MAGIC), writes=[magic_col])
        awst = [b.sbuf(f"awst{i}", [128, C], BF16) for i in range(2)]
        ident = b.sbuf("ident", [128, 128], BF16)
        iota_a = b.sbuf("iota_a", [128, 128], F32)
        iota_b = b.sbuf("iota_b", [128, 128], F32)
        ttile = b.sbuf("ttile", [128, 3072], BF16)
        otile = b.sbuf("otile", [128, 1024], BF16)
        pt = [b.psum(f"pt{i}", [128, 1024], BF16) for i in range(2)]
        ptd = b.psum("ptd", [128, I], BF16)
        tqo = [b.sbuf(f"tqo{i}", [128, I], BF16) for i in range(2)]
        b.emit("g", lambda g: g.iota(iota_a[:, :], [[0, 128]], base=0,
                                     channel_multiplier=1,
                                     allow_small_or_imprecise_dtypes=True),
               writes=[iota_a])
        b.emit("g", lambda g: g.iota(iota_b[:, :], [[1, 128]], base=0,
                                     channel_multiplier=0,
                                     allow_small_or_imprecise_dtypes=True),
               writes=[iota_b])
        b.emit("v", lambda v: v.tensor_tensor(ident[:, :], iota_a[:, :],
                                              iota_b[:, :], ALU.is_equal),
               reads=[iota_a, iota_b], writes=[ident])
        for pi, p in enumerate("rkvo"):
            st = awst[pi % 2]
            pp = pt[pi % 2]
            dst = otile if p == "o" else ttile
            off = 0 if p == "o" else {"r": 0, "k": 1, "v": 2}[p] * 1024
            b.emit("g", lambda g, p=p, st=st: g.dma_start(
                out=st[:, :], in_=aw_in[p].t[0:128, :]),
                reads=[], writes=[st], sem=f"dawi{pi % 2}", dma=True)

            def trans(tt, st=st, pp=pp):
                last = None
                for c in range(NCH):
                    last = tt.transpose(pp[:, c * 128:(c + 1) * 128],
                                        st[:, c * 128:(c + 1) * 128],
                                        ident[:, :])
                return last
            b.emit("t", trans, reads=[st, ident], writes=[pp])
            b.emit("s", lambda s, pp=pp, dst=dst, off=off: s.activation(
                dst[:, off:off + 1024], pp[:, :], AF.Copy),
                reads=[pp], writes=[dst])
        b.emit("y", lambda s: s.dma_start(out=wtt_d.t[0:128, :],
                                          in_=ttile[:, :]),
               reads=[ttile], writes=[], sem="dwtt", dma=True)
        b.emit("y", lambda s: s.dma_start(out=wot_d.t[0:128, :],
                                          in_=otile[:, :]),
               reads=[otile], writes=[], sem="dwot", dma=True)

        _unit_ctr = [0]

        def make_wq_unit(wsrc, wdst, c, ncols, sem_pref):
            def unit():
                par = _unit_ctr[0] % NPAR
                _unit_ctr[0] += 1
                st = gwst[par]
                qo = gqo[par]
                rm, rs, ri = rmaxc[par], rscl[par], rinv[par]
                b.emit("g", lambda g: g.dma_start(
                    out=st[:, 0:ncols], in_=wsrc.t[c * 128:(c + 1) * 128, :]),
                    reads=[], writes=[st], sem=f"dgw{par}", dma=True)
                b.emit("v", lambda v: v.reduce_max(
                    rm[:, :], st[:, 0:ncols], axis=AX, apply_absolute_value=True),
                    reads=[st], writes=[rm])
                b.emit("v", lambda v: v.tensor_scalar_mul(
                    rs[:, :], rm[:, :], 1.0 / QMAX),
                    reads=[rm], writes=[rs])
                b.emit("v", lambda v: v.reciprocal(ri[:, :], rs[:, :]),
                       reads=[rs], writes=[ri])
                b.emit("s", lambda s: s.activation(
                    st[:, 0:ncols], st[:, 0:ncols], AF.Identity,
                    bias=magic_col[:, 0:1], scale=ri[:, 0:1]),
                    reads=[st, ri, magic_col], writes=[st])
                b.emit("v", lambda v: v.tensor_scalar(
                    st[:, 0:ncols], st[:, 0:ncols], MAGIC, -(QMAX + 1.0),
                    ALU.subtract, ALU.max),
                    reads=[st], writes=[st])
                b.emit("v", lambda v: v.tensor_scalar(
                    qo[:, 0:ncols], st[:, 0:ncols], QMAX, rs[:, 0:1],
                    ALU.min, ALU.mult),
                    reads=[st, rs], writes=[qo])
                nt = ncols // 128
                pp = ptd if ncols == I else pt[par % 2]
                tq = tqo[par % 2]

                def trq(tt, qo=qo, pp=pp, nt=nt):
                    last = None
                    for c2 in range(nt):
                        last = tt.transpose(pp[:, c2 * 128:(c2 + 1) * 128],
                                            qo[:, c2 * 128:(c2 + 1) * 128],
                                            ident[:, :])
                    return last
                b.emit("t", trq, reads=[qo, ident], writes=[pp])
                b.emit("s", lambda s, pp=pp, tq=tq, ncols=ncols: s.activation(
                    tq[:, 0:ncols], pp[:, 0:ncols], AF.Copy),
                    reads=[pp], writes=[tq])
                b.emit("y", lambda s, tq=tq, nt=nt, ncols=ncols: s.dma_start(
                    out=wdst.t[c].rearrange("c2 p t -> p c2 t") if ncols == C
                    else wdst.t.ap().rearrange("c2 p t -> p c2 t"),
                    in_=tq[:, 0:ncols].rearrange("p (c2 t) -> p c2 t", t=128)),
                    reads=[tq], writes=[], sem=sem_pref + str(par % 2), dma=True)
            return unit

        wq_units = []
        for c in range(3):
            wq_units.append(make_wq_unit(wg_d, wgq_d, c, C, "dwgq"))
            wq_units.append(make_wq_unit(wu_d, wuq_d, c, C, "dwuq"))
        wq_units.append(make_wq_unit(wd_d, wdq_d, 0, I, "dwdq"))
        wq_iter = iter(wq_units)
        for k in range(NTT):
            pp = k % 2
            x = xt[pp]
            xf = xnf[pp]
            xb = xnb[pp]
            sq_, ss_, sm_ = sq[pp], ss[pp], sm[pp]
            mc, vc, sd_, rs_, nm_ = mcol[pp], vcol[pp], sdcol[pp], rstd[pp], nmr[pp]
            b.emit("g", lambda g, k=k, x=x: g.dma_start(
                out=x[:, :], in_=x_d.t[k * 128:(k + 1) * 128, :]),
                reads=[], writes=[x], sem=f"dx{k % 2}", dma=True)
            b.emit("s", lambda s, x=x, sq_=sq_, ss_=ss_: s.activation(
                sq_[:, :], x[:, :], AF.Square, accum_out=ss_[:, 0:1]),
                reads=[x], writes=[sq_, ss_])
            b.emit("v", lambda v, x=x, sm_=sm_: v.reduce_sum(
                sm_[:, :], x[:, :], axis=AX),
                reads=[x], writes=[sm_])
            b.emit("v", lambda v, sm_=sm_, mc=mc: v.tensor_scalar_mul(
                mc[:, :], sm_[:, :], 1.0 / C),
                reads=[sm_], writes=[mc])
            b.emit("v", lambda v, ss_=ss_, vc=vc: v.tensor_scalar_mul(
                vc[:, :], ss_[:, :], 1.0 / C),
                reads=[ss_], writes=[vc])
            b.emit("v", lambda v, mc=mc, sq_=sq_: v.scalar_tensor_tensor(
                sq_[:, 0:1], mc[:, :], -1.0, mc[:, :], ALU.mult, ALU.mult),
                reads=[mc], writes=[sq_])
            b.emit("v", lambda v, vc=vc, sq_=sq_: v.tensor_add(
                vc[:, :], vc[:, :], sq_[:, 0:1]),
                reads=[sq_], writes=[vc])
            b.emit("v", lambda v, vc=vc: v.tensor_scalar_add(vc[:, :], vc[:, :], 1e-5),
                   reads=[vc], writes=[vc])
            b.emit("s", lambda s, vc=vc, sd_=sd_: s.activation(
                sd_[:, :], vc[:, :], AF.Sqrt),
                reads=[vc], writes=[sd_])
            b.emit("v", lambda v, sd_=sd_, rs_=rs_: v.reciprocal(
                rs_[:, :], sd_[:, :]),
                reads=[sd_], writes=[rs_])
            b.emit("v", lambda v, mc=mc, rs_=rs_, nm_=nm_: v.scalar_tensor_tensor(
                nm_[:, :], mc[:, :], -1.0, rs_[:, :], ALU.mult, ALU.mult),
                reads=[mc, rs_], writes=[nm_])
            b.emit("v", lambda v, x=x, xf=xf, rs_=rs_, nm_=nm_: v.tensor_scalar(
                xf[:, :], x[:, :], rs_[:, 0:1], nm_[:, 0:1], ALU.mult, ALU.add),
                reads=[x, rs_, nm_], writes=[xf])
            b.emit("v", lambda v, xf=xf: v.reduce_max(
                mxc[:, :], xf[:, :], axis=AX, apply_absolute_value=True),
                reads=[xf], writes=[mxc])
            b.emit("v", lambda v: v.tensor_max(mxacc[:, :], mxacc[:, :], mxc[:, :]),
                   reads=[mxc, mxacc], writes=[mxacc])
            b.emit("s", lambda s, xf=xf, xb=xb: s.activation(
                xb[:, :], xf[:, :], AF.Copy),
                reads=[xf], writes=[xb])
            b.emit("y", lambda s, k=k, xf=xf: s.dma_start(
                out=xnf_d.t[k * 128:(k + 1) * 128, :], in_=xf[:, :]),
                reads=[xf], writes=[], sem=f"doxf{k % 2}", dma=True)
            b.emit("y", lambda s, k=k, xb=xb: s.dma_start(
                out=xnb_d.t[k * 128:(k + 1) * 128, :], in_=xb[:, :]),
                reads=[xb], writes=[], sem=f"doxb{k % 2}", dma=True)
            u = next(wq_iter, None)
            if u is not None:
                u()
        for u in wq_iter:
            u()
        b.emit("y", lambda s: s.dma_start(out=colscr.t[:], in_=mxacc[:, 0:1]),
               reads=[mxacc], writes=[colscr], sem="dcs", dma=True)
        b.emit("y", lambda s: s.dma_start(out=row[0:1, :], in_=colscr.t[:]),
               reads=[colscr], writes=[row], sem="drow", dma=True)
        b.emit("v", lambda v: v.reduce_max(one1[0:1, 0:1], row[0:1, :], axis=AX),
               reads=[row], writes=[one1])
        b.emit("g", lambda g: g.dma_start(out=lm1_d.t[0:1, 0:1], in_=one1[0:1, 0:1]),
               reads=[one1], writes=[lm1_d], sem="dlm", dma=True)
        b.final_wait("g", [b.tok("doxf0"), b.tok("doxf1"), b.tok("doxb0"),
                           b.tok("doxb1"), b.tok("dlm"),
                           b.tok("dwgq0"), b.tok("dwgq1"), b.tok("dwuq0"),
                           b.tok("dwuq1"), b.tok("dwdq0"), b.tok("dwdq1")])
        block = ctx.enter_context(nc.Block())
        b.build(block)
    return nc


def build_l2():
    nc = bass.Bass("TRN2", target_bir_lowering=False)
    ctx = contextlib.ExitStack()
    with ctx:
        b = Bld(nc, ctx)
        D = lambda n, sh, dt, k="ExternalInput": Buf(nc.dram_tensor(n, sh, dt, kind=k), n)
        x_d = D("x", [T, C], F32)
        xnf_d = D("xnf", [T, C], F32)
        xnb_d = D("xnb", [T, C], BF16)
        lm1_d = D("lm1", [1, 8], F32)
        wtt_d = D("wtt", [NCH, 128, 3072], BF16)
        wot_d = D("wot", [NCH, 128, 1024], BF16)
        br_d, bk_d, bv_d, bo_d = (D(n, [C], F32) for n in ("br", "bk", "bv", "bo"))
        td_d, U_d = D("td", [C], F32), D("Uv", [C], F32)
        clipx_d = D("clipx", [1], F32)
        wgq_dram = D("wgqt", [6, 128, 4096], BF16)
        wuq_dram = D("wuqt", [6, 128, 4096], BF16)
        acc_d = D("acc", [T, C], F32, "ExternalOutput")
        h_d = D("h", [I, T], F16, "ExternalOutput")
        lm2_d = D("lm2", [1, 1], F32, "ExternalOutput")
        qdram = b.dram("qdram", [T, C], BF16)
        owdram = b.dram("owdram", [C, T], BF16)
        colscr = b.dram("colscr", [128], F32)
        scaldr = b.dram("scaldr", [1, 2], F32)

        R4 = b.region("ov4", NCH * 4096)
        xnF = [b.at(R4, c * 4096, f"xnF{c}", [128, T], BF16) for c in range(NCH)]
        xqF = [b.sbuf(f"xqF{c}", [128, T], BF16) for c in range(NCH)]
        ramp = b.sbuf("ramp", [128, T], F32)
        R3 = b.region("ov3", 16384)
        WTJ = [b.at(R3, jj * 6144, f"wtj{jj}", [128, 3072], BF16)
               for jj in range(2)]
        bo_bc = b.at(R3, 12288, "bo_bc", [128, C], F32)
        P_IDX = {"r": 0, "k": 1, "v": 2}
        GT = {w: b.sbuf(f"{w}T", [128, 4096], BF16) for w in ("g", "u")}
        OWS = [b.sbuf(f"ows{i}", [128, H], BF16) for i in range(2)]
        OWL = [b.sbuf(f"owl{i}", [128, C], BF16) for i in range(2)]
        HST = b.sbuf("hst0", [128, T], F16)
        SGH = [b.sbuf(f"sg{i}", [128, H], F32) for i in range(2)]

        cols = {n: b.sbuf(f"col_{n}", [128, NCH], F32)
                for n in ("td", "U", "bk", "br", "bv", "w", "ubk")}
        hmax = b.sbuf("hmaxc", [128, 1], F32)
        hmx1 = b.sbuf("hmx1", [128, 1], F32)
        sx_col = b.sbuf("sx_col", [128, 1], F32)
        isx_col = b.sbuf("isx_col", [128, 1], F32)
        magic_col = b.sbuf("magic_col", [128, 1], F32)
        nmagic_col = b.sbuf("nmagic_col", [128, 1], F32)
        ecols = {}
        for nm, val in (("k", EPS_K), ("v", EPS_V), ("r", EPS_R)):
            ecols[nm] = b.sbuf(f"eps_{nm}", [128, 1], F32)
        row = b.sbuf("rowb", [1, 128], F32)
        one1 = b.sbuf("one1", [1, 1], F32)
        one2 = b.sbuf("one2", [1, 1], F32)
        lmrow = b.sbuf("lmrow", [1, 8], F32)
        sc2 = b.sbuf("sc2", [1, 2], F32)

        NS = 16
        R2 = b.region("ov2", NS * 4096)
        S = [b.at(R2, i * 4096, f"scr{i}", [128, H], F32) for i in range(NS)]

        ps = [b.psum(f"ps{i}", [128, T], F32) for i in range(2)]

        GD, SY = "g", "y"

        b.emit("g", lambda g: g.memset(magic_col[:, :], MAGIC), writes=[magic_col])
        b.emit("g", lambda g: g.memset(nmagic_col[:, :], -MAGIC), writes=[nmagic_col])
        b.emit("g", lambda g: g.memset(hmax[:, :], 0.0), writes=[hmax])
        for nm, val in (("k", EPS_K), ("v", EPS_V), ("r", EPS_R)):
            b.emit("g", lambda g, nm=nm, val=val: g.memset(ecols[nm][:, :], val),
                   writes=[ecols[nm]])
        b.emit("g", lambda g: g.iota(ramp[:, :], [[1, T]], base=-(T - 1),
                                     channel_multiplier=0,
                                     allow_small_or_imprecise_dtypes=True),
               writes=[ramp])
        b.emit("y", lambda s: s.dma_start(
            out=bo_bc[:, :], in_=bass.AP(bo_d.t, 0, [[0, 128], [1, C]])),
            reads=[], writes=[bo_bc], sem="dbo", dma=True)
        for n, d in (("td", td_d), ("U", U_d), ("bk", bk_d), ("br", br_d),
                     ("bv", bv_d)):
            b.emit("g", lambda g, n=n, d=d: g.dma_start(
                out=cols[n][:, :], in_=d.t.ap().rearrange("(f p) -> p f", p=128),
                allow_slow_non_contiguous=True),
                reads=[], writes=[cols[n]], sem=f"dc_{n}", dma=True)
        b.emit("s", lambda s: s.activation(cols["w"][:, :], cols["td"][:, :], AF.Relu),
               reads=[cols["td"]], writes=[cols["w"]])
        b.emit("v", lambda v: v.tensor_add(cols["ubk"][:, :], cols["U"][:, :],
                                           cols["bk"][:, :]),
               reads=[cols["U"], cols["bk"]], writes=[cols["ubk"]])
        b.emit("g", lambda g: g.dma_start(out=lmrow[0:1, :], in_=lm1_d.t[0:1, :]),
               reads=[], writes=[lmrow], sem="dlm1", dma=True)
        b.emit("g", lambda g: g.dma_start(out=one1[0:1, 0:1], in_=clipx_d.t[0:1]),
               reads=[], writes=[one1], sem="dcx", dma=True)
        b.emit("v", lambda v: v.reduce_max(one2[0:1, 0:1], lmrow[0:1, :], axis=AX),
               reads=[lmrow], writes=[one2])
        b.emit("s", lambda s: s.activation(one1[0:1, 0:1], one1[0:1, 0:1], AF.Sigmoid),
               reads=[one1], writes=[one1])
        b.emit("v", lambda v: v.tensor_mul(sc2[0:1, 0:1], one1[0:1, 0:1],
                                           one2[0:1, 0:1]),
               reads=[one1, one2], writes=[sc2])
        b.emit("v", lambda v: v.tensor_scalar_mul(sc2[0:1, 0:1], sc2[0:1, 0:1],
                                                  1.0 / QMAX),
               reads=[sc2], writes=[sc2])
        b.emit("v", lambda v: v.reciprocal(sc2[0:1, 1:2], sc2[0:1, 0:1]),
               reads=[sc2], writes=[sc2])
        b.emit("y", lambda s: s.dma_start(out=scaldr.t[0:1, :], in_=sc2[0:1, :]),
               reads=[sc2], writes=[scaldr], sem="dscw", dma=True)
        b.emit("y", lambda s: s.dma_start(
            out=sx_col[:, 0:1], in_=bass.AP(scaldr.t, 0, [[0, 128], [1, 1]])),
            reads=[scaldr], writes=[sx_col], sem="dsx", dma=True)
        b.emit("y", lambda s: s.dma_start(
            out=isx_col[:, 0:1], in_=bass.AP(scaldr.t, 1, [[0, 128], [1, 1]])),
            reads=[scaldr], writes=[isx_col], sem="disx", dma=True)

        for c in range(NCH):
            b.emit("y", lambda s, c=c: s.dma_start_transpose(
                xnF[c][:, :], xnb_d.t[0:T, c * 128:(c + 1) * 128]),
                reads=[], writes=[xnF[c]], sem=f"dxnF{c % 4}", dma=True)
        xnF_toks = [b.tok(f"dxnF{i}") for i in range(4)]
        for c in range(NCH):
            xnF[c].w = xnF_toks[0]
            xnF[c].extra = xnF_toks[1:]

        def load_wtj(j):
            jj = j % 2
            b.emit("y", lambda s, j=j, jj=jj: s.dma_start(
                out=WTJ[jj][:, :], in_=wtt_d.t[j]),
                reads=[wtt_d], writes=[WTJ[jj]], sem=f"dwtj{j % 2}", dma=True)

        def load_gt_group(grp):
            for wname, wdram in (("g", wgq_dram), ("u", wuq_dram)):
                b.emit("y", lambda s, wname=wname, wdram=wdram, grp=grp:
                       s.dma_start(out=GT[wname][:, :], in_=wdram.t[grp]),
                       reads=[wdram], writes=[GT[wname]],
                       sem=f"dgt{wname}", dma=True)

        load_wtj(0)
        load_wtj(1)
        load_gt_group(0)

        def proj_mm(tt, pbuf, p, j):
            last = None
            base = P_IDX[p] * 1024
            for c in range(NCH):
                for q in range(4):
                    last = tt.matmul(
                        pbuf[:, q * 512:(q + 1) * 512],
                        WTJ[j % 2][:, base + c * 128:base + (c + 1) * 128],
                        xnF[c][:, q * 512:(q + 1) * 512],
                        start=(c == 0), stop=(c == NCH - 1))
            return last

        XF = [b.at(R2, (12 + i) * 4096, f"xf32_{i}", [128, C], F32)
              for i in range(2)]
        QS = [b.at(R2, 14 * 4096 + i * 2048, f"qs{i}", [128, C], BF16)
              for i in range(2)]

        for k in range(NTT):
            xf = XF[k % 2]
            q = QS[k % 2]
            ldq = "g" if k % 2 == 0 else "s"
            b.emit(ldq, lambda e, k=k, xf=xf: e.dma_start(
                out=xf[:, :], in_=xnf_d.t[k * 128:(k + 1) * 128, :]),
                reads=[], writes=[xf], sem=f"dxf{k % 2}", dma=True)
            b.emit("s", lambda s, xf=xf: s.activation(
                xf[:, :], xf[:, :], AF.Identity, bias=magic_col[:, 0:1],
                scale=isx_col[:, 0:1]),
                reads=[xf, isx_col, magic_col], writes=[xf])
            b.emit("s", lambda s, xf=xf: s.activation(
                xf[:, :], xf[:, :], AF.Identity, bias=nmagic_col[:, 0:1]),
                reads=[xf, nmagic_col], writes=[xf])
            b.emit("v", lambda v, xf=xf, q=q: v.tensor_scalar(
                q[:, :], xf[:, :], -(QMAX + 1.0), QMAX, ALU.max, ALU.min),
                reads=[xf], writes=[q])
            b.emit("g", lambda g, k=k, q=q: g.dma_start(
                out=qdram.t[k * 128:(k + 1) * 128, :], in_=q[:, :]),
                reads=[q], writes=[], sem=f"dqo{k % 2}", dma=True)
        qtk = [b.tok("dqo0"), b.tok("dqo1")]
        for c in range(NCH):
            b.emit("y", lambda s, c=c: s.dma_start_transpose(
                xqF[c][:, :], qdram.t[0:T, c * 128:(c + 1) * 128]),
                reads=[], writes=[xqF[c]], sem=f"dxqF{c % 4}",
                dma=True, extra_waits=qtk)
        xqtk = [b.tok(f"dxqF{i}") for i in range(4)]
        for c in range(NCH):
            xqF[c].w = xqtk[0]
            xqF[c].extra = xqtk[1:]
        b.handoff([S[12], S[13], S[14]], XF + QS)

        npsum = [0]

        def next_ps():
            p = ps[npsum[0] % 2]
            npsum[0] += 1
            return p

        SA = [b.sbuf(f"scanA{i}", [128, H + 1], F32) for i in range(2)]
        SB = [b.sbuf(f"scanB{i}", [128, H + 1], F32) for i in range(2)]
        b.emit("g", lambda g: g.memset(SA[0][:, 0:1], 0.0), writes=[SA[0]])
        b.emit("g", lambda g: g.memset(SB[0][:, 0:1], 0.0), writes=[SB[0]])
        KFh = [S[1], S[2]]
        VFh = [S[3], S[4]]
        SGRh = [S[5], S[6]]
        MW = [(S[7], S[8], S[9], S[10]), (S[11], S[12], S[13], S[14])]
        T1a, T1b = S[0], S[15]
        RW = (MW[0][0], MW[1][0])

        def emit_mm(p, j):
            pbuf = next_ps()
            b.emit("t", lambda tt, pbuf=pbuf, p=p, j=j: proj_mm(tt, pbuf, p, j),
                   reads=[WTJ[j % 2]] + xnF, writes=[pbuf])
            return pbuf

        def emit_evac(pbuf, eps, ecol_b, bias_b, j, d0, d1):
            ec = ecol_b[:, 0:1]
            bj = bias_b[:, j:j + 1]
            b.emit("v", lambda v, bj=bj: v.tensor_copy(T1a[:, 0:1], bj),
                   reads=[bias_b], writes=[T1a])
            b.emit("s", lambda s, pbuf=pbuf, ec=ec, bj=bj: s.activation(
                T1a[:, 1:H], pbuf[:, 0:H - 1], AF.Identity, bias=bj, scale=ec),
                reads=[pbuf, bias_b, ecol_b], writes=[T1a])
            b.emit("s", lambda s, pbuf=pbuf, ec=ec, bj=bj: s.activation(
                T1b[:, 0:H], pbuf[:, H - 1:T - 1], AF.Identity, bias=bj, scale=ec),
                reads=[pbuf, bias_b, ecol_b], writes=[T1b])
            b.emit("v", lambda v, pbuf=pbuf, eps=eps, d0=d0: v.scalar_tensor_tensor(
                d0[:, 0:H], pbuf[:, 1:H + 1], 1.0 - eps, T1a[:, 0:H],
                ALU.mult, ALU.add),
                reads=[pbuf, T1a], writes=[d0])
            b.emit("v", lambda v, pbuf=pbuf, eps=eps, d1=d1: v.scalar_tensor_tensor(
                d1[:, 0:H - 1], pbuf[:, H + 1:T], 1.0 - eps, T1b[:, 0:H - 1],
                ALU.mult, ALU.add),
                reads=[pbuf, T1b], writes=[d1])
            b.emit("v", lambda v, d1=d1: v.tensor_copy(
                d1[:, H - 1:H], T1b[:, H - 1:H]),
                reads=[T1b], writes=[d1])

        def emit_rinterp(pR, j):
            emit_evac(pR, EPS_R, ecols["r"], cols["br"], j, RW[0], RW[1])
            for hh in range(2):
                b.emit("s", lambda s, hh=hh: s.activation(
                    SGRh[hh][:, :], RW[hh][:, :], AF.Sigmoid),
                    reads=[RW[hh]], writes=[SGRh[hh]])

        def pre_k(j):
            p = emit_mm("k", j)
            emit_evac(p, EPS_K, ecols["k"], cols["bk"], j, KFh[0], KFh[1])

        def pre_v(j):
            p = emit_mm("v", j)
            emit_evac(p, EPS_V, ecols["v"], cols["bv"], j, VFh[0], VFh[1])

        def pre_r(j):
            p = emit_mm("r", j)
            emit_rinterp(p, j)

        def emit_middle(j, ins1=None, ins2=None, ins3=None):
            wj = cols["w"][:, j:j + 1]
            ubkj = cols["ubk"][:, j:j + 1]
            HV = [dict(W=MW[hh], kf=KFh[hh], vf=VFh[hh], sgr=SGRh[hh],
                       sa=SA[hh], sb=SB[hh], lo=hh * H) for hh in range(2)]

            def A1(hh):
                W4 = HV[hh]["W"][3]
                kf = HV[hh]["kf"]
                b.emit("s", lambda s, W4=W4, kf=kf, ubkj=ubkj: s.activation(
                    W4[:, :], kf[:, :], AF.Identity, bias=ubkj),
                    reads=[kf, cols["ubk"]], writes=[W4])

            def D1(hh):
                W1 = HV[hh]["W"][0]
                kf = HV[hh]["kf"]
                lo = HV[hh]["lo"]
                b.emit("v", lambda v, W1=W1, kf=kf, lo=lo, wj=wj:
                       v.scalar_tensor_tensor(W1[:, :], ramp[:, lo:lo + H], wj,
                                              kf[:, :], ALU.mult, ALU.add),
                       reads=[ramp, cols["w"], kf], writes=[W1])

            def DCLIP(buf):
                b.emit("v", lambda v, buf=buf: v.tensor_scalar(
                    buf[:, :], buf[:, :], -20.0, 10.0, ALU.max, ALU.min),
                    reads=[buf], writes=[buf])

            def A2(hh):
                W1, W2 = HV[hh]["W"][0], HV[hh]["W"][1]
                b.emit("s", lambda s, W1=W1, W2=W2: s.activation(
                    W2[:, :], W1[:, :], AF.Exp), reads=[W1], writes=[W2])

            def A3(hh):
                W4 = HV[hh]["W"][3]
                b.emit("s", lambda s, W4=W4: s.activation(
                    W4[:, :], W4[:, :], AF.Exp), reads=[W4], writes=[W4])

            def D4(hh):
                W2, W3 = HV[hh]["W"][1], HV[hh]["W"][2]
                vf = HV[hh]["vf"]
                b.emit("v", lambda v, W2=W2, W3=W3, vf=vf: v.tensor_mul(
                    W3[:, :], W2[:, :], vf[:, :]), reads=[W2, vf], writes=[W3])

            def DSCAN(hh, which):
                src = HV[hh]["W"][2] if which == 0 else HV[hh]["W"][1]
                dst = HV[hh]["sa"] if which == 0 else HV[hh]["sb"]
                if hh == 0:
                    b.emit("v", lambda v, src=src, dst=dst: v.tensor_tensor_scan(
                        dst[:, 1:H + 1], src[:, :], src[:, :], 0.0,
                        ALU.add, ALU.bypass),
                        reads=[src], writes=[dst])
                else:
                    prev = SA[0] if which == 0 else SB[0]
                    b.emit("v", lambda v, src=src, dst=dst, prev=prev:
                           v.tensor_tensor_scan(dst[:, 1:H + 1], src[:, :],
                                                src[:, :], prev[:, H:H + 1],
                                                ALU.add, ALU.bypass),
                           reads=[src, prev], writes=[dst])

            def DCARRY(which):
                dst = SA[1] if which == 0 else SB[1]
                prev = SA[0] if which == 0 else SB[0]
                b.emit("v", lambda v, dst=dst, prev=prev: v.tensor_copy(
                    dst[:, 0:1], prev[:, H:H + 1]), reads=[prev], writes=[dst])

            def D7(hh):
                W3, W4 = HV[hh]["W"][2], HV[hh]["W"][3]
                vf = HV[hh]["vf"]
                b.emit("v", lambda v, W3=W3, W4=W4, vf=vf: v.tensor_mul(
                    W3[:, :], W4[:, :], vf[:, :]), reads=[W4, vf], writes=[W3])

            def D8(hh):
                W2, W3 = HV[hh]["W"][1], HV[hh]["W"][2]
                sa = HV[hh]["sa"]
                b.emit("v", lambda v, W2=W2, W3=W3, sa=sa: v.tensor_add(
                    W2[:, :], sa[:, 0:H], W3[:, :]), reads=[sa, W3], writes=[W2])

            def D9(hh):
                W1, W4 = HV[hh]["W"][0], HV[hh]["W"][3]
                sb = HV[hh]["sb"]
                b.emit("v", lambda v, W1=W1, W4=W4, sb=sb: v.tensor_add(
                    W1[:, :], sb[:, 0:H], W4[:, :]), reads=[sb, W4], writes=[W1])

            def A4(hh):
                W1, W3 = HV[hh]["W"][0], HV[hh]["W"][2]
                b.emit("s", lambda s, W1=W1, W3=W3: s.activation(
                    W3[:, :], W1[:, :], AF.Ln), reads=[W1], writes=[W3])

            def A5(hh):
                W1, W3 = HV[hh]["W"][0], HV[hh]["W"][2]
                b.emit("s", lambda s, W1=W1, W3=W3: s.activation(
                    W1[:, :], W3[:, :], AF.Exp, scale=-1.0),
                    reads=[W3], writes=[W1])

            def D10(hh):
                W1, W2, W3 = HV[hh]["W"][0], HV[hh]["W"][1], HV[hh]["W"][2]
                b.emit("v", lambda v, W1=W1, W2=W2, W3=W3: v.tensor_mul(
                    W3[:, :], W2[:, :], W1[:, :]), reads=[W2, W1], writes=[W3])

            def D11(hh):
                W3 = HV[hh]["W"][2]
                sgr = HV[hh]["sgr"]
                lo = HV[hh]["lo"]
                ows = OWS[hh]
                b.emit("v", lambda v, W3=W3, sgr=sgr, ows=ows: v.tensor_mul(
                    ows[:, :], W3[:, :], sgr[:, :]),
                    reads=[W3, sgr], writes=[ows])
                b.emit("y", lambda s, ows=ows, lo=lo, j=j: s.dma_start(
                    out=owdram.t[j * 128:(j + 1) * 128, lo:lo + H],
                    in_=ows[:, :]),
                    reads=[ows], writes=[], sem=f"dow{hh}", dma=True)

            for hh in range(2):
                A1(hh)
                D1(hh)
                DCLIP(HV[hh]["W"][0])
                A2(hh)
            for hh in range(2):
                DCLIP(HV[hh]["W"][3])
                A3(hh)
            D4(0)
            DSCAN(0, 0)
            DSCAN(0, 1)
            D4(1)
            D7(0)
            D8(0)
            D9(0)
            A4(0)
            DSCAN(1, 0)
            DSCAN(1, 1)
            DCARRY(0)
            DCARRY(1)
            A5(0)
            if ins1 is not None:
                ins1()
            D7(1)
            D8(1)
            D9(1)
            A4(1)
            D10(0)
            D11(0)
            A5(1)
            if ins2 is not None:
                ins2()
            D10(1)
            D11(1)
            if ins3 is not None:
                ins3()

        def emit_gate(j):
            grp, s_in = j // 4, j % 4
            pG = next_ps()
            pU = next_ps()

            def gmm(tt, pG=pG, s_in=s_in):
                last = None
                for c in range(NCH):
                    for q in range(4):
                        last = tt.matmul(
                            pG[:, q * 512:(q + 1) * 512],
                            GT["g"][:, c * 512 + s_in * 128:
                                    c * 512 + (s_in + 1) * 128],
                            xqF[c][:, q * 512:(q + 1) * 512],
                            start=(c == 0), stop=(c == NCH - 1))
                return last

            def umm(tt, pU=pU, s_in=s_in):
                last = None
                for c in range(NCH):
                    for q in range(4):
                        last = tt.matmul(
                            pU[:, q * 512:(q + 1) * 512],
                            GT["u"][:, c * 512 + s_in * 128:
                                    c * 512 + (s_in + 1) * 128],
                            xqF[c][:, q * 512:(q + 1) * 512],
                            start=(c == 0), stop=(c == NCH - 1))
                return last
            b.emit("t", gmm, reads=[GT["g"]] + xqF, writes=[pG])
            b.emit("t", umm, reads=[GT["u"]] + xqF, writes=[pU])
            for hh in range(2):
                sg = SGH[hh]
                lo = hh * H
                b.emit("s", lambda s, pG=pG, lo=lo, sg=sg: s.activation(
                    sg[:, :], pG[:, lo:lo + H], AF.Silu, scale=sx_col[:, 0:1]),
                    reads=[pG, sx_col], writes=[sg])
                b.emit("v", lambda v, pU=pU, lo=lo, sg=sg:
                       v.scalar_tensor_tensor(
                           HST[:, lo:lo + H], pU[:, lo:lo + H], sx_col[:, 0:1],
                           sg[:, :], ALU.mult, ALU.mult),
                       reads=[pU, sg, sx_col], writes=[HST])
            b.emit("v", lambda v: v.reduce_max(
                hmx1[:, :], HST[:, :], axis=AX, apply_absolute_value=True),
                reads=[HST], writes=[hmx1])
            b.emit("v", lambda v: v.tensor_max(hmax[:, :], hmax[:, :], hmx1[:, :]),
                   reads=[hmx1, hmax], writes=[hmax])
            b.emit("s", lambda s, j=j: s.dma_start(
                out=h_d.t[j * 128:(j + 1) * 128, :], in_=HST[:, :]),
                reads=[HST], writes=[], sem=f"dho{j % 2}", dma=True)
            if s_in == 3 and grp + 1 <= (NIC - 1) // 4:
                load_gt_group(grp + 1)

        gate_pos = [0]

        def emit_gates(nmax):
            for _ in range(nmax):
                jg = gate_pos[0]
                if jg >= NIC:
                    return
                gate_pos[0] += 1
                emit_gate(jg)

        def ins3_fn(n):
            def f():
                if n < NCH:
                    pre_r(n)
                    if n + 1 < NCH:
                        load_wtj(n + 1)
                if n >= 2:
                    emit_gates(2)
            return f

        pre_k(0)
        pre_v(0)
        pre_r(0)
        for j in range(NCH):
            n = j + 1
            emit_middle(j,
                        ins1=(lambda n=n: pre_k(n)) if n < NCH else None,
                        ins2=(lambda n=n: pre_v(n)) if n < NCH else None,
                        ins3=ins3_fn(n))

        WoT = [b.at(R4, 16384 + c * 2048, f"WoT{c}", [128, C], BF16)
               for c in range(NCH)]
        b.handoff(WoT, xnF)
        for c in range(NCH):
            b.emit("y", lambda s, c=c: s.dma_start(
                out=WoT[c][:, :], in_=wot_d.t[c]),
                reads=[wot_d], writes=[WoT[c]], sem=f"dwt_o{c % 2}", dma=True)
        wotk = [b.tok("dwt_o0"), b.tok("dwt_o1")]
        for c in range(NCH):
            WoT[c].w = wotk[0]
            WoT[c].extra = [wotk[1]]

        XR = [b.at(R2, i * 4096, f"xr{i}", [128, C], F32) for i in range(2)]
        AST = [b.at(R2, 8192 + i * 4096, f"ast{i}", [128, C], F32)
               for i in range(2)]
        b.handoff(XR + AST, S)

        owtk = [b.tok("dow0"), b.tok("dow1")]
        for k in range(NTT):
            pW = next_ps()
            ast = AST[k % 2]
            x = XR[k % 2]
            ow = OWL[k % 2]
            b.emit("g", lambda g, k=k, x=x: g.dma_start(
                out=x[:, :], in_=x_d.t[k * 128:(k + 1) * 128, :]),
                reads=[], writes=[x], sem=f"dx{k % 2}", dma=True)
            b.emit("g", lambda g, k=k, ow=ow: g.dma_start(
                out=ow.t.ap().rearrange("p (c t) -> p c t", c=NCH),
                in_=owdram.t[0:C, k * 128:(k + 1) * 128].rearrange(
                    "(c p) t -> p c t", p=128)),
                reads=[], writes=[ow], sem=f"dowl{k % 2}", dma=True,
                extra_waits=owtk)
            b.emit("v", lambda v, x=x: v.tensor_add(
                x[:, :], x[:, :], bo_bc[:, :]),
                reads=[bo_bc, x], writes=[x])

            def womm(tt, pW=pW, k=k, ow=ow):
                last = None
                for c in range(NCH):
                    for q in range(2):
                        last = tt.matmul(
                            pW[:, q * 512:(q + 1) * 512],
                            ow[:, c * 128:(c + 1) * 128],
                            WoT[c][:, q * 512:(q + 1) * 512],
                            start=(c == 0), stop=(c == NCH - 1))
                return last
            b.emit("t", womm, reads=[ow] + WoT, writes=[pW])
            b.emit("v", lambda v, pW=pW, ast=ast, x=x: v.tensor_add(
                ast[:, :], pW[:, 0:C], x[:, :]),
                reads=[pW, x], writes=[ast])
            b.emit("s", lambda s, k=k, ast=ast: s.dma_start(
                out=acc_d.t[k * 128:(k + 1) * 128, :], in_=ast[:, :]),
                reads=[ast], writes=[], sem=f"dacc{k % 2}", dma=True)
            if k % 2 == 1:
                emit_gates(1)
        emit_gates(NIC)

        b.emit("y", lambda s: s.dma_start(out=colscr.t[:], in_=hmax[:, 0:1]),
               reads=[hmax], writes=[colscr], sem="dcs2", dma=True)
        b.emit("y", lambda s: s.dma_start(out=row[0:1, :], in_=colscr.t[:]),
               reads=[colscr], writes=[row], sem="drow2", dma=True)
        b.emit("v", lambda v: v.reduce_max(one1[0:1, 0:1], row[0:1, :], axis=AX),
               reads=[row], writes=[one1])
        b.emit("g", lambda g: g.dma_start(out=lm2_d.t[0:1, 0:1], in_=one1[0:1, 0:1]),
               reads=[one1], writes=[lm2_d], sem="dlm2", dma=True)

        b.final_wait("g", [b.tok("dacc0"), b.tok("dacc1"), b.tok("dho0"),
                           b.tok("dho1"), b.tok("dlm2")])
        block = ctx.enter_context(nc.Block())
        b.build(block)
    return nc


def build_l3():
    nc = bass.Bass("TRN2", target_bir_lowering=False)
    ctx = contextlib.ExitStack()
    with ctx:
        b = Bld(nc, ctx)
        D = lambda n, sh, dt, k="ExternalInput": Buf(nc.dram_tensor(n, sh, dt, kind=k), n)
        h_d = D("h", [I, T], F16)
        acc_d = D("acc", [T, C], F32)
        wdq_dram = D("wdqt", [NIC, 128, 1024], BF16)
        lm2_d = D("lm2", [1, 8], F32)
        clipd_d = D("clipd", [1], F32)
        out_d = D("out", [T, C], F32, "ExternalOutput")
        scaldr = b.dram("scaldr", [1, 2], F32)

        qh = [b.sbuf(f"qh{j}", [128, T], BF16) for j in range(NIC)]
        wdT = [b.sbuf(f"wdT{j}", [128, C], BF16) for j in range(NIC)]
        hst = [b.sbuf(f"hld{i}", [128, T], F16) for i in range(2)]
        accr = [b.sbuf(f"accr{i}", [128, C], F32) for i in range(2)]
        ost = [b.sbuf(f"ost{i}", [128, C], F32) for i in range(2)]
        t32 = [b.sbuf(f"t32_{i}", [128, T], F32) for i in range(2)]
        s2_col = b.sbuf("s2_col", [128, 1], F32)
        is2_col = b.sbuf("is2_col", [128, 1], F32)
        one1 = b.sbuf("one1", [1, 1], F32)
        one2 = b.sbuf("one2", [1, 1], F32)
        lmrow = b.sbuf("lmrow", [1, 8], F32)
        sc2 = b.sbuf("sc2", [1, 2], F32)
        magic_col = b.sbuf("magic_col", [128, 1], F32)
        ps = [b.psum(f"ps{i}", [128, C], F32) for i in range(4)]
        b.emit("g", lambda g: g.memset(magic_col[:, :], MAGIC), writes=[magic_col])

        b.emit("g", lambda g: g.dma_start(out=lmrow[0:1, :], in_=lm2_d.t[0:1, :]),
               reads=[], writes=[lmrow], sem="dlm2", dma=True)
        b.emit("g", lambda g: g.dma_start(out=one1[0:1, 0:1], in_=clipd_d.t[0:1]),
               reads=[], writes=[one1], sem="dcd", dma=True)
        b.emit("v", lambda v: v.reduce_max(one2[0:1, 0:1], lmrow[0:1, :], axis=AX),
               reads=[lmrow], writes=[one2])
        b.emit("s", lambda s: s.activation(one1[0:1, 0:1], one1[0:1, 0:1], AF.Sigmoid),
               reads=[one1], writes=[one1])
        b.emit("v", lambda v: v.tensor_mul(sc2[0:1, 0:1], one1[0:1, 0:1],
                                           one2[0:1, 0:1]),
               reads=[one1, one2], writes=[sc2])
        b.emit("v", lambda v: v.tensor_scalar_mul(sc2[0:1, 0:1], sc2[0:1, 0:1],
                                                  1.0 / QMAX),
               reads=[sc2], writes=[sc2])
        b.emit("v", lambda v: v.reciprocal(sc2[0:1, 1:2], sc2[0:1, 0:1]),
               reads=[sc2], writes=[sc2])
        b.emit("y", lambda s: s.dma_start(out=scaldr.t[0:1, :], in_=sc2[0:1, :]),
               reads=[sc2], writes=[scaldr], sem="dscw", dma=True)
        b.emit("y", lambda s: s.dma_start(
            out=s2_col[:, 0:1], in_=bass.AP(scaldr.t, 0, [[0, 128], [1, 1]])),
            reads=[scaldr], writes=[s2_col], sem="ds2", dma=True)
        b.emit("y", lambda s: s.dma_start(
            out=is2_col[:, 0:1], in_=bass.AP(scaldr.t, 1, [[0, 128], [1, 1]])),
            reads=[scaldr], writes=[is2_col], sem="dis2", dma=True)

        for j in range(NIC):
            b.emit("y", lambda s, j=j: s.dma_start(
                out=wdT[j][:, :], in_=wdq_dram.t[j]),
                reads=[wdq_dram], writes=[wdT[j]], sem=f"dwdT{j % 2}", dma=True)
        wtk = [b.tok("dwdT0"), b.tok("dwdT1")]
        for j in range(NIC):
            wdT[j].w = wtk[0]
            wdT[j].extra = [wtk[1]]

        for j in range(NIC):
            hl = hst[j % 2]
            tl = t32[j % 2]
            b.emit("g", lambda g, j=j, hl=hl: g.dma_start(
                out=hl[:, :], in_=h_d.t[j * 128:(j + 1) * 128, :]),
                reads=[], writes=[hl], sem=f"dh{j % 2}", dma=True)
            b.emit("s", lambda s, hl=hl, tl=tl: s.activation(
                tl[:, :], hl[:, :], AF.Identity, bias=magic_col[:, 0:1],
                scale=is2_col[:, 0:1]),
                reads=[hl, is2_col, magic_col], writes=[tl])
            b.emit("v", lambda v, tl=tl: v.tensor_scalar(
                tl[:, :], tl[:, :], MAGIC, -(QMAX + 1.0), ALU.subtract, ALU.max),
                reads=[tl], writes=[tl])
            b.emit("v", lambda v, j=j, tl=tl: v.tensor_scalar_min(
                qh[j][:, :], tl[:, :], QMAX),
                reads=[tl], writes=[qh[j]])

        for k in range(NTT):
            pD = ps[k % 4]
            ac = accr[k % 2]
            os_ = ost[k % 2]
            b.emit("g", lambda g, k=k, ac=ac: g.dma_start(
                out=ac[:, :], in_=acc_d.t[k * 128:(k + 1) * 128, :]),
                reads=[], writes=[ac], sem=f"dac{k % 2}", dma=True)

            def dmm(tt, pD=pD, k=k):
                last = None
                for j in range(NIC):
                    for q in range(2):
                        last = tt.matmul(
                            pD[:, q * 512:(q + 1) * 512],
                            qh[j][:, k * 128:(k + 1) * 128],
                            wdT[j][:, q * 512:(q + 1) * 512],
                            start=(j == 0), stop=(j == NIC - 1))
                return last
            b.emit("t", dmm, reads=qh + wdT, writes=[pD])
            b.emit("v", lambda v, pD=pD, ac=ac, os_=os_: v.scalar_tensor_tensor(
                os_[:, :], pD[:, 0:C], s2_col[:, 0:1], ac[:, :],
                ALU.mult, ALU.add),
                reads=[pD, ac, s2_col], writes=[os_])
            b.emit("g", lambda g, k=k, os_=os_: g.dma_start(
                out=out_d.t[k * 128:(k + 1) * 128, :], in_=os_[:, :]),
                reads=[os_], writes=[], sem=f"dout{k % 2}", dma=True)

        b.final_wait("g", [b.tok("dout0"), b.tok("dout1")])
        block = ctx.enter_context(nc.Block())
        b.build(block)
    return nc


def kernel(L0_attn_x, L0_attn_Wr, L0_attn_br, L0_attn_Wk, L0_attn_bk,
           L0_attn_Wv, L0_attn_bv, L0_attn_Wo, L0_attn_bo, L0_attn_time_decay,
           L0_attn_U, L0_ffn_clip_x, L0_ffn_clip_down, L0_ffn_w_gate,
           L0_ffn_w_up, L0_ffn_w_down):
    n = 8
    x = np.ascontiguousarray(np.asarray(L0_attn_x, dtype=np.float32))
    xs = [np.ascontiguousarray(x[i]) for i in range(n)]
    f32 = lambda a: np.ascontiguousarray(np.asarray(a, dtype=np.float32))
    Wr, Wk, Wv, Wo = f32(L0_attn_Wr), f32(L0_attn_Wk), f32(L0_attn_Wv), f32(L0_attn_Wo)
    br, bk, bv, bo = f32(L0_attn_br), f32(L0_attn_bk), f32(L0_attn_bv), f32(L0_attn_bo)
    td, Uv = f32(L0_attn_time_decay), f32(L0_attn_U)
    clipx, clipd = f32(L0_ffn_clip_x), f32(L0_ffn_clip_down)
    wg, wu, wd = f32(L0_ffn_w_gate), f32(L0_ffn_w_up), f32(L0_ffn_w_down)

    cores = list(range(n))
    gstart = [0, 384, 768, 1152, 1536, 1920, 2304, 2560]
    grows = [384] * 6 + [256] * 2

    def wslice(w, i):
        blk = w[gstart[i]:gstart[i] + grows[i]]
        if grows[i] < 384:
            blk = np.concatenate([blk, blk[:384 - grows[i]]], axis=0)
        return np.ascontiguousarray(blk)

    nc1 = build_l1()
    in1 = [{"x": xs[i], "wg": wslice(wg, i), "wu": wslice(wu, i),
            "wd": np.ascontiguousarray(wd[i * 128:(i + 1) * 128]),
            "Wr": np.ascontiguousarray(Wr[i * 128:(i + 1) * 128]),
            "Wk": np.ascontiguousarray(Wk[i * 128:(i + 1) * 128]),
            "Wv": np.ascontiguousarray(Wv[i * 128:(i + 1) * 128]),
            "Wo": np.ascontiguousarray(Wo[i * 128:(i + 1) * 128])}
           for i in range(n)]
    r1 = run_bass_kernel_spmd(nc1, in1, core_ids=cores)
    lm1 = np.concatenate(
        [np.asarray(r1.results[i]["lm1"]).reshape(1, 1) for i in range(n)], axis=1)
    xnf = [np.asarray(r1.results[i]["xnf"]) for i in range(n)]
    xnb = [np.asarray(r1.results[i]["xnb"]) for i in range(n)]
    def wassemble(key):
        tiles = []
        for i in range(n):
            t = np.asarray(r1.results[i][key])
            tiles.extend(t[:grows[i] // 128])
        tiles.extend(tiles[-1:] * 2)
        arr = np.stack(tiles, axis=0).reshape(6, 4, NCH, 128, 128)
        return np.ascontiguousarray(
            arr.transpose(0, 3, 2, 1, 4).reshape(6, 128, 4096))

    wgq = wassemble("wgqt")
    wuq = wassemble("wuqt")
    wdq = np.ascontiguousarray(
        np.stack([np.asarray(r1.results[i]["wdqt"]) for i in range(n)],
                 axis=2).reshape(NIC, 128, 1024))
    wtt = np.ascontiguousarray(
        np.stack([np.asarray(r1.results[i]["wtt"]) for i in range(n)], axis=0))
    wo_j = np.stack([np.asarray(r1.results[i]["wot"]).reshape(128, 8, 128)
                     for i in range(n)], axis=0)
    wot = np.ascontiguousarray(
        wo_j.transpose(2, 1, 0, 3).reshape(8, 128, 1024))

    nc2 = build_l2()
    in2 = [{"x": xs[i], "xnf": xnf[i], "xnb": xnb[i], "lm1": lm1,
            "wtt": wtt, "wot": wot,
            "br": br, "bk": bk, "bv": bv, "bo": bo,
            "td": td, "Uv": Uv, "clipx": clipx, "wgqt": wgq, "wuqt": wuq}
           for i in range(n)]
    r2 = run_bass_kernel_spmd(nc2, in2, core_ids=cores)
    lm2 = np.concatenate(
        [np.asarray(r2.results[i]["lm2"]).reshape(1, 1) for i in range(n)], axis=1)
    acc = [np.asarray(r2.results[i]["acc"]) for i in range(n)]
    hh = [np.asarray(r2.results[i]["h"]) for i in range(n)]

    nc3 = build_l3()
    in3 = [{"h": hh[i], "acc": acc[i], "wdqt": wdq, "lm2": lm2, "clipd": clipd}
           for i in range(n)]
    r3 = run_bass_kernel_spmd(nc3, in3, core_ids=cores)
    out = np.stack([np.asarray(r3.results[i]["out"]) for i in range(n)], axis=0)
    return out.astype(np.float32)


# version 19 (final)
# speedup vs baseline: 1.1103x; 1.0150x over previous
import math
import contextlib
import numpy as np
import ml_dtypes
import concourse.bass as bass
import concourse.mybir as mybir
from concourse.bass_utils import run_bass_kernel_spmd

F32 = mybir.dt.float32
BF16 = mybir.dt.bfloat16
F16 = mybir.dt.float16
U8 = mybir.dt.uint8
AF = mybir.ActivationFunctionType
ALU = mybir.AluOpType
AX = mybir.AxisListType.X

MAGIC = 12582912.0
T, C, I = 2048, 1024, 2816
NCH = C // 128
NIC = I // 128
NTT = T // 128
H = T // 2
_EPS = math.exp(-1.0 / 12.0)
EPS_R = _EPS / 2.0
EPS_K = _EPS + 0.3 / 11.0
EPS_V = _EPS
QMAX = 127.0


_ENG_ATTR = {"v": "vector", "s": "scalar", "t": "tensor", "g": "gpsimd", "y": "sync"}


class Buf:
    __slots__ = ("t", "w", "rs", "extra", "name")

    def __init__(self, t, name=""):
        self.t = t
        self.w = None
        self.rs = []
        self.extra = []
        self.name = name

    def __getitem__(self, idx):
        return self.t[idx]


class Bld:
    def __init__(self, nc, ctx):
        self.nc = nc
        self.ctx = ctx
        self.prog = []
        self.sems = {}
        self.waited = {}
        self.dma_sems = set()

    def sbuf(self, name, shape, dtype):
        return Buf(self.ctx.enter_context(self.nc.sbuf_tensor(name, shape, dtype)), name)

    def psum(self, name, shape, dtype):
        return Buf(self.ctx.enter_context(self.nc.psum_tensor(name, shape, dtype)), name)

    def dram(self, name, shape, dtype, kind="Internal"):
        return Buf(self.nc.dram_tensor(name, shape, dtype, kind=kind), name)

    def region(self, name, nbytes):
        t = self.ctx.enter_context(self.nc.sbuf_tensor(name, [128, nbytes], U8))
        return self.nc.lookup_mloc(t).addr

    def at(self, addr, rel, name, shape, dtype):
        t = self.nc.alloc_sbuf_tensor_at(name, shape, dtype, offset=addr + rel)
        return Buf(t, name)

    def handoff(self, new_bufs, old_bufs):
        toks = []
        for b in old_bufs:
            if b.w:
                toks.append(b.w)
            toks.extend(b.rs)
            toks.extend(b.extra)
        for nb in new_bufs:
            nb.extra = list(toks)

    def _sem(self, key):
        if key not in self.sems:
            h = self.ctx.enter_context(self.nc.semaphore(key[:30]))
            self.sems[key] = [h, 0]
        return self.sems[key]

    def emit(self, eng, fn, reads=(), writes=(), sem=None, dma=False, n_dma=1,
             extra_waits=()):
        if dma:
            assert sem is not None
        sem_key = sem or ("c" + eng)
        s = self._sem(sem_key)
        amount = 16 * n_dma if dma else 1
        toks = list(extra_waits)
        for b in reads:
            if b.w:
                toks.append(b.w)
            toks.extend(b.extra)
        for b in writes:
            if b.w:
                toks.append(b.w)
            toks.extend(b.rs)
            toks.extend(b.extra)
        if dma:
            toks.append((sem_key, s[1]))
            self.dma_sems.add(sem_key)
        wd = self.waited.setdefault(eng, {})
        final = {}
        for (sk, v) in toks:
            if v > wd.get(sk, 0) and v > final.get(sk, 0):
                final[sk] = v
        s[1] += amount
        tok = (sem_key, s[1])
        waits = sorted(final.items())

        def closure(e, waits=waits, semh=s[0], amt=amount, fn=fn):
            for sk, v in waits:
                e.wait_ge(self.sems[sk][0], v)
            fn(e).then_inc(semh, amt)

        self.prog.append((eng, closure))
        for sk, v in final.items():
            wd[sk] = v
        for b in reads:
            b.rs.append(tok)
        for b in writes:
            b.w = tok
            b.rs = []
            b.extra = []
        return tok

    def tok(self, sem_key):
        return (sem_key, self._sem(sem_key)[1])

    def settle(self, bufs, tok):
        for b in bufs:
            b.w = tok
            b.rs = []
            b.extra = []

    def final_wait(self, eng, toks=None):
        toks = [self.tok(k) for k in sorted(self.dma_sems)]
        def closure(e, toks=toks):
            for sk, v in toks:
                e.wait_ge(self.sems[sk][0], v)
        self.prog.append((eng, closure))

    def build(self, block):
        per = {k: [] for k in _ENG_ATTR}
        for eng, closure in self.prog:
            per[eng].append(closure)
        for key, closures in per.items():
            if not closures:
                continue

            def body(e, closures=closures):
                for c in closures:
                    c(e)

            getattr(block, _ENG_ATTR[key])(body)


def build_l1():
    nc = bass.Bass("TRN2", target_bir_lowering=False)
    ctx = contextlib.ExitStack()
    with ctx:
        b = Bld(nc, ctx)
        x_d = Buf(nc.dram_tensor("x", [T, C], F32, kind="ExternalInput"), "x")
        wg_d = Buf(nc.dram_tensor("wg", [384, C], F32, kind="ExternalInput"), "wg")
        wu_d = Buf(nc.dram_tensor("wu", [384, C], F32, kind="ExternalInput"), "wu")
        wd_d = Buf(nc.dram_tensor("wd", [128, I], F32, kind="ExternalInput"), "wd")
        xnf_d = Buf(nc.dram_tensor("xnf", [T, C], F32, kind="ExternalOutput"), "xnf")
        xnb_d = Buf(nc.dram_tensor("xnb", [T, C], BF16, kind="ExternalOutput"), "xnb")
        lm1_d = Buf(nc.dram_tensor("lm1", [1, 1], F32, kind="ExternalOutput"), "lm1")
        wgq_d = Buf(nc.dram_tensor("wgqt", [3, NCH, 128, 128], BF16,
                                   kind="ExternalOutput"), "wgqt")
        wuq_d = Buf(nc.dram_tensor("wuqt", [3, NCH, 128, 128], BF16,
                                   kind="ExternalOutput"), "wuqt")
        wdq_d = Buf(nc.dram_tensor("wdqt", [NIC, 128, 128], BF16,
                                   kind="ExternalOutput"), "wdqt")
        aw_in = {p: Buf(nc.dram_tensor(f"W{p}", [128, C], F32, kind="ExternalInput"),
                        f"W{p}") for p in "rkvo"}
        wtt_d = Buf(nc.dram_tensor("wtt", [128, 3072], BF16,
                                   kind="ExternalOutput"), "wtt")
        wot_d = Buf(nc.dram_tensor("wot", [128, 1024], BF16,
                                   kind="ExternalOutput"), "wot")
        colscr = b.dram("colscr", [128], F32)

        xt = [b.sbuf(f"xt{i}", [128, C], F32) for i in range(2)]
        xnf = [b.sbuf(f"xnf{i}", [128, C], F32) for i in range(2)]
        xnb = [b.sbuf(f"xnb{i}", [128, C], BF16) for i in range(2)]
        sq = [b.sbuf(f"sq{i}", [128, C], F32) for i in range(2)]
        ss = [b.sbuf(f"ss{i}", [128, 1], F32) for i in range(2)]
        sm = [b.sbuf(f"sm{i}", [128, 1], F32) for i in range(2)]
        mcol = [b.sbuf(f"mcol{i}", [128, 1], F32) for i in range(2)]
        vcol = [b.sbuf(f"vcol{i}", [128, 1], F32) for i in range(2)]
        sdcol = [b.sbuf(f"sdcol{i}", [128, 1], F32) for i in range(2)]
        rstd = [b.sbuf(f"rstd{i}", [128, 1], F32) for i in range(2)]
        nmr = [b.sbuf(f"nmr{i}", [128, 1], F32) for i in range(2)]
        mxc = b.sbuf("mxc", [128, 1], F32)
        mxacc = b.sbuf("mxacc", [128, 1], F32)
        row = b.sbuf("rowb", [1, 128], F32)
        one1 = b.sbuf("one1", [1, 1], F32)
        magic_col = b.sbuf("magic_col", [128, 1], F32)
        NPAR = 4
        gwst = [b.sbuf(f"gwst{i}", [128, I], F32) for i in range(NPAR)]
        gqo = [b.sbuf(f"gqo{i}", [128, I], BF16) for i in range(NPAR)]
        rmaxc = [b.sbuf(f"rmaxc{i}", [128, 1], F32) for i in range(NPAR)]
        rscl = [b.sbuf(f"rscl{i}", [128, 1], F32) for i in range(NPAR)]
        rinv = [b.sbuf(f"rinv{i}", [128, 1], F32) for i in range(NPAR)]

        b.emit("g", lambda g: g.memset(mxacc[:, :], 0.0), writes=[mxacc])
        b.emit("g", lambda g: g.memset(magic_col[:, :], MAGIC), writes=[magic_col])
        awst = [b.sbuf(f"awst{i}", [128, C], BF16) for i in range(2)]
        ident = b.sbuf("ident", [128, 128], BF16)
        iota_a = b.sbuf("iota_a", [128, 128], F32)
        iota_b = b.sbuf("iota_b", [128, 128], F32)
        ttile = b.sbuf("ttile", [128, 3072], BF16)
        otile = b.sbuf("otile", [128, 1024], BF16)
        pt = [b.psum(f"pt{i}", [128, 1024], BF16) for i in range(2)]
        ptd = b.psum("ptd", [128, I], BF16)
        tqo = [b.sbuf(f"tqo{i}", [128, I], BF16) for i in range(2)]
        b.emit("g", lambda g: g.iota(iota_a[:, :], [[0, 128]], base=0,
                                     channel_multiplier=1,
                                     allow_small_or_imprecise_dtypes=True),
               writes=[iota_a])
        b.emit("g", lambda g: g.iota(iota_b[:, :], [[1, 128]], base=0,
                                     channel_multiplier=0,
                                     allow_small_or_imprecise_dtypes=True),
               writes=[iota_b])
        b.emit("v", lambda v: v.tensor_tensor(ident[:, :], iota_a[:, :],
                                              iota_b[:, :], ALU.is_equal),
               reads=[iota_a, iota_b], writes=[ident])
        for pi, p in enumerate("rkvo"):
            st = awst[pi % 2]
            pp = pt[pi % 2]
            dst = otile if p == "o" else ttile
            off = 0 if p == "o" else {"r": 0, "k": 1, "v": 2}[p] * 1024
            b.emit("g", lambda g, p=p, st=st: g.dma_start(
                out=st[:, :], in_=aw_in[p].t[0:128, :]),
                reads=[], writes=[st], sem=f"dawi{pi % 2}", dma=True)

            def trans(tt, st=st, pp=pp):
                last = None
                for c in range(NCH):
                    last = tt.transpose(pp[:, c * 128:(c + 1) * 128],
                                        st[:, c * 128:(c + 1) * 128],
                                        ident[:, :])
                return last
            b.emit("t", trans, reads=[st, ident], writes=[pp])
            b.emit("s", lambda s, pp=pp, dst=dst, off=off: s.activation(
                dst[:, off:off + 1024], pp[:, :], AF.Copy),
                reads=[pp], writes=[dst])
        b.emit("y", lambda s: s.dma_start(out=wtt_d.t[0:128, :],
                                          in_=ttile[:, :]),
               reads=[ttile], writes=[], sem="dwtt", dma=True)
        b.emit("y", lambda s: s.dma_start(out=wot_d.t[0:128, :],
                                          in_=otile[:, :]),
               reads=[otile], writes=[], sem="dwot", dma=True)

        _unit_ctr = [0]

        def make_wq_unit(wsrc, wdst, c, ncols, sem_pref):
            def unit():
                par = _unit_ctr[0] % NPAR
                _unit_ctr[0] += 1
                st = gwst[par]
                qo = gqo[par]
                rm, rs, ri = rmaxc[par], rscl[par], rinv[par]
                b.emit("g", lambda g: g.dma_start(
                    out=st[:, 0:ncols], in_=wsrc.t[c * 128:(c + 1) * 128, :]),
                    reads=[], writes=[st], sem=f"dgw{par}", dma=True)
                b.emit("v", lambda v: v.reduce_max(
                    rm[:, :], st[:, 0:ncols], axis=AX, apply_absolute_value=True),
                    reads=[st], writes=[rm])
                b.emit("v", lambda v: v.tensor_scalar_mul(
                    rs[:, :], rm[:, :], 1.0 / QMAX),
                    reads=[rm], writes=[rs])
                b.emit("v", lambda v: v.reciprocal(ri[:, :], rs[:, :]),
                       reads=[rs], writes=[ri])
                b.emit("s", lambda s: s.activation(
                    st[:, 0:ncols], st[:, 0:ncols], AF.Identity,
                    bias=magic_col[:, 0:1], scale=ri[:, 0:1]),
                    reads=[st, ri, magic_col], writes=[st])
                b.emit("v", lambda v: v.tensor_scalar(
                    st[:, 0:ncols], st[:, 0:ncols], MAGIC, -(QMAX + 1.0),
                    ALU.subtract, ALU.max),
                    reads=[st], writes=[st])
                b.emit("v", lambda v: v.tensor_scalar(
                    qo[:, 0:ncols], st[:, 0:ncols], QMAX, rs[:, 0:1],
                    ALU.min, ALU.mult),
                    reads=[st, rs], writes=[qo])
                nt = ncols // 128
                pp = ptd if ncols == I else pt[par % 2]
                tq = tqo[par % 2]

                def trq(tt, qo=qo, pp=pp, nt=nt):
                    last = None
                    for c2 in range(nt):
                        last = tt.transpose(pp[:, c2 * 128:(c2 + 1) * 128],
                                            qo[:, c2 * 128:(c2 + 1) * 128],
                                            ident[:, :])
                    return last
                b.emit("t", trq, reads=[qo, ident], writes=[pp])
                b.emit("s", lambda s, pp=pp, tq=tq, ncols=ncols: s.activation(
                    tq[:, 0:ncols], pp[:, 0:ncols], AF.Copy),
                    reads=[pp], writes=[tq])
                b.emit("y", lambda s, tq=tq, nt=nt, ncols=ncols: s.dma_start(
                    out=wdst.t[c].rearrange("c2 p t -> p c2 t") if ncols == C
                    else wdst.t.ap().rearrange("c2 p t -> p c2 t"),
                    in_=tq[:, 0:ncols].rearrange("p (c2 t) -> p c2 t", t=128)),
                    reads=[tq], writes=[], sem=sem_pref + str(par % 2), dma=True)
            return unit

        wq_units = []
        for c in range(3):
            wq_units.append(make_wq_unit(wg_d, wgq_d, c, C, "dwgq"))
            wq_units.append(make_wq_unit(wu_d, wuq_d, c, C, "dwuq"))
        wq_units.append(make_wq_unit(wd_d, wdq_d, 0, I, "dwdq"))
        wq_iter = iter(wq_units)
        for k in range(NTT):
            pp = k % 2
            x = xt[pp]
            xf = xnf[pp]
            xb = xnb[pp]
            sq_, ss_, sm_ = sq[pp], ss[pp], sm[pp]
            mc, vc, sd_, rs_, nm_ = mcol[pp], vcol[pp], sdcol[pp], rstd[pp], nmr[pp]
            b.emit("g", lambda g, k=k, x=x: g.dma_start(
                out=x[:, :], in_=x_d.t[k * 128:(k + 1) * 128, :]),
                reads=[], writes=[x], sem=f"dx{k % 2}", dma=True)
            b.emit("s", lambda s, x=x, sq_=sq_, ss_=ss_: s.activation(
                sq_[:, :], x[:, :], AF.Square, accum_out=ss_[:, 0:1]),
                reads=[x], writes=[sq_, ss_])
            b.emit("v", lambda v, x=x, sm_=sm_: v.reduce_sum(
                sm_[:, :], x[:, :], axis=AX),
                reads=[x], writes=[sm_])
            b.emit("v", lambda v, sm_=sm_, mc=mc: v.tensor_scalar_mul(
                mc[:, :], sm_[:, :], 1.0 / C),
                reads=[sm_], writes=[mc])
            b.emit("v", lambda v, ss_=ss_, vc=vc: v.tensor_scalar_mul(
                vc[:, :], ss_[:, :], 1.0 / C),
                reads=[ss_], writes=[vc])
            b.emit("v", lambda v, mc=mc, sq_=sq_: v.scalar_tensor_tensor(
                sq_[:, 0:1], mc[:, :], -1.0, mc[:, :], ALU.mult, ALU.mult),
                reads=[mc], writes=[sq_])
            b.emit("v", lambda v, vc=vc, sq_=sq_: v.tensor_add(
                vc[:, :], vc[:, :], sq_[:, 0:1]),
                reads=[sq_], writes=[vc])
            b.emit("v", lambda v, vc=vc: v.tensor_scalar_add(vc[:, :], vc[:, :], 1e-5),
                   reads=[vc], writes=[vc])
            b.emit("s", lambda s, vc=vc, sd_=sd_: s.activation(
                sd_[:, :], vc[:, :], AF.Sqrt),
                reads=[vc], writes=[sd_])
            b.emit("v", lambda v, sd_=sd_, rs_=rs_: v.reciprocal(
                rs_[:, :], sd_[:, :]),
                reads=[sd_], writes=[rs_])
            b.emit("v", lambda v, mc=mc, rs_=rs_, nm_=nm_: v.scalar_tensor_tensor(
                nm_[:, :], mc[:, :], -1.0, rs_[:, :], ALU.mult, ALU.mult),
                reads=[mc, rs_], writes=[nm_])
            b.emit("v", lambda v, x=x, xf=xf, rs_=rs_, nm_=nm_: v.tensor_scalar(
                xf[:, :], x[:, :], rs_[:, 0:1], nm_[:, 0:1], ALU.mult, ALU.add),
                reads=[x, rs_, nm_], writes=[xf])
            b.emit("v", lambda v, xf=xf: v.reduce_max(
                mxc[:, :], xf[:, :], axis=AX, apply_absolute_value=True),
                reads=[xf], writes=[mxc])
            b.emit("v", lambda v: v.tensor_max(mxacc[:, :], mxacc[:, :], mxc[:, :]),
                   reads=[mxc, mxacc], writes=[mxacc])
            b.emit("s", lambda s, xf=xf, xb=xb: s.activation(
                xb[:, :], xf[:, :], AF.Copy),
                reads=[xf], writes=[xb])
            b.emit("y", lambda s, k=k, xf=xf: s.dma_start(
                out=xnf_d.t[k * 128:(k + 1) * 128, :], in_=xf[:, :]),
                reads=[xf], writes=[], sem=f"doxf{k % 2}", dma=True)
            b.emit("y", lambda s, k=k, xb=xb: s.dma_start(
                out=xnb_d.t[k * 128:(k + 1) * 128, :], in_=xb[:, :]),
                reads=[xb], writes=[], sem=f"doxb{k % 2}", dma=True)
            u = next(wq_iter, None)
            if u is not None:
                u()
        for u in wq_iter:
            u()
        b.emit("y", lambda s: s.dma_start(out=colscr.t[:], in_=mxacc[:, 0:1]),
               reads=[mxacc], writes=[colscr], sem="dcs", dma=True)
        b.emit("y", lambda s: s.dma_start(out=row[0:1, :], in_=colscr.t[:]),
               reads=[colscr], writes=[row], sem="drow", dma=True)
        b.emit("v", lambda v: v.reduce_max(one1[0:1, 0:1], row[0:1, :], axis=AX),
               reads=[row], writes=[one1])
        b.emit("g", lambda g: g.dma_start(out=lm1_d.t[0:1, 0:1], in_=one1[0:1, 0:1]),
               reads=[one1], writes=[lm1_d], sem="dlm", dma=True)
        b.final_wait("g", [b.tok("doxf0"), b.tok("doxf1"), b.tok("doxb0"),
                           b.tok("doxb1"), b.tok("dlm"),
                           b.tok("dwgq0"), b.tok("dwgq1"), b.tok("dwuq0"),
                           b.tok("dwuq1"), b.tok("dwdq0"), b.tok("dwdq1")])
        block = ctx.enter_context(nc.Block())
        b.build(block)
    return nc


def build_l2():
    nc = bass.Bass("TRN2", target_bir_lowering=False)
    ctx = contextlib.ExitStack()
    with ctx:
        b = Bld(nc, ctx)
        D = lambda n, sh, dt, k="ExternalInput": Buf(nc.dram_tensor(n, sh, dt, kind=k), n)
        x_d = D("x", [T, C], F32)
        xnf_d = D("xnf", [T, C], F32)
        xnb_d = D("xnb", [T, C], BF16)
        lm1_d = D("lm1", [1, 8], F32)
        wtt_d = D("wtt", [NCH, 128, 3072], BF16)
        wot_d = D("wot", [NCH, 128, 1024], BF16)
        br_d, bk_d, bv_d, bo_d = (D(n, [C], F32) for n in ("br", "bk", "bv", "bo"))
        td_d, U_d = D("td", [C], F32), D("Uv", [C], F32)
        clipx_d = D("clipx", [1], F32)
        wgq_dram = D("wgqt", [6, 128, 4096], BF16)
        wuq_dram = D("wuqt", [6, 128, 4096], BF16)
        acc_d = D("acc", [T, C], F32, "ExternalOutput")
        h_d = D("h", [I, T], F16, "ExternalOutput")
        lm2_d = D("lm2", [1, 1], F32, "ExternalOutput")
        qdram = b.dram("qdram", [T, C], BF16)
        owdram = b.dram("owdram", [C, T], BF16)
        colscr = b.dram("colscr", [128], F32)
        scaldr = b.dram("scaldr", [1, 2], F32)

        R4 = b.region("ov4", NCH * 4096)
        xnF = [b.at(R4, c * 4096, f"xnF{c}", [128, T], BF16) for c in range(NCH)]
        xqF = [b.sbuf(f"xqF{c}", [128, T], BF16) for c in range(NCH)]
        ramp = b.sbuf("ramp", [128, T], F32)
        R3 = b.region("ov3", 16384)
        WTJ = [b.at(R3, jj * 6144, f"wtj{jj}", [128, 3072], BF16)
               for jj in range(2)]
        bo_bc = b.at(R3, 12288, "bo_bc", [128, C], F32)
        P_IDX = {"r": 0, "k": 1, "v": 2}
        GT = {w: b.sbuf(f"{w}T", [128, 4096], BF16) for w in ("g", "u")}
        OWS = [b.sbuf(f"ows{i}", [128, H], BF16) for i in range(2)]
        OWL = [b.sbuf(f"owl{i}", [128, C], BF16) for i in range(2)]
        HST = b.sbuf("hst0", [128, T], F16)
        SGH = [b.sbuf(f"sg{i}", [128, H], F32) for i in range(2)]

        cols = {n: b.sbuf(f"col_{n}", [128, NCH], F32)
                for n in ("td", "U", "bk", "br", "bv", "w", "ubk")}
        hmax = b.sbuf("hmaxc", [128, 1], F32)
        hmx1 = b.sbuf("hmx1", [128, 1], F32)
        sx_col = b.sbuf("sx_col", [128, 1], F32)
        isx_col = b.sbuf("isx_col", [128, 1], F32)
        magic_col = b.sbuf("magic_col", [128, 1], F32)
        nmagic_col = b.sbuf("nmagic_col", [128, 1], F32)
        ecols = {}
        for nm, val in (("k", EPS_K), ("v", EPS_V), ("r", EPS_R)):
            ecols[nm] = b.sbuf(f"eps_{nm}", [128, 1], F32)
        row = b.sbuf("rowb", [1, 128], F32)
        one1 = b.sbuf("one1", [1, 1], F32)
        one2 = b.sbuf("one2", [1, 1], F32)
        lmrow = b.sbuf("lmrow", [1, 8], F32)
        sc2 = b.sbuf("sc2", [1, 2], F32)

        NS = 16
        R2 = b.region("ov2", NS * 4096)
        S = [b.at(R2, i * 4096, f"scr{i}", [128, H], F32) for i in range(NS)]

        ps = [b.psum(f"ps{i}", [128, T], F32) for i in range(2)]

        GD, SY = "g", "y"

        b.emit("g", lambda g: g.memset(magic_col[:, :], MAGIC), writes=[magic_col])
        b.emit("g", lambda g: g.memset(nmagic_col[:, :], -MAGIC), writes=[nmagic_col])
        b.emit("g", lambda g: g.memset(hmax[:, :], 0.0), writes=[hmax])
        for nm, val in (("k", EPS_K), ("v", EPS_V), ("r", EPS_R)):
            b.emit("g", lambda g, nm=nm, val=val: g.memset(ecols[nm][:, :], val),
                   writes=[ecols[nm]])
        b.emit("g", lambda g: g.iota(ramp[:, :], [[1, T]], base=-(T - 1),
                                     channel_multiplier=0,
                                     allow_small_or_imprecise_dtypes=True),
               writes=[ramp])
        b.emit("y", lambda s: s.dma_start(
            out=bo_bc[:, :], in_=bass.AP(bo_d.t, 0, [[0, 128], [1, C]])),
            reads=[], writes=[bo_bc], sem="dbo", dma=True)
        for n, d in (("td", td_d), ("U", U_d), ("bk", bk_d), ("br", br_d),
                     ("bv", bv_d)):
            b.emit("g", lambda g, n=n, d=d: g.dma_start(
                out=cols[n][:, :], in_=d.t.ap().rearrange("(f p) -> p f", p=128),
                allow_slow_non_contiguous=True),
                reads=[], writes=[cols[n]], sem=f"dc_{n}", dma=True)
        b.emit("s", lambda s: s.activation(cols["w"][:, :], cols["td"][:, :], AF.Relu),
               reads=[cols["td"]], writes=[cols["w"]])
        b.emit("v", lambda v: v.tensor_add(cols["ubk"][:, :], cols["U"][:, :],
                                           cols["bk"][:, :]),
               reads=[cols["U"], cols["bk"]], writes=[cols["ubk"]])
        b.emit("g", lambda g: g.dma_start(out=lmrow[0:1, :], in_=lm1_d.t[0:1, :]),
               reads=[], writes=[lmrow], sem="dlm1", dma=True)
        b.emit("g", lambda g: g.dma_start(out=one1[0:1, 0:1], in_=clipx_d.t[0:1]),
               reads=[], writes=[one1], sem="dcx", dma=True)
        b.emit("v", lambda v: v.reduce_max(one2[0:1, 0:1], lmrow[0:1, :], axis=AX),
               reads=[lmrow], writes=[one2])
        b.emit("s", lambda s: s.activation(one1[0:1, 0:1], one1[0:1, 0:1], AF.Sigmoid),
               reads=[one1], writes=[one1])
        b.emit("v", lambda v: v.tensor_mul(sc2[0:1, 0:1], one1[0:1, 0:1],
                                           one2[0:1, 0:1]),
               reads=[one1, one2], writes=[sc2])
        b.emit("v", lambda v: v.tensor_scalar_mul(sc2[0:1, 0:1], sc2[0:1, 0:1],
                                                  1.0 / QMAX),
               reads=[sc2], writes=[sc2])
        b.emit("v", lambda v: v.reciprocal(sc2[0:1, 1:2], sc2[0:1, 0:1]),
               reads=[sc2], writes=[sc2])
        b.emit("y", lambda s: s.dma_start(out=scaldr.t[0:1, :], in_=sc2[0:1, :]),
               reads=[sc2], writes=[scaldr], sem="dscw", dma=True)
        b.emit("y", lambda s: s.dma_start(
            out=sx_col[:, 0:1], in_=bass.AP(scaldr.t, 0, [[0, 128], [1, 1]])),
            reads=[scaldr], writes=[sx_col], sem="dsx", dma=True)
        b.emit("y", lambda s: s.dma_start(
            out=isx_col[:, 0:1], in_=bass.AP(scaldr.t, 1, [[0, 128], [1, 1]])),
            reads=[scaldr], writes=[isx_col], sem="disx", dma=True)

        for c in range(NCH):
            b.emit("y", lambda s, c=c: s.dma_start_transpose(
                xnF[c][:, :], xnb_d.t[0:T, c * 128:(c + 1) * 128]),
                reads=[], writes=[xnF[c]], sem=f"dxnF{c % 4}", dma=True)
        xnF_toks = [b.tok(f"dxnF{i}") for i in range(4)]
        for c in range(NCH):
            xnF[c].w = xnF_toks[0]
            xnF[c].extra = xnF_toks[1:]

        def load_wtj(j):
            jj = j % 2
            b.emit("y", lambda s, j=j, jj=jj: s.dma_start(
                out=WTJ[jj][:, :], in_=wtt_d.t[j]),
                reads=[wtt_d], writes=[WTJ[jj]], sem=f"dwtj{j % 2}", dma=True)

        def load_gt_group(grp):
            for wname, wdram in (("g", wgq_dram), ("u", wuq_dram)):
                b.emit("y", lambda s, wname=wname, wdram=wdram, grp=grp:
                       s.dma_start(out=GT[wname][:, :], in_=wdram.t[grp]),
                       reads=[wdram], writes=[GT[wname]],
                       sem=f"dgt{wname}", dma=True)

        load_wtj(0)
        load_wtj(1)
        load_gt_group(0)

        def proj_mm(tt, pbuf, p, j):
            last = None
            base = P_IDX[p] * 1024
            for c in range(NCH):
                for q in range(4):
                    last = tt.matmul(
                        pbuf[:, q * 512:(q + 1) * 512],
                        WTJ[j % 2][:, base + c * 128:base + (c + 1) * 128],
                        xnF[c][:, q * 512:(q + 1) * 512],
                        start=(c == 0), stop=(c == NCH - 1))
            return last

        XF = [b.at(R2, (12 + i) * 4096, f"xf32_{i}", [128, C], F32)
              for i in range(2)]
        QS = [b.at(R2, 14 * 4096 + i * 2048, f"qs{i}", [128, C], BF16)
              for i in range(2)]

        for k in range(NTT):
            xf = XF[k % 2]
            q = QS[k % 2]
            ldq = "g" if k % 2 == 0 else "s"
            b.emit(ldq, lambda e, k=k, xf=xf: e.dma_start(
                out=xf[:, :], in_=xnf_d.t[k * 128:(k + 1) * 128, :]),
                reads=[], writes=[xf], sem=f"dxf{k % 2}", dma=True)
            b.emit("s", lambda s, xf=xf: s.activation(
                xf[:, :], xf[:, :], AF.Identity, bias=magic_col[:, 0:1],
                scale=isx_col[:, 0:1]),
                reads=[xf, isx_col, magic_col], writes=[xf])
            b.emit("s", lambda s, xf=xf: s.activation(
                xf[:, :], xf[:, :], AF.Identity, bias=nmagic_col[:, 0:1]),
                reads=[xf, nmagic_col], writes=[xf])
            b.emit("v", lambda v, xf=xf, q=q: v.tensor_scalar(
                q[:, :], xf[:, :], -(QMAX + 1.0), QMAX, ALU.max, ALU.min),
                reads=[xf], writes=[q])
            b.emit("g", lambda g, k=k, q=q: g.dma_start(
                out=qdram.t[k * 128:(k + 1) * 128, :], in_=q[:, :]),
                reads=[q], writes=[], sem=f"dqo{k % 2}", dma=True)
        qtk = [b.tok("dqo0"), b.tok("dqo1")]

        def emit_xqf():
            for c in range(NCH):
                b.emit("y", lambda s, c=c: s.dma_start_transpose(
                    xqF[c][:, :], qdram.t[0:T, c * 128:(c + 1) * 128]),
                    reads=[], writes=[xqF[c]], sem=f"dxqF{c % 4}",
                    dma=True, extra_waits=qtk)
            xqtk = [b.tok(f"dxqF{i}") for i in range(4)]
            for c in range(NCH):
                xqF[c].w = xqtk[0]
                xqF[c].extra = xqtk[1:]
        b.handoff([S[12], S[13], S[14]], XF + QS)

        npsum = [0]

        def next_ps():
            p = ps[npsum[0] % 2]
            npsum[0] += 1
            return p

        SA = [b.sbuf(f"scanA{i}", [128, H + 1], F32) for i in range(2)]
        SB = [b.sbuf(f"scanB{i}", [128, H + 1], F32) for i in range(2)]
        b.emit("g", lambda g: g.memset(SA[0][:, 0:1], 0.0), writes=[SA[0]])
        b.emit("g", lambda g: g.memset(SB[0][:, 0:1], 0.0), writes=[SB[0]])
        KFh = [S[1], S[2]]
        VFh = [S[3], S[4]]
        SGRh = [S[5], S[6]]
        MW = [(S[7], S[8], S[9], S[10]), (S[11], S[12], S[13], S[14])]
        T1a, T1b = S[0], S[15]
        RW = (MW[0][0], MW[1][0])

        def emit_mm(p, j):
            pbuf = next_ps()
            b.emit("t", lambda tt, pbuf=pbuf, p=p, j=j: proj_mm(tt, pbuf, p, j),
                   reads=[WTJ[j % 2]] + xnF, writes=[pbuf])
            return pbuf

        def emit_evac(pbuf, eps, ecol_b, bias_b, j, d0, d1):
            ec = ecol_b[:, 0:1]
            bj = bias_b[:, j:j + 1]
            b.emit("v", lambda v, bj=bj: v.tensor_copy(T1a[:, 0:1], bj),
                   reads=[bias_b], writes=[T1a])
            b.emit("s", lambda s, pbuf=pbuf, ec=ec, bj=bj: s.activation(
                T1a[:, 1:H], pbuf[:, 0:H - 1], AF.Identity, bias=bj, scale=ec),
                reads=[pbuf, bias_b, ecol_b], writes=[T1a])
            b.emit("s", lambda s, pbuf=pbuf, ec=ec, bj=bj: s.activation(
                T1b[:, 0:H], pbuf[:, H - 1:T - 1], AF.Identity, bias=bj, scale=ec),
                reads=[pbuf, bias_b, ecol_b], writes=[T1b])
            b.emit("v", lambda v, pbuf=pbuf, eps=eps, d0=d0: v.scalar_tensor_tensor(
                d0[:, 0:H], pbuf[:, 1:H + 1], 1.0 - eps, T1a[:, 0:H],
                ALU.mult, ALU.add),
                reads=[pbuf, T1a], writes=[d0])
            b.emit("v", lambda v, pbuf=pbuf, eps=eps, d1=d1: v.scalar_tensor_tensor(
                d1[:, 0:H - 1], pbuf[:, H + 1:T], 1.0 - eps, T1b[:, 0:H - 1],
                ALU.mult, ALU.add),
                reads=[pbuf, T1b], writes=[d1])
            b.emit("v", lambda v, d1=d1: v.tensor_copy(
                d1[:, H - 1:H], T1b[:, H - 1:H]),
                reads=[T1b], writes=[d1])

        def emit_rinterp(pR, j):
            emit_evac(pR, EPS_R, ecols["r"], cols["br"], j, RW[0], RW[1])
            for hh in range(2):
                b.emit("s", lambda s, hh=hh: s.activation(
                    SGRh[hh][:, :], RW[hh][:, :], AF.Sigmoid),
                    reads=[RW[hh]], writes=[SGRh[hh]])

        def pre_k(j):
            p = emit_mm("k", j)
            emit_evac(p, EPS_K, ecols["k"], cols["bk"], j, KFh[0], KFh[1])

        def pre_v(j):
            p = emit_mm("v", j)
            emit_evac(p, EPS_V, ecols["v"], cols["bv"], j, VFh[0], VFh[1])

        def pre_r(j):
            p = emit_mm("r", j)
            emit_rinterp(p, j)

        def emit_middle(j, ins1=None, ins2=None, ins3=None):
            wj = cols["w"][:, j:j + 1]
            ubkj = cols["ubk"][:, j:j + 1]
            HV = [dict(W=MW[hh], kf=KFh[hh], vf=VFh[hh], sgr=SGRh[hh],
                       sa=SA[hh], sb=SB[hh], lo=hh * H) for hh in range(2)]

            def A1(hh):
                W4 = HV[hh]["W"][3]
                kf = HV[hh]["kf"]
                b.emit("s", lambda s, W4=W4, kf=kf, ubkj=ubkj: s.activation(
                    W4[:, :], kf[:, :], AF.Identity, bias=ubkj),
                    reads=[kf, cols["ubk"]], writes=[W4])

            def D1(hh):
                W1 = HV[hh]["W"][0]
                kf = HV[hh]["kf"]
                lo = HV[hh]["lo"]
                b.emit("v", lambda v, W1=W1, kf=kf, lo=lo, wj=wj:
                       v.scalar_tensor_tensor(W1[:, :], ramp[:, lo:lo + H], wj,
                                              kf[:, :], ALU.mult, ALU.add),
                       reads=[ramp, cols["w"], kf], writes=[W1])

            def DCLIP(buf):
                b.emit("v", lambda v, buf=buf: v.tensor_scalar(
                    buf[:, :], buf[:, :], -20.0, 10.0, ALU.max, ALU.min),
                    reads=[buf], writes=[buf])

            def A2(hh):
                W1, W2 = HV[hh]["W"][0], HV[hh]["W"][1]
                b.emit("s", lambda s, W1=W1, W2=W2: s.activation(
                    W2[:, :], W1[:, :], AF.Exp), reads=[W1], writes=[W2])

            def A3(hh):
                W4 = HV[hh]["W"][3]
                b.emit("s", lambda s, W4=W4: s.activation(
                    W4[:, :], W4[:, :], AF.Exp), reads=[W4], writes=[W4])

            def D4(hh):
                W2, W3 = HV[hh]["W"][1], HV[hh]["W"][2]
                vf = HV[hh]["vf"]
                b.emit("v", lambda v, W2=W2, W3=W3, vf=vf: v.tensor_mul(
                    W3[:, :], W2[:, :], vf[:, :]), reads=[W2, vf], writes=[W3])

            def DSCAN(hh, which):
                src = HV[hh]["W"][2] if which == 0 else HV[hh]["W"][1]
                dst = HV[hh]["sa"] if which == 0 else HV[hh]["sb"]
                if hh == 0:
                    b.emit("v", lambda v, src=src, dst=dst: v.tensor_tensor_scan(
                        dst[:, 1:H + 1], src[:, :], src[:, :], 0.0,
                        ALU.add, ALU.bypass),
                        reads=[src], writes=[dst])
                else:
                    prev = SA[0] if which == 0 else SB[0]
                    b.emit("v", lambda v, src=src, dst=dst, prev=prev:
                           v.tensor_tensor_scan(dst[:, 1:H + 1], src[:, :],
                                                src[:, :], prev[:, H:H + 1],
                                                ALU.add, ALU.bypass),
                           reads=[src, prev], writes=[dst])

            def DCARRY(which):
                dst = SA[1] if which == 0 else SB[1]
                prev = SA[0] if which == 0 else SB[0]
                b.emit("v", lambda v, dst=dst, prev=prev: v.tensor_copy(
                    dst[:, 0:1], prev[:, H:H + 1]), reads=[prev], writes=[dst])

            def D7(hh):
                W3, W4 = HV[hh]["W"][2], HV[hh]["W"][3]
                vf = HV[hh]["vf"]
                b.emit("v", lambda v, W3=W3, W4=W4, vf=vf: v.tensor_mul(
                    W3[:, :], W4[:, :], vf[:, :]), reads=[W4, vf], writes=[W3])

            def D8(hh):
                W2, W3 = HV[hh]["W"][1], HV[hh]["W"][2]
                sa = HV[hh]["sa"]
                b.emit("v", lambda v, W2=W2, W3=W3, sa=sa: v.tensor_add(
                    W2[:, :], sa[:, 0:H], W3[:, :]), reads=[sa, W3], writes=[W2])

            def D9(hh):
                W1, W4 = HV[hh]["W"][0], HV[hh]["W"][3]
                sb = HV[hh]["sb"]
                b.emit("v", lambda v, W1=W1, W4=W4, sb=sb: v.tensor_add(
                    W1[:, :], sb[:, 0:H], W4[:, :]), reads=[sb, W4], writes=[W1])

            def A4(hh):
                W1, W3 = HV[hh]["W"][0], HV[hh]["W"][2]
                b.emit("s", lambda s, W1=W1, W3=W3: s.activation(
                    W3[:, :], W1[:, :], AF.Ln), reads=[W1], writes=[W3])

            def A5(hh):
                W1, W3 = HV[hh]["W"][0], HV[hh]["W"][2]
                b.emit("s", lambda s, W1=W1, W3=W3: s.activation(
                    W1[:, :], W3[:, :], AF.Exp, scale=-1.0),
                    reads=[W3], writes=[W1])

            def D10(hh):
                W1, W2, W3 = HV[hh]["W"][0], HV[hh]["W"][1], HV[hh]["W"][2]
                b.emit("v", lambda v, W1=W1, W2=W2, W3=W3: v.tensor_mul(
                    W3[:, :], W2[:, :], W1[:, :]), reads=[W2, W1], writes=[W3])

            def D11(hh):
                W3 = HV[hh]["W"][2]
                sgr = HV[hh]["sgr"]
                lo = HV[hh]["lo"]
                ows = OWS[hh]
                b.emit("v", lambda v, W3=W3, sgr=sgr, ows=ows: v.tensor_mul(
                    ows[:, :], W3[:, :], sgr[:, :]),
                    reads=[W3, sgr], writes=[ows])
                b.emit("y", lambda s, ows=ows, lo=lo, j=j: s.dma_start(
                    out=owdram.t[j * 128:(j + 1) * 128, lo:lo + H],
                    in_=ows[:, :]),
                    reads=[ows], writes=[], sem=f"dow{hh}", dma=True)

            for hh in range(2):
                A1(hh)
                D1(hh)
                DCLIP(HV[hh]["W"][0])
                A2(hh)
            for hh in range(2):
                DCLIP(HV[hh]["W"][3])
                A3(hh)
            D4(0)
            DSCAN(0, 0)
            DSCAN(0, 1)
            D4(1)
            D7(0)
            D8(0)
            D9(0)
            A4(0)
            DSCAN(1, 0)
            DSCAN(1, 1)
            DCARRY(0)
            DCARRY(1)
            A5(0)
            if ins1 is not None:
                ins1()
            D7(1)
            D8(1)
            D9(1)
            A4(1)
            D10(0)
            D11(0)
            A5(1)
            if ins2 is not None:
                ins2()
            D10(1)
            D11(1)
            if ins3 is not None:
                ins3()

        def emit_gate(j):
            grp, s_in = j // 4, j % 4
            pG = next_ps()
            pU = next_ps()

            def gmm(tt, pG=pG, s_in=s_in):
                last = None
                for c in range(NCH):
                    for q in range(4):
                        last = tt.matmul(
                            pG[:, q * 512:(q + 1) * 512],
                            GT["g"][:, c * 512 + s_in * 128:
                                    c * 512 + (s_in + 1) * 128],
                            xqF[c][:, q * 512:(q + 1) * 512],
                            start=(c == 0), stop=(c == NCH - 1))
                return last

            def umm(tt, pU=pU, s_in=s_in):
                last = None
                for c in range(NCH):
                    for q in range(4):
                        last = tt.matmul(
                            pU[:, q * 512:(q + 1) * 512],
                            GT["u"][:, c * 512 + s_in * 128:
                                    c * 512 + (s_in + 1) * 128],
                            xqF[c][:, q * 512:(q + 1) * 512],
                            start=(c == 0), stop=(c == NCH - 1))
                return last
            b.emit("t", gmm, reads=[GT["g"]] + xqF, writes=[pG])
            b.emit("t", umm, reads=[GT["u"]] + xqF, writes=[pU])
            for hh in range(2):
                sg = SGH[hh]
                lo = hh * H
                b.emit("s", lambda s, pG=pG, lo=lo, sg=sg: s.activation(
                    sg[:, :], pG[:, lo:lo + H], AF.Silu, scale=sx_col[:, 0:1]),
                    reads=[pG, sx_col], writes=[sg])
                b.emit("v", lambda v, pU=pU, lo=lo, sg=sg:
                       v.scalar_tensor_tensor(
                           HST[:, lo:lo + H], pU[:, lo:lo + H], sx_col[:, 0:1],
                           sg[:, :], ALU.mult, ALU.mult),
                       reads=[pU, sg, sx_col], writes=[HST])
            b.emit("v", lambda v: v.reduce_max(
                hmx1[:, :], HST[:, :], axis=AX, apply_absolute_value=True),
                reads=[HST], writes=[hmx1])
            b.emit("v", lambda v: v.tensor_max(hmax[:, :], hmax[:, :], hmx1[:, :]),
                   reads=[hmx1, hmax], writes=[hmax])
            b.emit("s", lambda s, j=j: s.dma_start(
                out=h_d.t[j * 128:(j + 1) * 128, :], in_=HST[:, :]),
                reads=[HST], writes=[], sem=f"dho{j % 2}", dma=True)
            if s_in == 3 and grp + 1 <= (NIC - 1) // 4:
                load_gt_group(grp + 1)

        gate_pos = [0]

        def emit_gates(nmax):
            for _ in range(nmax):
                jg = gate_pos[0]
                if jg >= NIC:
                    return
                gate_pos[0] += 1
                emit_gate(jg)

        def ins1_fn(n):
            def f():
                if n == 1:
                    emit_xqf()
                if n < NCH:
                    pre_k(n)
            return f

        def ins3_fn(n):
            def f():
                if n >= 2:
                    emit_gates(2)
                if n < NCH:
                    pre_r(n)
                    if n + 1 < NCH:
                        load_wtj(n + 1)
            return f

        pre_k(0)
        pre_v(0)
        pre_r(0)
        for j in range(NCH):
            n = j + 1
            emit_middle(j,
                        ins1=ins1_fn(n),
                        ins2=(lambda n=n: pre_v(n)) if n < NCH else None,
                        ins3=ins3_fn(n))

        WoT = [b.at(R4, 16384 + c * 2048, f"WoT{c}", [128, C], BF16)
               for c in range(NCH)]
        b.handoff(WoT, xnF)
        for c in range(NCH):
            b.emit("y", lambda s, c=c: s.dma_start(
                out=WoT[c][:, :], in_=wot_d.t[c]),
                reads=[wot_d], writes=[WoT[c]], sem=f"dwt_o{c % 2}", dma=True)
        wotk = [b.tok("dwt_o0"), b.tok("dwt_o1")]
        for c in range(NCH):
            WoT[c].w = wotk[0]
            WoT[c].extra = [wotk[1]]

        XR = [b.at(R2, i * 4096, f"xr{i}", [128, C], F32) for i in range(2)]
        AST = [b.at(R2, 8192 + i * 4096, f"ast{i}", [128, C], F32)
               for i in range(2)]
        b.handoff(XR + AST, S)

        owtk = [b.tok("dow0"), b.tok("dow1")]
        for k in range(NTT):
            pW = next_ps()
            ast = AST[k % 2]
            x = XR[k % 2]
            ow = OWL[k % 2]
            b.emit("g", lambda g, k=k, x=x: g.dma_start(
                out=x[:, :], in_=x_d.t[k * 128:(k + 1) * 128, :]),
                reads=[], writes=[x], sem=f"dx{k % 2}", dma=True)
            b.emit("g", lambda g, k=k, ow=ow: g.dma_start(
                out=ow.t.ap().rearrange("p (c t) -> p c t", c=NCH),
                in_=owdram.t[0:C, k * 128:(k + 1) * 128].rearrange(
                    "(c p) t -> p c t", p=128)),
                reads=[], writes=[ow], sem=f"dowl{k % 2}", dma=True,
                extra_waits=owtk)
            b.emit("v", lambda v, x=x: v.tensor_add(
                x[:, :], x[:, :], bo_bc[:, :]),
                reads=[bo_bc, x], writes=[x])

            def womm(tt, pW=pW, k=k, ow=ow):
                last = None
                for c in range(NCH):
                    for q in range(2):
                        last = tt.matmul(
                            pW[:, q * 512:(q + 1) * 512],
                            ow[:, c * 128:(c + 1) * 128],
                            WoT[c][:, q * 512:(q + 1) * 512],
                            start=(c == 0), stop=(c == NCH - 1))
                return last
            b.emit("t", womm, reads=[ow] + WoT, writes=[pW])
            b.emit("v", lambda v, pW=pW, ast=ast, x=x: v.tensor_add(
                ast[:, :], pW[:, 0:C], x[:, :]),
                reads=[pW, x], writes=[ast])
            b.emit("s", lambda s, k=k, ast=ast: s.dma_start(
                out=acc_d.t[k * 128:(k + 1) * 128, :], in_=ast[:, :]),
                reads=[ast], writes=[], sem=f"dacc{k % 2}", dma=True)
            if k % 2 == 1:
                emit_gates(1)
        emit_gates(NIC)

        b.emit("y", lambda s: s.dma_start(out=colscr.t[:], in_=hmax[:, 0:1]),
               reads=[hmax], writes=[colscr], sem="dcs2", dma=True)
        b.emit("y", lambda s: s.dma_start(out=row[0:1, :], in_=colscr.t[:]),
               reads=[colscr], writes=[row], sem="drow2", dma=True)
        b.emit("v", lambda v: v.reduce_max(one1[0:1, 0:1], row[0:1, :], axis=AX),
               reads=[row], writes=[one1])
        b.emit("g", lambda g: g.dma_start(out=lm2_d.t[0:1, 0:1], in_=one1[0:1, 0:1]),
               reads=[one1], writes=[lm2_d], sem="dlm2", dma=True)

        b.final_wait("g", [b.tok("dacc0"), b.tok("dacc1"), b.tok("dho0"),
                           b.tok("dho1"), b.tok("dlm2")])
        block = ctx.enter_context(nc.Block())
        b.build(block)
    return nc


def build_l3():
    nc = bass.Bass("TRN2", target_bir_lowering=False)
    ctx = contextlib.ExitStack()
    with ctx:
        b = Bld(nc, ctx)
        D = lambda n, sh, dt, k="ExternalInput": Buf(nc.dram_tensor(n, sh, dt, kind=k), n)
        h_d = D("h", [I, T], F16)
        acc_d = D("acc", [T, C], F32)
        wdq_dram = D("wdqt", [NIC, 128, 1024], BF16)
        lm2_d = D("lm2", [1, 8], F32)
        clipd_d = D("clipd", [1], F32)
        out_d = D("out", [T, C], F32, "ExternalOutput")
        scaldr = b.dram("scaldr", [1, 2], F32)

        qh = [b.sbuf(f"qh{j}", [128, T], BF16) for j in range(NIC)]
        wdT = [b.sbuf(f"wdT{j}", [128, C], BF16) for j in range(NIC)]
        hst = [b.sbuf(f"hld{i}", [128, T], F16) for i in range(2)]
        accr = [b.sbuf(f"accr{i}", [128, C], F32) for i in range(2)]
        ost = [b.sbuf(f"ost{i}", [128, C], F32) for i in range(2)]
        t32 = [b.sbuf(f"t32_{i}", [128, T], F32) for i in range(2)]
        s2_col = b.sbuf("s2_col", [128, 1], F32)
        is2_col = b.sbuf("is2_col", [128, 1], F32)
        one1 = b.sbuf("one1", [1, 1], F32)
        one2 = b.sbuf("one2", [1, 1], F32)
        lmrow = b.sbuf("lmrow", [1, 8], F32)
        sc2 = b.sbuf("sc2", [1, 2], F32)
        magic_col = b.sbuf("magic_col", [128, 1], F32)
        ps = [b.psum(f"ps{i}", [128, C], F32) for i in range(4)]
        b.emit("g", lambda g: g.memset(magic_col[:, :], MAGIC), writes=[magic_col])

        b.emit("g", lambda g: g.dma_start(out=lmrow[0:1, :], in_=lm2_d.t[0:1, :]),
               reads=[], writes=[lmrow], sem="dlm2", dma=True)
        b.emit("g", lambda g: g.dma_start(out=one1[0:1, 0:1], in_=clipd_d.t[0:1]),
               reads=[], writes=[one1], sem="dcd", dma=True)
        b.emit("v", lambda v: v.reduce_max(one2[0:1, 0:1], lmrow[0:1, :], axis=AX),
               reads=[lmrow], writes=[one2])
        b.emit("s", lambda s: s.activation(one1[0:1, 0:1], one1[0:1, 0:1], AF.Sigmoid),
               reads=[one1], writes=[one1])
        b.emit("v", lambda v: v.tensor_mul(sc2[0:1, 0:1], one1[0:1, 0:1],
                                           one2[0:1, 0:1]),
               reads=[one1, one2], writes=[sc2])
        b.emit("v", lambda v: v.tensor_scalar_mul(sc2[0:1, 0:1], sc2[0:1, 0:1],
                                                  1.0 / QMAX),
               reads=[sc2], writes=[sc2])
        b.emit("v", lambda v: v.reciprocal(sc2[0:1, 1:2], sc2[0:1, 0:1]),
               reads=[sc2], writes=[sc2])
        b.emit("y", lambda s: s.dma_start(out=scaldr.t[0:1, :], in_=sc2[0:1, :]),
               reads=[sc2], writes=[scaldr], sem="dscw", dma=True)
        b.emit("y", lambda s: s.dma_start(
            out=s2_col[:, 0:1], in_=bass.AP(scaldr.t, 0, [[0, 128], [1, 1]])),
            reads=[scaldr], writes=[s2_col], sem="ds2", dma=True)
        b.emit("y", lambda s: s.dma_start(
            out=is2_col[:, 0:1], in_=bass.AP(scaldr.t, 1, [[0, 128], [1, 1]])),
            reads=[scaldr], writes=[is2_col], sem="dis2", dma=True)

        for j in range(NIC):
            b.emit("y", lambda s, j=j: s.dma_start(
                out=wdT[j][:, :], in_=wdq_dram.t[j]),
                reads=[wdq_dram], writes=[wdT[j]], sem=f"dwdT{j % 2}", dma=True)
        wtk = [b.tok("dwdT0"), b.tok("dwdT1")]
        for j in range(NIC):
            wdT[j].w = wtk[0]
            wdT[j].extra = [wtk[1]]

        for j in range(NIC):
            hl = hst[j % 2]
            tl = t32[j % 2]
            b.emit("g", lambda g, j=j, hl=hl: g.dma_start(
                out=hl[:, :], in_=h_d.t[j * 128:(j + 1) * 128, :]),
                reads=[], writes=[hl], sem=f"dh{j % 2}", dma=True)
            b.emit("s", lambda s, hl=hl, tl=tl: s.activation(
                tl[:, :], hl[:, :], AF.Identity, bias=magic_col[:, 0:1],
                scale=is2_col[:, 0:1]),
                reads=[hl, is2_col, magic_col], writes=[tl])
            b.emit("v", lambda v, tl=tl: v.tensor_scalar(
                tl[:, :], tl[:, :], MAGIC, -(QMAX + 1.0), ALU.subtract, ALU.max),
                reads=[tl], writes=[tl])
            b.emit("v", lambda v, j=j, tl=tl: v.tensor_scalar_min(
                qh[j][:, :], tl[:, :], QMAX),
                reads=[tl], writes=[qh[j]])

        for k in range(NTT):
            pD = ps[k % 4]
            ac = accr[k % 2]
            os_ = ost[k % 2]
            b.emit("g", lambda g, k=k, ac=ac: g.dma_start(
                out=ac[:, :], in_=acc_d.t[k * 128:(k + 1) * 128, :]),
                reads=[], writes=[ac], sem=f"dac{k % 2}", dma=True)

            def dmm(tt, pD=pD, k=k):
                last = None
                for j in range(NIC):
                    for q in range(2):
                        last = tt.matmul(
                            pD[:, q * 512:(q + 1) * 512],
                            qh[j][:, k * 128:(k + 1) * 128],
                            wdT[j][:, q * 512:(q + 1) * 512],
                            start=(j == 0), stop=(j == NIC - 1))
                return last
            b.emit("t", dmm, reads=qh + wdT, writes=[pD])
            b.emit("v", lambda v, pD=pD, ac=ac, os_=os_: v.scalar_tensor_tensor(
                os_[:, :], pD[:, 0:C], s2_col[:, 0:1], ac[:, :],
                ALU.mult, ALU.add),
                reads=[pD, ac, s2_col], writes=[os_])
            b.emit("g", lambda g, k=k, os_=os_: g.dma_start(
                out=out_d.t[k * 128:(k + 1) * 128, :], in_=os_[:, :]),
                reads=[os_], writes=[], sem=f"dout{k % 2}", dma=True)

        b.final_wait("g", [b.tok("dout0"), b.tok("dout1")])
        block = ctx.enter_context(nc.Block())
        b.build(block)
    return nc


def kernel(L0_attn_x, L0_attn_Wr, L0_attn_br, L0_attn_Wk, L0_attn_bk,
           L0_attn_Wv, L0_attn_bv, L0_attn_Wo, L0_attn_bo, L0_attn_time_decay,
           L0_attn_U, L0_ffn_clip_x, L0_ffn_clip_down, L0_ffn_w_gate,
           L0_ffn_w_up, L0_ffn_w_down):
    n = 8
    x = np.ascontiguousarray(np.asarray(L0_attn_x, dtype=np.float32))
    xs = [np.ascontiguousarray(x[i]) for i in range(n)]
    f32 = lambda a: np.ascontiguousarray(np.asarray(a, dtype=np.float32))
    Wr, Wk, Wv, Wo = f32(L0_attn_Wr), f32(L0_attn_Wk), f32(L0_attn_Wv), f32(L0_attn_Wo)
    br, bk, bv, bo = f32(L0_attn_br), f32(L0_attn_bk), f32(L0_attn_bv), f32(L0_attn_bo)
    td, Uv = f32(L0_attn_time_decay), f32(L0_attn_U)
    clipx, clipd = f32(L0_ffn_clip_x), f32(L0_ffn_clip_down)
    wg, wu, wd = f32(L0_ffn_w_gate), f32(L0_ffn_w_up), f32(L0_ffn_w_down)

    cores = list(range(n))
    gstart = [0, 384, 768, 1152, 1536, 1920, 2304, 2560]
    grows = [384] * 6 + [256] * 2

    def wslice(w, i):
        blk = w[gstart[i]:gstart[i] + grows[i]]
        if grows[i] < 384:
            blk = np.concatenate([blk, blk[:384 - grows[i]]], axis=0)
        return np.ascontiguousarray(blk)

    nc1 = build_l1()
    in1 = [{"x": xs[i], "wg": wslice(wg, i), "wu": wslice(wu, i),
            "wd": np.ascontiguousarray(wd[i * 128:(i + 1) * 128]),
            "Wr": np.ascontiguousarray(Wr[i * 128:(i + 1) * 128]),
            "Wk": np.ascontiguousarray(Wk[i * 128:(i + 1) * 128]),
            "Wv": np.ascontiguousarray(Wv[i * 128:(i + 1) * 128]),
            "Wo": np.ascontiguousarray(Wo[i * 128:(i + 1) * 128])}
           for i in range(n)]
    r1 = run_bass_kernel_spmd(nc1, in1, core_ids=cores)
    lm1 = np.concatenate(
        [np.asarray(r1.results[i]["lm1"]).reshape(1, 1) for i in range(n)], axis=1)
    xnf = [np.asarray(r1.results[i]["xnf"]) for i in range(n)]
    xnb = [np.asarray(r1.results[i]["xnb"]) for i in range(n)]
    def wassemble(key):
        tiles = []
        for i in range(n):
            t = np.asarray(r1.results[i][key])
            tiles.extend(t[:grows[i] // 128])
        tiles.extend(tiles[-1:] * 2)
        arr = np.stack(tiles, axis=0).reshape(6, 4, NCH, 128, 128)
        return np.ascontiguousarray(
            arr.transpose(0, 3, 2, 1, 4).reshape(6, 128, 4096))

    wgq = wassemble("wgqt")
    wuq = wassemble("wuqt")
    wdq = np.ascontiguousarray(
        np.stack([np.asarray(r1.results[i]["wdqt"]) for i in range(n)],
                 axis=2).reshape(NIC, 128, 1024))
    wtt = np.ascontiguousarray(
        np.stack([np.asarray(r1.results[i]["wtt"]) for i in range(n)], axis=0))
    wo_j = np.stack([np.asarray(r1.results[i]["wot"]).reshape(128, 8, 128)
                     for i in range(n)], axis=0)
    wot = np.ascontiguousarray(
        wo_j.transpose(2, 1, 0, 3).reshape(8, 128, 1024))

    nc2 = build_l2()
    in2 = [{"x": xs[i], "xnf": xnf[i], "xnb": xnb[i], "lm1": lm1,
            "wtt": wtt, "wot": wot,
            "br": br, "bk": bk, "bv": bv, "bo": bo,
            "td": td, "Uv": Uv, "clipx": clipx, "wgqt": wgq, "wuqt": wuq}
           for i in range(n)]
    r2 = run_bass_kernel_spmd(nc2, in2, core_ids=cores)
    lm2 = np.concatenate(
        [np.asarray(r2.results[i]["lm2"]).reshape(1, 1) for i in range(n)], axis=1)
    acc = [np.asarray(r2.results[i]["acc"]) for i in range(n)]
    hh = [np.asarray(r2.results[i]["h"]) for i in range(n)]

    nc3 = build_l3()
    in3 = [{"h": hh[i], "acc": acc[i], "wdqt": wdq, "lm2": lm2, "clipd": clipd}
           for i in range(n)]
    r3 = run_bass_kernel_spmd(nc3, in3, core_ids=cores)
    out = np.stack([np.asarray(r3.results[i]["out"]) for i in range(n)], axis=0)
    return out.astype(np.float32)
